# version 08
# baseline (speedup 1.0000x reference)
import numpy as np
import ml_dtypes
import concourse.bass as bass
import concourse.mybir as mybir
from concourse.bass_utils import run_bass_kernel_spmd

F32 = mybir.dt.float32
BF16 = mybir.dt.bfloat16
AF = mybir.ActivationFunctionType
ALU = mybir.AluOpType
AX = mybir.AxisListType
NBF = ml_dtypes.bfloat16

B, T, D = 2, 8192, 1024
NCORES = 8
SPAN = T // 4
DCH = D // 4
N_SSM = 16
R_SSM = 64
K_CONV = 4
D_FF = 4096
N_EXP = 8
H = 16
DH = 64


class Prog:

    ENGINES = ("sync", "tensor", "scalar", "vector", "gpsimd")

    def __init__(self, nc):
        self.nc = nc
        self.pv = {}
        self.base_deps = []
        self.ops = {e: [] for e in self.ENGINES}
        self.counts = {}
        self.dpar = {e: 0 for e in self.ENGINES}
        for e in self.ENGINES:
            for kind in ("i", "d0", "d1"):
                self.counts[(e, kind)] = 0

    def add(self, engine, fn, deps=(), dma=False, force_deps=()):
        deps = list(deps) + self.base_deps
        if dma:
            kind = f"d{self.dpar[engine]}"
            self.dpar[engine] ^= 1
            n = len(fn) if isinstance(fn, (list, tuple)) else 1
            self.counts[(engine, kind)] += 16 * n
        else:
            kind = "i"
            self.counts[(engine, kind)] += 1
        handle = (engine, kind, self.counts[(engine, kind)])
        self.ops[engine].append((fn, tuple(d for d in deps if d is not None),
                                 dma, handle, tuple(d for d in force_deps if d is not None)))
        return handle

    def frontier(self):
        out = []
        for e in self.ENGINES:
            for kind in ("i", "d0", "d1"):
                if self.counts[(e, kind)] > 0:
                    out.append((e, kind, self.counts[(e, kind)]))
        return out

    def emit(self):
        nc = self.nc
        sems = {}
        for e in self.ENGINES:
            for kind in ("i", "d0", "d1"):
                if self.counts[(e, kind)] > 0:
                    sems[(e, kind)] = nc.alloc_semaphore(f"s_{e}_{kind}")
        with nc.Block() as block:
            for e in self.ENGINES:
                if not self.ops[e]:
                    continue

                def make(e):
                    def run(eng):
                        watermark = {}
                        mysem = sems.get((e, "i"))
                        for fn, deps, dma, handle, fdeps in self.ops[e]:
                            need = {}
                            for (de, dk, dv) in deps:
                                if de == e and dk == "i":
                                    continue
                                key = (de, dk)
                                need[key] = max(need.get(key, 0), dv)
                            for (de, dk, dv) in fdeps:
                                key = (de, dk)
                                need[key] = max(need.get(key, 0), dv)
                            for key, val in sorted(need.items()):
                                if watermark.get(key, 0) >= val:
                                    continue
                                eng.wait_ge(sems[key], val)
                                watermark[key] = val
                            if dma:
                                thunks = fn if isinstance(fn, (list, tuple)) else [fn]
                                kind = handle[1]
                                dsem = sems[(e, kind)]
                                before = handle[2] - 16 * len(thunks)
                                if before > 0 and watermark.get((e, kind), 0) < before:
                                    eng.wait_ge(dsem, before)
                                    watermark[(e, kind)] = before
                                for th in thunks:
                                    th(eng).then_inc(dsem, 16)
                            else:
                                fn(eng).then_inc(mysem, 1)
                        for kind in ("d0", "d1"):
                            if sems.get((e, kind)) is not None and self.counts[(e, kind)] > 0:
                                eng.wait_ge(sems[(e, kind)], self.counts[(e, kind)])
                    return run

                getattr(block, e)(make(e))


class Scope:

    def __init__(self, nc, prefix):
        self.nc = nc
        self.prefix = prefix
        self.guards = []

    def A(self, name, shape, dtype):
        g = self.nc.sbuf_tensor(self.prefix + name, shape, dtype)
        h = g.__enter__()
        self.guards.append(g)
        return h

    def P(self, name, shape, dtype):
        g = self.nc.psum_tensor(self.prefix + name, shape, dtype)
        h = g.__enter__()
        self.guards.append(g)
        return h

    def close(self):
        for g in reversed(self.guards):
            g.__exit__(None, None, None)
        self.guards = []


def dma(dst, src, slow=False):
    if slow:
        return lambda e: e.dma_start(out=dst, in_=src, allow_slow_non_contiguous=True)
    return lambda e: e.dma_start(out=dst, in_=src)


def _p1_drams(nc, kind_out="ExternalOutput"):
    return {
        "x": nc.dram_tensor("x", [D, T], F32, kind="ExternalInput").ap(),
        "wx": nc.dram_tensor("wx", [D, DCH], F32, kind="ExternalInput").ap(),
        "wz": nc.dram_tensor("wz", [D, DCH], F32, kind="ExternalInput").ap(),
        "wp": nc.dram_tensor("wp", [D, 96], F32, kind="ExternalInput").ap(),
        "wdt": nc.dram_tensor("wdt", [R_SSM, DCH], F32, kind="ExternalInput").ap(),
        "bdt": nc.dram_tensor("bdt", [2, 128, 1], F32, kind="ExternalInput").ap(),
        "alog": nc.dram_tensor("alog", [2, 128, N_SSM], F32, kind="ExternalInput").ap(),
        "cw": nc.dram_tensor("cw", [2, 128, K_CONV], F32, kind="ExternalInput").ap(),
        "dp": nc.dram_tensor("dp", [2, 128, 1], F32, kind="ExternalInput").ap(),
        "y1": nc.dram_tensor("y1", [DCH, T], BF16, kind=kind_out).ap(),
    }


def build_phase1(nchk=None, debug=False):
    nc = bass.Bass("TRN2", target_bir_lowering=False, detect_race_conditions=False)
    E = _p1_drams(nc)
    pr = Prog(nc)
    sc = Scope(nc, "p1_")
    emit_phase1(nc, pr, E, sc, nchk=nchk, debug=debug)
    sc.close()
    pr.emit()
    return nc


def emit_phase1(nc, pr, E, sc, nchk=None, debug=False):
    TC = 2048
    NCHK = T // TC if nchk is None else nchk
    HTC = TC // 2
    x_d = E["x"]
    wx_d = E["wx"]
    wz_d = E["wz"]
    wp_d = E["wp"]
    wdt_d = E["wdt"]
    bdt_d = E["bdt"]
    alog_d = E["alog"]
    cw_d = E["cw"]
    dp_d = E["dp"]
    y1_d = E["y1"]

    A = sc.A
    P = sc.P

    wstage = [A(f"wstage{i}", [128, DCH], F32) for i in range(2)]
    wxb = [A(f"wxb{k}", [128, DCH], BF16) for k in range(8)]
    wzb = [A(f"wzb{k}", [128, DCH], BF16) for k in range(8)]
    wpb = [A(f"wpb{k}", [128, 96], BF16) for k in range(8)]
    wdtb = A("wdtb", [R_SSM, DCH], BF16)
    bdt = [A(f"bdt{m}", [128, 1], F32) for m in range(2)]
    alog = [A(f"alog{m}", [128, N_SSM], F32) for m in range(2)]
    cw = [A(f"cw{m}", [128, K_CONV], F32) for m in range(2)]
    dpar = [A(f"dpar{m}", [128, 1], F32) for m in range(2)]
    Acol = [A(f"Acol{m}", [128, N_SSM], F32) for m in range(2)]
    rcol = [A(f"rcol{m}", [128, N_SSM], F32) for m in range(2)]
    nrcol = [A(f"nrcol{m}", [128, N_SSM], F32) for m in range(2)]
    mcol = [A(f"mcol{m}", [128, N_SSM], F32) for m in range(2)]
    t_ea = [A(f"t_ea{m}", [128, N_SSM], F32) for m in range(2)]
    t_s1 = [A(f"t_s1{m}", [128, N_SSM], F32) for m in range(2)]
    t_s2 = [A(f"t_s2{m}", [128, N_SSM], F32) for m in range(2)]
    t_s3 = [A(f"t_s3{m}", [128, N_SSM], F32) for m in range(2)]
    ones_f = A("ones_f", [1, 128], F32)

    xstage = [A(f"xstage{i}", [128, 512], F32) for i in range(2)]
    xbf = [A(f"xbf{k}", [128, 512], BF16) for k in range(8)]
    xres = [A(f"xres{m}", [128, K_CONV - 1 + TC], F32) for m in range(2)]
    halo = [A(f"halo{m}", [128, K_CONV - 1], F32) for m in range(2)]
    sz_sb = [A(f"sz{m}", [128, TC], F32) for m in range(2)]
    cva = A("cva", [128, TC], F32)
    cvb = A("cvb", [128, TC], F32)
    u_sb = [A(f"u{m}", [128, TC], F32) for m in range(2)]
    dt_sb = [A(f"dt{m}", [128, TC], F32) for m in range(2)]
    par_sb = A("par_sb", [96, TC], F32)
    dtu_bf = A("dtu_bf", [R_SSM, TC], BF16)
    sp_r = A("sp_r", [128, 512], F32)
    sp_t = A("sp_t", [128, 512], F32)
    bp_n = A("bp_n", [1, TC], F32)
    cp_n = A("cp_n", [1, TC], F32)
    bp_rep = A("bp_rep", [128, TC], F32)
    cp_rep = A("cp_rep", [128, TC], F32)
    at_sb = A("at_sb", [128, TC], F32)
    s1_sb = A("s1_sb", [128, TC], F32)
    bt_sb = A("bt_sb", [128, TC], F32)
    w_sb = A("w_sb", [128, TC], F32)
    yacc = [A(f"yacc{m}", [128, TC], F32) for m in range(2)]
    state = [A(f"state{m}", [128, N_SSM], F32) for m in range(2)]
    y1bf = A("y1bf", [128, TC], BF16)

    ps = [P(f"ps{i}", [128, 512], F32) for i in range(4)]
    pbp = P("pbp", [128, HTC], F32)
    pcp = P("pcp", [128, HTC], F32)

    pr.pv = {}
    add = pr.add

    wx_r = wx_d.rearrange("(a p) c -> a p c", p=128)
    wz_r = wz_d.rearrange("(a p) c -> a p c", p=128)
    wp_r = wp_d.rearrange("(a p) c -> a p c", p=128)
    srcs = [(wx_r[k], wxb[k], DCH) for k in range(8)] + \
           [(wz_r[k], wzb[k], DCH) for k in range(8)] + \
           [(wp_r[k], wpb[k], 96) for k in range(8)]
    prev_cast = {}
    cast_h = None
    for i, (src, dstb, w) in enumerate(srcs):
        sbuf = wstage[i % 2]
        ldh = add("sync", dma(sbuf[:, 0:w], src), deps=[prev_cast.get(i % 2)], dma=True)
        cast_h = add("vector", (lambda dstb, sbuf, w: lambda e: e.tensor_copy(
            dstb[:, 0:w], sbuf[:, 0:w]))(dstb, sbuf, w), deps=[ldh])
        prev_cast[i % 2] = cast_h
    ld_dt = add("sync", dma(wstage[0][0:R_SSM, 0:DCH], wdt_d), deps=[prev_cast[0]], dma=True)
    cast_w_done = add("vector", lambda e: e.tensor_copy(wdtb[:], wstage[0][0:R_SSM, 0:DCH]),
                      deps=[ld_dt])
    ld_s = add("sync", [dma(bdt[m][:], bdt_d[m]) for m in range(2)]
               + [dma(alog[m][:], alog_d[m]) for m in range(2)]
               + [dma(cw[m][:], cw_d[m]) for m in range(2)]
               + [dma(dpar[m][:], dp_d[m]) for m in range(2)], dma=True)
    ones_set = add("vector", lambda e: e.memset(ones_f[:], 1.0))

    setup_done = []
    for m in range(2):
        ea = add("scalar", (lambda m: lambda e: e.activation(t_ea[m][:], alog[m][:], AF.Exp))(m),
                 deps=[ld_s])
        p_ = add("vector", (lambda m: lambda e: e.tensor_scalar(Acol[m][:], t_ea[m][:], -1.0, None, ALU.mult))(m),
                 deps=[ea])
        p_ = add("vector", (lambda m: lambda e: e.tensor_scalar(mcol[m][:], t_ea[m][:], 1e-8, None, ALU.is_lt))(m),
                 force_deps=[p_])
        h1 = p_ = add("vector", (lambda m: lambda e: e.tensor_scalar(t_s1[m][:], Acol[m][:], 1e-10, None, ALU.add))(m),
                      force_deps=[p_])
        h2 = add("vector", (lambda m: lambda e: e.reciprocal(t_s2[m][:], t_s1[m][:]))(m), force_deps=[h1])
        h3 = add("vector", (lambda m: lambda e: e.tensor_scalar(t_s3[m][:], mcol[m][:], -1.0, 1.0, ALU.mult, ALU.add))(m),
                 force_deps=[h2])
        p_ = add("vector", (lambda m: lambda e: e.tensor_tensor(rcol[m][:], t_s2[m][:], t_s3[m][:], ALU.mult))(m),
                 force_deps=[h3])
        p_ = add("vector", (lambda m: lambda e: e.tensor_scalar(nrcol[m][:], rcol[m][:], -1.0, None, ALU.mult))(m),
                 force_deps=[p_])
        add("vector", (lambda m: lambda e: e.memset(state[m][:], 0.0))(m))
        setup_done.append(add("vector", (lambda m: lambda e: e.memset(halo[m][:], 0.0))(m)))

    pv = {}
    prev_last_acc = None
    prev_hs = [None, None]
    prev_rowdma = None
    prev_st = None
    prev_e1 = [None, None]
    prev_e2 = None

    for c in range(NCHK):
        c0 = c * TC
        par_ev = []
        dt_done = [None, None]
        xres_ev = [[], []]
        sz_ev = [[], []]
        last_xreader = pv.get("last_xreader")
        for tt in range(4):
            t0 = tt * 512
            casts = []
            for k in range(8):
                sbuf = xstage[k % 2]
                ldh = add("sync", dma(sbuf[:], x_d[k * 128:(k + 1) * 128, c0 + t0:c0 + t0 + 512]),
                          deps=[pv.get(("xstage_cast", k % 2))], dma=True)
                ch = add("vector", (lambda k, sbuf: lambda e: e.tensor_copy(xbf[k][:], sbuf[:]))(k, sbuf),
                         deps=[ldh, last_xreader])
                pv[("xstage_cast", k % 2)] = ch
                casts.append(ch)
            mm_deps = [cast_w_done, casts[-1]]

            def mk_acc8(psdst, rows, wb, mslice):
                def f(e):
                    inst = None
                    for k in range(8):
                        inst = e.matmul(psdst[0:rows, :], wb[k][:, mslice],
                                        xbf[k][:], start=(k == 0), stop=(k == 7))
                    return inst
                return f

            par_mm = add("tensor", mk_acc8(ps[0], 96, wpb, slice(0, 96)),
                         deps=mm_deps + [pv.get("ps0"), dt_done[1]])
            ev = add("scalar", (lambda t0: lambda e: e.copy(par_sb[:, t0:t0 + 512], ps[0][0:96, :]))(t0),
                     deps=[par_mm, prev_rowdma])
            par_ev.append(ev)
            dtu_c = add("vector", (lambda t0: lambda e: e.tensor_copy(
                dtu_bf[:, t0:t0 + 512], par_sb[0:64, t0:t0 + 512]))(t0),
                deps=[ev, pv.get("last_dtmm")])
            for m in range(2):
                msl = slice(m * 128, (m + 1) * 128)
                xr_mm = add("tensor", mk_acc8(ps[1 + m], 128, wxb, msl),
                            deps=mm_deps + ([xres_ev[m][-1]] if xres_ev[m] else [pv.get(f"ps{1+m}")]))
                xe = add("scalar", (lambda m, t0: lambda e: e.copy(
                    xres[m][:, K_CONV - 1 + t0:K_CONV - 1 + t0 + 512], ps[1 + m][:, :]))(m, t0),
                    deps=[xr_mm, prev_hs[m]])
                xres_ev[m].append(xe)
                z_mm = add("tensor", mk_acc8(ps[1 + m], 128, wzb, msl), deps=[xe])
                last_xreader = z_mm
                ze = add("scalar", (lambda m, t0: lambda e: e.copy(
                    sz_sb[m][:, t0:t0 + 512], ps[1 + m][:, :]))(m, t0),
                    deps=[z_mm, pv.get(("szwar", m))])
                sz_ev[m].append(ze)
                psi_dt = 3 if m == 0 else 0
                dt_mm = add("tensor", (lambda m, t0, psi_dt: lambda e: e.matmul(
                    ps[psi_dt][:, :], wdtb[:, m * 128:(m + 1) * 128], dtu_bf[:, t0:t0 + 512],
                    start=True, stop=True))(m, t0, psi_dt),
                    deps=[dtu_c, dt_done[m] or pv.get(f"ps{psi_dt}")]
                    + ([par_ev[-1]] if psi_dt == 0 else []))
                pv["last_dtmm"] = dt_mm
                sp1 = add("scalar", (lambda m, psi_dt: lambda e: e.activation(
                    sp_r[:], ps[psi_dt][:, :], AF.Relu, bias=bdt[m][:]))(m, psi_dt),
                    deps=[dt_mm, dt_done[1 - m] or pv.get("last_dtdone")])
                add("scalar", (lambda m, psi_dt: lambda e: e.activation(
                    sp_t[:], ps[psi_dt][:, :], AF.Abs, bias=bdt[m][:]))(m, psi_dt))
                add("scalar", lambda e: e.activation(sp_t[:], sp_t[:], AF.Exp, scale=-1.0))
                sp4 = add("scalar", lambda e: e.activation(sp_t[:], sp_t[:], AF.Ln, bias=1.0))
                dt_done[m] = add("vector", (lambda m, t0: lambda e: e.tensor_add(
                    dt_sb[m][:, t0:t0 + 512], sp_r[:], sp_t[:]))(m, t0), deps=[sp4])
        pv["last_xreader"] = last_xreader
        pv["last_dtdone"] = dt_done[1]

        u_done = [None, None]
        sz_done = [None, None]
        hs_h = [None, None]
        for m in range(2):
            hcopy = add("vector", (lambda m: lambda e: e.tensor_copy(
                xres[m][:, 0:K_CONV - 1], halo[m][:]))(m),
                deps=[setup_done[m]])
            add("vector", (lambda m: lambda e: e.tensor_scalar(
                cva[:], xres[m][:, 0:TC], cw[m][:, 0:1], None, ALU.mult))(m),
                deps=[hcopy, ld_s, u_done[0]] + xres_ev[m])
            add("vector", (lambda m: lambda e: e.scalar_tensor_tensor(
                cvb[:], xres[m][:, 1:1 + TC], cw[m][:, 1:2], cva[:], ALU.mult, ALU.add))(m))
            add("vector", (lambda m: lambda e: e.scalar_tensor_tensor(
                cva[:], xres[m][:, 2:2 + TC], cw[m][:, 2:3], cvb[:], ALU.mult, ALU.add))(m))
            cv4 = add("vector", (lambda m: lambda e: e.scalar_tensor_tensor(
                cvb[:], xres[m][:, 3:3 + TC], cw[m][:, 3:4], cva[:], ALU.mult, ALU.add))(m))
            hs_h[m] = add("vector", (lambda m: lambda e: e.tensor_copy(
                halo[m][:], xres[m][:, TC:TC + K_CONV - 1]))(m))
            u_done[m] = add("scalar", (lambda m: lambda e: e.activation(
                u_sb[m][:], cvb[:], AF.Silu))(m), deps=[cv4, prev_e1[m]])
            sz_done[m] = add("scalar", (lambda m: lambda e: e.activation(
                sz_sb[m][:], sz_sb[m][:], AF.Silu))(m), deps=sz_ev[m] + [prev_e2])
        prev_hs = hs_h

        last_acc = None
        for n in range(N_SSM):
            rowdma = add("gpsimd", [
                dma(bp_n[:], par_sb[64 + n:65 + n, :]),
                dma(cp_n[:], par_sb[80 + n:81 + n, :])],
                deps=par_ev + [pv.get("rowmm")], dma=True)
            prev_rowdma = rowdma
            reps_done = []
            rowmm = None
            for (src, psu, rep, nm) in [(bp_n, pbp, bp_rep, "bp"), (cp_n, pcp, cp_rep, "cp")]:
                for hh in range(2):
                    def mk_b(src, psu, hh):
                        def f(e):
                            inst = None
                            for s4 in range(2):
                                off = hh * HTC + s4 * 512
                                inst = e.matmul(psu[:, s4 * 512:s4 * 512 + 512], ones_f[:],
                                                src[0:1, off:off + 512], start=True, stop=True)
                            return inst
                        return f
                    mm = add("tensor", mk_b(src, psu, hh),
                             deps=[rowdma, ones_set, pv.get(("bcev_last", nm))])
                    rowmm = mm
                    ev = add("scalar", (lambda psu, rep, hh: lambda e: e.copy(
                        rep[:, hh * HTC:(hh + 1) * HTC], psu[:, :]))(psu, rep, hh),
                        deps=[mm, last_acc or prev_last_acc])
                    pv[("bcev_last", nm)] = ev
                    reps_done.append(ev)
            pv["rowmm"] = rowmm
            for m in range(2):
                at = add("scalar", (lambda m, n: lambda e: e.activation(
                    at_sb[:], dt_sb[m][:], AF.Exp, scale=Acol[m][:, n:n + 1]))(m, n),
                    deps=dt_done + [setup_done[m], last_acc or prev_last_acc])
                s1 = add("vector", (lambda m, n: lambda e: e.tensor_scalar(
                    s1_sb[:], at_sb[:], rcol[m][:, n:n + 1], nrcol[m][:, n:n + 1],
                    ALU.mult, ALU.add))(m, n), deps=[at])
                add("vector", (lambda m, n: lambda e: e.scalar_tensor_tensor(
                    bt_sb[:], dt_sb[m][:], mcol[m][:, n:n + 1], s1_sb[:], ALU.mult, ALU.add))(m, n))
                add("vector", (lambda m: lambda e: e.tensor_tensor(
                    w_sb[:], u_sb[m][:], bp_rep[:], ALU.mult))(m), deps=[u_done[m]] + reps_done)
                Xv = add("vector", lambda e: e.tensor_tensor(s1_sb[:], bt_sb[:], w_sb[:], ALU.mult))
                if c == 0:
                    scn = add("vector", lambda e: e.tensor_tensor_scan(
                        bt_sb[:], at_sb[:], s1_sb[:], 0.0, ALU.mult, ALU.add),
                        deps=[at], force_deps=[Xv])
                else:
                    scn = add("vector", (lambda m, n: lambda e: e.tensor_tensor_scan(
                        bt_sb[:], at_sb[:], s1_sb[:], state[m][:, n:n + 1], ALU.mult, ALU.add))(m, n),
                        deps=[at], force_deps=[Xv])
                add("vector", (lambda m, n: lambda e: e.tensor_copy(
                    state[m][:, n:n + 1], bt_sb[:, TC - 1:TC]))(m, n), force_deps=[scn])
                add("vector", lambda e: e.tensor_tensor(at_sb[:], bt_sb[:], cp_rep[:], ALU.mult))
                if n == 0:
                    last_acc = add("vector", (lambda m: lambda e: e.tensor_copy(yacc[m][:], at_sb[:]))(m),
                                   deps=[prev_e1[m]])
                else:
                    last_acc = add("vector", (lambda m: lambda e: e.tensor_add(
                        yacc[m][:], yacc[m][:], at_sb[:]))(m))
                pv[("lastn", m)] = last_acc

        for m in range(2):
            e1 = add("vector", (lambda m: lambda e: e.scalar_tensor_tensor(
                w_sb[:], u_sb[m][:], dpar[m][:], yacc[m][:], ALU.mult, ALU.add))(m),
                deps=[pv[("lastn", m)], ld_s])
            e2 = add("vector", (lambda m: lambda e: e.tensor_tensor(
                y1bf[:], w_sb[:], sz_sb[m][:], ALU.mult))(m),
                deps=[sz_done[m], prev_st])
            prev_e2 = e2
            st = add("gpsimd", [dma(y1_d[m * 128:(m + 1) * 128, c0:c0 + TC], y1bf[:])],
                     deps=[e2], dma=True)
            prev_st = st
            prev_e1[m] = e1
            pv[("szwar", m)] = e2
        pv["ps0"] = dt_done[1]
        pv["ps1"] = sz_ev[0][-1]
        pv["ps2"] = sz_ev[1][-1]
        pv["ps3"] = dt_done[0]
        prev_last_acc = pv[("lastn", 1)]

    if debug:
        def dump(name, tile, shape, dtype=F32, dep=None):
            d_ = nc.dram_tensor("dbg_" + name, shape, dtype, kind="ExternalOutput").ap()
            add("gpsimd", [dma(d_, tile)], deps=[dep] if dep else [prev_last_acc], dma=True)
        dump("Acol0", Acol[0][:], [128, N_SSM])
        dump("rcol0", rcol[0][:], [128, N_SSM])
        dump("mcol0", mcol[0][:], [128, N_SSM])
        dump("dt0", dt_sb[0][:], [128, TC])
        dump("u0", u_sb[0][:], [128, TC])
        dump("sz0", sz_sb[0][:], [128, TC])
        dump("par", par_sb[:], [96, TC])
        dump("bprep", bp_rep[:], [128, TC])
        dump("cprep", cp_rep[:], [128, TC])
        dump("yacc0", yacc[0][:], [128, TC])
        dump("state0", state[0][:], [128, N_SSM])
        dump("dtu", dtu_bf[:], [R_SSM, TC], BF16)
        dump("xres0", xres[0][:], [128, K_CONV - 1 + TC])


def emit_ln(pr, tiles, c0, w, ps_m, ps_v, hbf2, sq2, rstd, c128, epsc,
            tile_deps=(), gamma=None, beta=None):
    add = pr.add
    pv = pr.pv
    mms = []
    for k in range(8):
        cast = add("vector", (lambda k: lambda e: e.tensor_copy(
            hbf2[k % 2][:, 0:w], tiles[k][:, c0:c0 + w]))(k),
            deps=list(tile_deps) + [pv.get(("ln_hbf", k % 2))])
        mm = add("tensor", (lambda k: lambda e: e.matmul(
            ps_m[:, 0:w], c128[:], hbf2[k % 2][:, 0:w],
            start=(k == 0), stop=(k == 7)))(k),
            deps=[cast] + ([pv.get("ln_psm")] if k == 0 else []))
        pv[("ln_hbf", k % 2)] = mm
        mms.append(mm)
    xcs = []
    for k in range(8):
        xcs.append(add("vector", (lambda k: lambda e: e.tensor_tensor(
            tiles[k][:, c0:c0 + w], tiles[k][:, c0:c0 + w], ps_m[:, 0:w],
            ALU.subtract))(k), deps=[mms[7]]))
    pv["ln_psm"] = xcs[-1]
    vms = []
    for k in range(8):
        sq = add("scalar", (lambda k: lambda e: e.activation(
            sq2[k % 2][:, 0:w], tiles[k][:, c0:c0 + w], AF.Square))(k),
            deps=[xcs[k], pv.get(("ln_sq", k % 2))])
        vm = add("tensor", (lambda k: lambda e: e.matmul(
            ps_v[:, 0:w], c128[:], sq2[k % 2][:, 0:w],
            start=(k == 0), stop=(k == 7)))(k),
            deps=[sq] + ([pv.get("ln_psv")] if k == 0 else []))
        pv[("ln_sq", k % 2)] = vm
        vms.append(vm)
    eps_ap, nhalf_ap, cdep = epsc
    r1 = add("scalar", lambda e: e.activation(rstd[:, 0:w], ps_v[:, 0:w], AF.Ln, bias=eps_ap),
             deps=[vms[7], pv.get("ln_rstd"), cdep])
    r2 = add("scalar", lambda e: e.activation(rstd[:, 0:w], rstd[:, 0:w], AF.Exp, scale=nhalf_ap))
    pv["ln_psv"] = r2
    apps = []
    for k in range(8):
        a1 = add("vector", (lambda k: lambda e: e.tensor_tensor(
            tiles[k][:, c0:c0 + w], tiles[k][:, c0:c0 + w], rstd[:, 0:w], ALU.mult))(k),
            deps=[r2])
        if gamma is not None:
            a1 = add("vector", (lambda k: lambda e: e.tensor_scalar(
                tiles[k][:, c0:c0 + w], tiles[k][:, c0:c0 + w], gamma[k][:], beta[k][:],
                ALU.mult, ALU.add))(k))
        apps.append(a1)
    pv["ln_rstd"] = apps[-1]
    return apps


def _p2_drams(nc, kind_out="ExternalOutput"):
    SP = SPAN + 1
    return {
        "xh": nc.dram_tensor("xh", [D, SP], F32, kind="ExternalInput").ap(),
        "y1h": nc.dram_tensor("y1h", [D, SP], BF16, kind="ExternalInput").ap(),
        "woT": nc.dram_tensor("woT", [D, D], F32, kind="ExternalInput").ap(),
        "wrT": nc.dram_tensor("wrT", [D, D], F32, kind="ExternalInput").ap(),
        "wkT": nc.dram_tensor("wkT", [D, D], F32, kind="ExternalInput").ap(),
        "wvT": nc.dram_tensor("wvT", [D, D], F32, kind="ExternalInput").ap(),
        "winT": nc.dram_tensor("winT", [D, 3 * D], F32, kind="ExternalInput").ap(),
        "mur": nc.dram_tensor("mur", [8, 128, 1], F32, kind="ExternalInput").ap(),
        "muk": nc.dram_tensor("muk", [8, 128, 1], F32, kind="ExternalInput").ap(),
        "sinT": nc.dram_tensor("sinT", [128, SPAN], BF16, kind="ExternalInput").ap(),
        "cosT": nc.dram_tensor("cosT", [128, SPAN], BF16, kind="ExternalInput").ap(),
        "x2": nc.dram_tensor("x2", [D, SPAN], F32, kind=kind_out).ap(),
        "qb": nc.dram_tensor("qb", [D, SPAN], BF16, kind=kind_out).ap(),
        "qrb": nc.dram_tensor("qrb", [D, SPAN], BF16, kind=kind_out).ap(),
        "kb": nc.dram_tensor("kb", [D, SPAN], BF16, kind=kind_out).ap(),
        "krb": nc.dram_tensor("krb", [D, SPAN], BF16, kind=kind_out).ap(),
        "vb": nc.dram_tensor("vb", [D, SPAN], BF16, kind=kind_out).ap(),
        "x2s": nc.dram_tensor("x2s", [D, SPAN], BF16, kind="Internal").ap(),
    }


def build_phase2():
    nc = bass.Bass("TRN2", target_bir_lowering=False, detect_race_conditions=False)
    E = _p2_drams(nc)
    pr = Prog(nc)
    sc = Scope(nc, "p2_")
    emit_phase2(nc, pr, E, sc)
    sc.close()
    pr.emit()
    return nc


def emit_phase2(nc, pr, E, sc):
    SP = SPAN + 1
    xh_d = E["xh"]
    y1h_d = E["y1h"]
    woT_d = E["woT"]
    wrT_d = E["wrT"]
    wkT_d = E["wkT"]
    wvT_d = E["wvT"]
    winT_d = E["winT"]
    mur_d = E["mur"]
    muk_d = E["muk"]
    sinT_d = E["sinT"]
    cosT_d = E["cosT"]
    x2_d = E["x2"]
    qb_d = E["qb"]
    qrb_d = E["qrb"]
    kb_d = E["kb"]
    krb_d = E["krb"]
    vb_d = E["vb"]
    x2s_d = E["x2s"]

    A = sc.A
    P = sc.P
    x1 = [A(f"x1_{k}", [128, SP], F32) for k in range(8)]
    tx2 = [A(f"tx2_{k}", [128, 512], F32) for k in range(8)]
    wstage = A("wstage", [128, 3 * D], F32)
    wbuf = [A(f"wbuf{k}", [128, 3 * D], BF16) for k in range(8)]
    y1t = [A(f"y1t{k}", [128, 512], BF16) for k in range(8)]
    hbf2 = [A(f"hbf{i}", [128, 512], BF16) for i in range(2)]
    sq2 = [A(f"sq{i}", [128, 512], BF16) for i in range(2)]
    rstd = A("rstd", [128, 512], F32)
    mur = [A(f"mur{k}", [128, 1], F32) for k in range(8)]
    muk = [A(f"muk{k}", [128, 1], F32) for k in range(8)]
    xr_bf = [A(f"xrbf{k}", [128, 512], BF16) for k in range(8)]
    xk_bf = [A(f"xkbf{k}", [128, 512], BF16) for k in range(8)]
    ka_bf = [A(f"kabf{k}", [128, 512], BF16) for k in range(8)]
    r_sb = A("r_sb", [128, 512], F32)
    t_f1 = A("t_f1", [128, 512], F32)
    t_f2 = A("t_f2", [128, 512], F32)
    sinT = A("sinT_s", [128, SPAN], BF16)
    cosT = A("cosT_s", [128, SPAN], BF16)
    x2bf = [A(f"x2bf{k}", [128, 512], BF16) for k in range(8)]
    q_bf = A("q_bf", [128, 512], BF16)
    rot_bf = A("rot_bf", [128, 512], BF16)
    o_bf = A("o_bf", [128, 512], BF16)
    c128s = A("c128s", [128, 128], BF16)
    Rrot = A("Rrot_s", [128, 128], BF16)
    eps_t = A("eps_t", [128, 1], F32)
    nh_t = A("nh_t", [128, 1], F32)
    c128_d = nc.inline_tensor(np.full((128, 128), 1.0 / 1024.0, dtype=NBF), name="c128_p2")
    Rnp = np.zeros((128, 128), dtype=NBF)
    for i in range(64):
        Rnp[2 * i, 2 * i + 1] = 1.0
        Rnp[2 * i + 1, 2 * i] = -1.0
    Rrot_d = nc.inline_tensor(Rnp, name="Rrot")

    ps = [P(f"ps{i}", [128, 512], F32) for i in range(8)]

    pr.pv = {}
    add = pr.add
    pv = pr.pv

    def psdep(i):
        return pv.get(("ps", i))

    def psset(i, h):
        pv[("ps", i)] = h

    cst1 = add("vector", lambda e: e.memset(eps_t[:], 1e-5))
    cst2 = add("vector", lambda e: e.memset(nh_t[:], -0.5))
    epsc = (eps_t[:], nh_t[:], cst2)
    ld_x = add("sync", [dma(x1[k][:], xh_d[k * 128:(k + 1) * 128, :]) for k in range(8)], dma=True)
    ld_c = add("sync", [dma(c128s[:], c128_d.ap()), dma(Rrot[:], Rrot_d.ap()),
                        dma(sinT[:], sinT_d), dma(cosT[:], cosT_d)]
               + [dma(mur[k][:], mur_d[k]) for k in range(8)]
               + [dma(muk[k][:], muk_d[k]) for k in range(8)], dma=True)

    def load_weights(src_d, ncols, col_off, dep_extra=()):
        hs = []
        for k in range(8):
            ldh = add("sync", dma(wstage[:, 0:ncols], src_d[k * 128:(k + 1) * 128, 0:ncols]),
                      deps=[pv.get("wcast")] + list(dep_extra), dma=True)
            ch = add("vector", (lambda k, ncols, col_off: lambda e: e.tensor_copy(
                wbuf[k][:, col_off:col_off + ncols], wstage[:, 0:ncols]))(k, ncols, col_off),
                deps=[ldh] + list(dep_extra))
            pv["wcast"] = ch
            hs.append(ch)
        return hs

    wC = load_weights(woT_d, D, 0)
    tilesC = [(0, 512), (512, 512), (1024, 512), (1536, 512), (2048, 1)]
    for (c0, w) in tilesC:
        ldy = add("sync", [dma(y1t[k][:, 0:w], y1h_d[k * 128:(k + 1) * 128, c0:c0 + w], slow=(w == 1))
                           for k in range(8)], deps=[pv.get("y1done")], dma=True)
        h0s = []
        for o in range(8):
            pso = ps[o % 2]

            def mk8(pso, w, cols, rhs_tiles, rw=None):
                def f(e):
                    inst = None
                    for k in range(8):
                        inst = e.matmul(pso[:, 0:w], wbuf[k][:, cols[0] + 0:cols[1]][:, o * 128:o * 128 + 128] if False else wbuf[k][:, cols[0] + o * 128:cols[0] + (o + 1) * 128],
                                        rhs_tiles[k][:, 0:w] if rw is None else rhs_tiles[k][:, rw[0]:rw[1]],
                                        start=(k == 0), stop=(k == 7))
                    return inst
                return f
            mmo = add("tensor", mk8(pso, w, (0, D), y1t), deps=wC + [ldy, psdep(o % 2)])
            h0 = add("vector", (lambda o, w, pso, c0: lambda e: e.tensor_tensor(
                x1[o][:, c0:c0 + w], pso[:, 0:w], x1[o][:, c0:c0 + w], ALU.add))(o, w, pso, c0),
                deps=[mmo, ld_x])
            psset(o % 2, h0)
            h0s.append(h0)
        pv["y1done"] = mmo
        emit_ln(pr, x1, c0, w, ps[2][:, :], ps[3][:, :], hbf2, sq2, rstd, c128s,
                epsc, h0s + [ld_c, cst1])
        pv["ln_psm_guard"] = None

    wD = load_weights(wrT_d, D, 0, dep_extra=[pv["y1done"]])
    wD += load_weights(wkT_d, D, D)
    wD += load_weights(wvT_d, D, 2 * D)
    x2st = []
    for tt in range(4):
        s = 1 + tt * 512
        c0 = tt * 512
        shfts = []
        for k in range(8):
            add("vector", (lambda k, s: lambda e: e.tensor_tensor(
                t_f1[:], x1[k][:, s:s + 512], x1[k][:, s - 1:s - 1 + 512], ALU.subtract))(k, s),
                deps=[ld_c, pv.get("xrwar")])
            xrh = add("vector", (lambda k, s: lambda e: e.scalar_tensor_tensor(
                xr_bf[k][:], t_f1[:], mur[k][:], x1[k][:, s - 1:s - 1 + 512],
                ALU.mult, ALU.add))(k, s))
            add("vector", (lambda k, s: lambda e: e.tensor_tensor(
                t_f1[:], x1[k][:, s:s + 512], x1[k][:, s - 1:s - 1 + 512], ALU.subtract))(k, s))
            xkh = add("vector", (lambda k, s: lambda e: e.scalar_tensor_tensor(
                xk_bf[k][:], t_f1[:], muk[k][:], x1[k][:, s - 1:s - 1 + 512],
                ALU.mult, ALU.add))(k, s))
            shfts += [xrh, xkh]
        kacts = []
        for o in range(8):
            psi = o % 2
            mmk = add("tensor", (lambda o, psi: lambda e: [
                e.matmul(ps[psi][:, :], wbuf[k][:, D + o * 128:D + (o + 1) * 128],
                         xk_bf[k][:], start=(k == 0), stop=(k == 7)) for k in range(8)][-1])(o, psi),
                deps=wD + shfts + [psdep(psi)])
            a1 = add("scalar", (lambda psi: lambda e: e.activation(
                t_f1[:], ps[psi][:, :], AF.Relu))(psi), deps=[mmk, pv.get("tf1_sc")])
            a2 = add("scalar", (lambda o: lambda e: e.activation(
                ka_bf[o][:], t_f1[:], AF.Square))(o), deps=[pv.get(("kawar", o))])
            pv["tf1_sc"] = a2
            psset(psi, a2)
            kacts.append(a2)
        h1s = []
        for o in range(8):
            mkv = add("tensor", (lambda o: lambda e: [
                e.matmul(ps[2 + o % 2][:, :], wbuf[k][:, 2 * D + o * 128:2 * D + (o + 1) * 128],
                         ka_bf[k][:], start=(k == 0), stop=(k == 7)) for k in range(8)][-1])(o),
                deps=kacts + [psdep(2 + o % 2)])
            pv[("kawar", o)] = mkv
            mr = add("tensor", (lambda o: lambda e: [
                e.matmul(ps[4 + o % 2][:, :], wbuf[k][:, o * 128:(o + 1) * 128],
                         xr_bf[k][:], start=(k == 0), stop=(k == 7)) for k in range(8)][-1])(o),
                deps=shfts + [psdep(4 + o % 2)])
            sg = add("scalar", (lambda o: lambda e: e.activation(
                r_sb[:], ps[4 + o % 2][:, :], AF.Sigmoid))(o), deps=[mr, pv.get("rwar")])
            psset(4 + o % 2, sg)
            of = add("vector", (lambda o: lambda e: e.tensor_tensor(
                t_f2[:], ps[2 + o % 2][:, :], r_sb[:], ALU.mult))(o), deps=[sg, mkv])
            pv["rwar"] = of
            psset(2 + o % 2, of)
            h1 = add("vector", (lambda o, s: lambda e: e.tensor_tensor(
                tx2[o][:], t_f2[:], x1[o][:, s:s + 512], ALU.add))(o, s),
                deps=[pv.get(("tx2war", o))])
            h1s.append(h1)
        apps = emit_ln(pr, tx2, 0, 512, ps[6][:, :], ps[7][:, :], hbf2, sq2, rstd,
                       c128s, epsc, h1s)
        for o in range(8):
            cb = add("vector", (lambda o: lambda e: e.tensor_copy(
                x2bf[o][:], tx2[o][:]))(o), deps=[apps[o], pv.get(("x2bfwar", o))])
            st1 = add("gpsimd", [dma(x2_d[o * 128:(o + 1) * 128, c0:c0 + 512], tx2[o][:]),
                                 dma(x2s_d[o * 128:(o + 1) * 128, c0:c0 + 512], x2bf[o][:])],
                      deps=[apps[o], cb], dma=True)
            pv[("tx2war", o)] = st1
            pv[("x2bfwar", o)] = st1
            x2st.append(st1)

    wE = load_weights(winT_d, 3 * D, 0, dep_extra=[pv["rwar"]])
    for tt in range(4):
        c0 = tt * 512
        casts = []
        for k in range(8):
            ldh = add("gpsimd", dma(x2bf[k][:], x2s_d[k * 128:(k + 1) * 128, c0:c0 + 512]),
                      deps=[pv.get(("x2ld", k))] + x2st, dma=True)
            casts.append(ldh)
        for ot in range(24):
            psi = ot % 2
            mmq = add("tensor", (lambda ot, psi: lambda e: [
                e.matmul(ps[psi][:, :], wbuf[k][:, ot * 128:(ot + 1) * 128],
                         x2bf[k][:], start=(k == 0), stop=(k == 7)) for k in range(8)][-1])(ot, psi),
                deps=wE + casts + [psdep(psi)])
            for k in range(8):
                pv[("x2ld", k)] = mmq
            ev = add("vector", (lambda psi: lambda e: e.tensor_copy(
                q_bf[:], ps[psi][:, :]))(psi), deps=[mmq, pv.get("qbfwar")])
            psset(psi, ev)
            if ot < 16:
                dst, dstr = (qb_d, qrb_d) if ot < 8 else (kb_d, krb_d)
                row = (ot % 8) * 128
                mrot = add("tensor", (lambda psi2: lambda e: e.matmul(
                    ps[2 + psi2][:, :], Rrot[:], q_bf[:], start=True, stop=True))(psi),
                    deps=[ev, ld_c, psdep(2 + psi)])
                rbf = add("vector", (lambda psi2: lambda e: e.tensor_copy(
                    rot_bf[:], ps[2 + psi2][:, :]))(psi), deps=[mrot, pv.get("rotwar")])
                psset(2 + psi, rbf)
                t1h = add("vector", (lambda c0: lambda e: e.tensor_tensor(
                    t_f1[:], q_bf[:], cosT[:, c0:c0 + 512], ALU.mult))(c0),
                    deps=[pv.get("tf1e")])
                t2h = add("vector", (lambda c0: lambda e: e.tensor_tensor(
                    t_f2[:], rot_bf[:], sinT[:, c0:c0 + 512], ALU.mult))(c0))
                pv["rotwar"] = t2h
                ob = add("vector", lambda e: e.tensor_add(o_bf[:], t_f1[:], t_f2[:]))
                pv["tf1e"] = ob
                st = add("gpsimd", [dma(dst[row:row + 128, c0:c0 + 512], q_bf[:]),
                                    dma(dstr[row:row + 128, c0:c0 + 512], o_bf[:])],
                         deps=[ev, ob], dma=True)
                pv["qbfwar"] = st
            else:
                row = (ot - 16) * 128
                st = add("gpsimd", [dma(vb_d[row:row + 128, c0:c0 + 512], q_bf[:])],
                         deps=[ev], dma=True)
                pv["qbfwar"] = st


def build_phase3(debug=False, DBG_H=0):
    nc = bass.Bass("TRN2", target_bir_lowering=False, detect_race_conditions=False)
    E = {
        "qb": nc.dram_tensor("qb", [D, SPAN], BF16, kind="ExternalInput").ap(),
        "qrb": nc.dram_tensor("qrb", [D, SPAN], BF16, kind="ExternalInput").ap(),
        "kA": nc.dram_tensor("kA", [D, T], BF16, kind="ExternalInput").ap(),
        "krA": nc.dram_tensor("krA", [D, T], BF16, kind="ExternalInput").ap(),
        "vT": nc.dram_tensor("vT", [T, D], BF16, kind="ExternalInput").ap(),
        "ya": nc.dram_tensor("ya", [D, SPAN], BF16, kind="ExternalOutput").ap(),
    }
    pr = Prog(nc)
    sc = Scope(nc, "p3_")
    emit_phase3(nc, pr, E, sc, debug=debug, DBG_H=DBG_H, vt_transposed=True)
    sc.close()
    pr.emit()
    return nc


def emit_phase3(nc, pr, E, sc, debug=False, DBG_H=0, vt_transposed=True):
    qb_d = E["qb"]
    qrb_d = E["qrb"]
    kA_d = E.get("kA")
    krA_d = E.get("krA")
    vT_d = E.get("vT")
    ya_d = E["ya"]

    A = sc.A
    P = sc.P
    QT = 512
    ya_sb = [A(f"ya{k}", [128, SPAN], BF16) for k in range(8)]
    kh = [A(f"kh{i}", [64, T], BF16) for i in range(2)]
    krh = [A(f"krh{i}", [64, T], BF16) for i in range(2)]
    vh = [[A(f"vh{i}_{j}", [128, 64], BF16) for j in range(64)] for i in range(2)]
    q_hd = A("q_hd", [64, QT], BF16)
    qr_hd = A("qr_hd", [64, QT], BF16)
    E_sb = [A(f"E_sb{i}", [128, QT], BF16) for i in range(2)]
    Eu_sb = [A(f"Eu_sb{i}", [128, QT], BF16) for i in range(2)]
    den_sb = A("den_sb", [64, QT], F32)
    dacc_sb = A("dacc_sb", [64, QT], F32)
    db_sb = A("db_sb", [64, QT], F32)
    ones_bf = A("ones_bf", [128, 64], BF16)
    scl_t = A("scl_t", [128, 1], F32)
    scl_neg1 = A("scl_neg1", [128, 1], F32)

    psr = [P(f"psr{i}", [128, QT], F32) for i in range(2)]
    psu = [P(f"psu{i}", [128, QT], F32) for i in range(2)]
    psy = P("psy", [64, QT], F32)
    psd = P("psd", [64, QT], F32)

    pr.pv = {}
    add = pr.add
    pv = pr.pv

    ones = add("vector", lambda e: e.memset(ones_bf[:], 1.0))
    sclh = add("vector", lambda e: e.memset(scl_t[:], 0.125))
    sneg = add("vector", lambda e: e.memset(scl_neg1[:], -1.0))

    for h in range(H):
        par = h % 2
        if "k_parts" in E:
            ldk = add("sync", [dma(kh[par][:, r * SPAN:(r + 1) * SPAN], E["k_parts"](r, h))
                               for r in range(4)]
                      + [dma(krh[par][:, r * SPAN:(r + 1) * SPAN], E["kr_parts"](r, h))
                         for r in range(4)],
                      deps=[pv.get(("kdone", par))], dma=True)
            ldv = add("scalar", [
                (lambda j, h2, par2: lambda e: e.dma_start(
                    out=vh[par2][j][:], in_=E["v_slice"](h2, j), transpose=True))(j, h, par)
                for j in range(64)],
                deps=[pv.get(("vdone", par))], dma=True)
        else:
            ldk = add("sync", [dma(kh[par][:], kA_d[h * 64:(h + 1) * 64, :]),
                               dma(krh[par][:], krA_d[h * 64:(h + 1) * 64, :])],
                      deps=[pv.get(("kdone", par))], dma=True)
            ldv = add("gpsimd", [dma(vh[par][j][:], vT_d[j * 128:(j + 1) * 128, h * 64:(h + 1) * 64])
                                 for j in range(64)],
                      deps=[pv.get(("vdone", par))], dma=True)
        qtile = h // 2
        for qt in range(SPAN // QT):
            q0 = qt * QT
            qpart = (h % 2) * 64
            qtl = h // 2
            qcp = add("sync", [dma(q_hd[:], qb_d[h * 64:(h + 1) * 64, q0:q0 + QT]),
                               dma(qr_hd[:], qrb_d[h * 64:(h + 1) * 64, q0:q0 + QT])],
                      deps=[pv.get("qhdw")], dma=True)
            qcp2 = qcp
            if debug and h == DBG_H and qt == 0:
                dbg_qh = nc.dram_tensor("dbg_qh", [64, QT], BF16, kind="ExternalOutput").ap()
                add("gpsimd", [dma(dbg_qh, q_hd[:])], deps=[qcp], dma=True)
            NKT = T // 128
            for kt in range(NKT + 1):
                par2 = kt % 2
                if kt < NKT:
                    k0 = kt * 128
                    mmr = add("tensor", (lambda par, k0, par2: lambda e: [
                        e.matmul(p[par2][:, :], kk[par][:, k0:k0 + 128], qh[:, :],
                                 start=True, stop=True)
                        for (p, kk, qh) in ((psr, krh, qr_hd), (psu, kh, q_hd))][-1])(par, k0, par2),
                        deps=[ldk, qcp2, pv.get(("psrw", par2)), pv.get(("psuw", par2))])
                    er = add("scalar", (lambda par2: lambda e: e.activation(
                        E_sb[par2][:], psr[par2][:, :], AF.Exp, scale=scl_t[:]))(par2),
                        deps=[mmr, sclh, pv.get(("Ew", par2))])
                    pv[("psrw", par2)] = er
                    eu = add("scalar", (lambda par2: lambda e: e.activation(
                        Eu_sb[par2][:], psu[par2][:, :], AF.Exp, scale=scl_t[:]))(par2),
                        deps=[mmr, sclh, pv.get(("Euw", par2))])
                    mmu = mmr
                    pv[("psuw", par2)] = eu
                    pv[("erh", par2)] = er
                    pv[("euh", par2)] = eu
                if kt > 0:
                    kc = kt - 1
                    pc = kc % 2
                    mmy = add("tensor", (lambda par, kc, pc: lambda e: [
                        e.matmul(p[:, :], lh if lh is not None else vh[par][kc][:],
                                 ee[pc][:, :],
                                 start=(kc == 0 if p is psy else kc % 8 == 0),
                                 stop=(kc == NKT - 1 if p is psy else kc % 8 == 7))
                        for (p, lh, ee) in ((psy, None, E_sb), (psd, ones_bf[:], Eu_sb))][-1])(par, kc, pc),
                        deps=[ldv, ones, pv.get(("erh", pc)), pv.get(("euh", pc))]
                        + ([pv.get("psyw")] if kc == 0 else [])
                        + ([pv.get("psdw")] if kc % 8 == 0 else []))
                    pv[("Ew", pc)] = mmy
                    pv[("Euw", pc)] = mmy
                    mmd = mmy
                    if kc % 8 == 7:
                        if kc == 7:
                            dadd = add("vector", (lambda: lambda e: e.tensor_copy(
                                dacc_sb[:], psd[:, :]))(), deps=[mmd, pv.get("daccw")])
                        else:
                            dadd = add("vector", (lambda: lambda e: e.tensor_tensor(
                                dacc_sb[:], dacc_sb[:], psd[:, :], ALU.add))(), deps=[mmd])
                        pv["psdw"] = dadd
            pv[("kdone", par)] = mmu
            pv[("vdone", par)] = mmy
            rcp = add("vector", lambda e: e.reciprocal(db_sb[:], dacc_sb[:]),
                      deps=[pv.get("rcpw")], force_deps=[dadd])
            pv["daccw"] = rcp
            if debug and h == DBG_H and qt == 0:
                dbg_db = nc.dram_tensor("dbg_db", [64, QT], F32, kind="ExternalOutput").ap()
                add("gpsimd", [dma(dbg_db, db_sb[:])], deps=[rcp], dma=True)
            ya1 = add("vector", (lambda qpart, qtl, q0: lambda e: e.tensor_tensor(
                ya_sb[qtl][qpart:qpart + 64, q0:q0 + QT], psy[:, :], db_sb[:], ALU.mult))(qpart, qtl, q0),
                deps=[mmy, pv.get(("yawar", h // 2))], force_deps=[rcp])
            pv["rcpw"] = ya1
            pv["psyw"] = ya1
            pv["qhdw"] = mmu
    st = add("gpsimd", [dma(ya_d[k * 128:(k + 1) * 128, :], ya_sb[k][:]) for k in range(8)],
             deps=pr.frontier(), dma=True)


def _p4_drams(nc, skip_io=False):
    d = {}
    if not skip_io:
        d["ya"] = nc.dram_tensor("ya", [D, SPAN], BF16, kind="ExternalInput").ap()
        d["x2"] = nc.dram_tensor("x2", [D, SPAN], F32, kind="ExternalInput").ap()
    d.update({
        "woT4": nc.dram_tensor("woT4", [D, D], F32, kind="ExternalInput").ap(),
        "lng": nc.dram_tensor("lng", [8, 128, 1], F32, kind="ExternalInput").ap(),
        "lnb": nc.dram_tensor("lnb", [8, 128, 1], F32, kind="ExternalInput").ap(),
        "wg": nc.dram_tensor("wg", [D, N_EXP], F32, kind="ExternalInput").ap(),
        "bg": nc.dram_tensor("bg", [1, N_EXP], F32, kind="ExternalInput").ap(),
        "w1": nc.dram_tensor("w1", [N_EXP, D, D_FF], F32, kind="ExternalInput").ap(),
        "b1h": nc.dram_tensor("b1h", [128, N_EXP * 32], F32, kind="ExternalInput").ap(),
        "w2": nc.dram_tensor("w2", [N_EXP, D_FF, D], F32, kind="ExternalInput").ap(),
        "b2mm": nc.dram_tensor("b2mm", [N_EXP, D], F32, kind="ExternalInput").ap(),
        "yout": nc.dram_tensor("yout", [D, SPAN], F32, kind="ExternalOutput").ap(),
        "x3s": nc.dram_tensor("x3s", [D, SPAN], F32, kind="Internal").ap(),
    })
    return d


def build_phase4(debug=False):
    nc = bass.Bass("TRN2", target_bir_lowering=False, detect_race_conditions=False)
    E = _p4_drams(nc)
    pr = Prog(nc)
    sc = Scope(nc, "p4_")
    emit_phase4(nc, pr, E, sc, debug=debug)
    sc.close()
    pr.emit()
    return nc


def emit_phase4(nc, pr, E, sc, debug=False):
    ya_d = E["ya"]
    x2_d = E["x2"]
    woT_d = E["woT4"]
    lng_d = E["lng"]
    lnb_d = E["lnb"]
    wg_d = E["wg"]
    bg_d = E["bg"]
    w1_d = E["w1"]
    b1_d = E["b1h"]
    w2_d = E["w2"]
    b2_d = E["b2mm"]
    yout_d = E["yout"]
    x3s_d = E["x3s"]

    A = sc.A
    P = sc.P
    TT4 = 1024

    x3bf = [A(f"x3bf{k}", [128, SPAN], BF16) for k in range(8)]
    hbf2 = [A(f"hbf{i}", [128, TT4], BF16) for i in range(2)]
    sq2 = [A(f"sq{i}", [128, TT4], BF16) for i in range(2)]
    rstd = A("rstd", [128, TT4], F32)
    lng = [A(f"lng{k}", [128, 1], F32) for k in range(8)]
    lnb = [A(f"lnb{k}", [128, 1], F32) for k in range(8)]
    c128s = A("c128s", [128, 128], BF16)
    id128 = A("id128", [128, 128], F32)
    ones1_bf = A("ones1_bf", [1, 128], BF16)
    ones1_f = A("ones1_f", [1, 128], F32)
    wg_sb = A("wg_sb", [128, 8 * N_EXP], BF16)
    bg_bf = A("bg_bf", [1, N_EXP], BF16)
    b1c = A("b1c", [128, N_EXP * 32], F32)
    b2c = A("b2c", [N_EXP, D], F32)
    lg_sb = A("lg_sb", [128, N_EXP], F32)
    gbar_t = A("gbar_t", [1, 1], F32)
    mx4 = A("mx4", [128, 4], F32)
    mx2 = A("mx2", [128, 2], F32)
    negone_t = A("negone_t", [128, 1], F32)
    m1 = A("m1", [128, 1], F32)
    m2 = A("m2", [128, 1], F32)
    msk1 = A("msk1", [128, N_EXP], F32)
    msk2 = A("msk2", [128, N_EXP], F32)
    s_sc = A("s_sc", [128, 1], F32)
    w1_sc = A("w1_sc", [128, 1], F32)
    w2_sc = A("w2_sc", [128, 1], F32)
    comb_tm = A("comb_tm", [128, N_EXP], F32)
    comb_fm = A("comb_fm", [N_EXP, SPAN], F32)
    comb_n = A("comb_n", [1, TT4], F32)
    crep = A("crep", [128, TT4], F32)
    eps_t = A("eps_t", [128, 1], F32)
    nh_t = A("nh_t", [128, 1], F32)
    c128_d = nc.inline_tensor(np.full((128, 128), 1.0 / 1024.0, dtype=NBF), name="c128_p4")
    id_d = nc.inline_tensor(np.eye(128, dtype=np.float32), name="id128_c")

    ps = [P(f"ps{i}", [128, 512], F32) for i in range(2)]
    psh = [P(f"psh{i}", [128, TT4], F32) for i in range(2)]
    psy2 = P("psy2", [128, TT4], F32)

    pr.pv = {}
    add = pr.add
    pv = pr.pv

    def psdep(i):
        return pv.get(("ps", i))

    def psset(i, h):
        pv[("ps", i)] = h

    cst1 = add("vector", lambda e: e.memset(eps_t[:], 1e-5))
    cst2 = add("vector", lambda e: e.memset(nh_t[:], -0.5))
    csneg = add("vector", lambda e: e.memset(negone_t[:], -1.0))
    epsc = (eps_t[:], nh_t[:], cst2)
    ld_c = add("sync", [dma(c128s[:], c128_d.ap()), dma(id128[:], id_d.ap()),
                        dma(b1c[:], b1_d), dma(b2c[:], b2_d)]
               + [dma(lng[k][:], lng_d[k]) for k in range(8)]
               + [dma(lnb[k][:], lnb_d[k]) for k in range(8)], dma=True)
    wgst = A("wgst", [128, N_EXP], F32)
    wgh = []
    for k in range(8):
        l1 = add("sync", dma(wgst[:], wg_d[k * 128:(k + 1) * 128, :]),
                 deps=[pv.get("wgc")], dma=True)
        c1 = add("vector", (lambda k: lambda e: e.tensor_copy(
            wg_sb[:, k * N_EXP:(k + 1) * N_EXP], wgst[:]))(k), deps=[l1])
        pv["wgc"] = c1
        wgh.append(c1)
    ones_set = add("vector", lambda e: e.memset(ones1_bf[:], 1.0))
    ones_set2 = add("vector", lambda e: e.memset(ones1_f[:], 1.0))
    lbg = add("sync", dma(wgst[0:1, 0:N_EXP], bg_d), deps=[pv.get("wgc")], dma=True)
    cbg = add("vector", lambda e: e.tensor_copy(bg_bf[:, 0:N_EXP], wgst[0:1, 0:N_EXP]), deps=[lbg])
    pv["wgc"] = cbg

    gF = [nc.sbuf_tensor(f"x2f_{k}", [128, SPAN], F32) for k in range(8)]
    x2f = [g.__enter__() for g in gF]
    gF2 = [nc.sbuf_tensor(f"t1_{k}", [128, TT4], F32) for k in range(8)]
    t1 = [g.__enter__() for g in gF2]
    gF4 = [nc.sbuf_tensor(f"yat_{k}", [128, TT4], BF16) for k in range(8)]
    ya_t = [g.__enter__() for g in gF4]
    gF5 = [nc.sbuf_tensor(f"wo_{k}", [128, D], BF16) for k in range(8)]
    wo_b = [g.__enter__() for g in gF5]
    gF6 = nc.sbuf_tensor("wstF", [128, D], F32)
    wstF = gF6.__enter__()

    ld_x2 = add("sync", [dma(x2f[k][:], x2_d[k * 128:(k + 1) * 128, :]) for k in range(8)], dma=True)
    woh = []
    for k in range(8):
        l1 = add("sync", dma(wstF[:], woT_d[k * 128:(k + 1) * 128, :]),
                 deps=[pv.get("woc")], dma=True)
        c1 = add("vector", (lambda k: lambda e: e.tensor_copy(wo_b[k][:], wstF[:]))(k), deps=[l1])
        pv["woc"] = c1
        woh.append(c1)

    for tt in range(SPAN // TT4):
        c0 = tt * TT4
        ldy = add("sync", [dma(ya_t[k][:], ya_d[k * 128:(k + 1) * 128, c0:c0 + TT4])
                           for k in range(8)], deps=[pv.get("yald")], dma=True)
        h2s = []
        for o in range(8):
            mmo = add("tensor", (lambda o: lambda e: [
                e.matmul(psh[o % 2][:, i * 512:(i + 1) * 512], wo_b[k][:, o * 128:(o + 1) * 128],
                         ya_t[k][:, i * 512:(i + 1) * 512],
                         start=(k == 0), stop=(k == 7))
                for i in range(2) for k in range(8)][-1])(o),
                deps=woh + [ldy, pv.get(("pshw", o % 2))])
            h2 = add("vector", (lambda o, c0: lambda e: e.tensor_tensor(
                t1[o][:], psh[o % 2][:, :], x2f[o][:, c0:c0 + TT4], ALU.add))(o, c0),
                deps=[mmo, ld_x2, pv.get("lastF")])
            pv[("pshw", o % 2)] = h2
            h2s.append(h2)
        pv["yald"] = mmo
        for hw in range(2):
            emit_ln(pr, t1, hw * 512, 512, ps[0][:, :], ps[1][:, :], hbf2, sq2, rstd, c128s,
                    epsc, h2s + [ld_c, cst1], gamma=lng, beta=lnb)
        h3s = []
        for o in range(8):
            h3 = add("vector", (lambda o, c0: lambda e: e.tensor_tensor(
                t1[o][:], t1[o][:], x2f[o][:, c0:c0 + TT4], ALU.add))(o, c0),
                deps=[pv["ln_rstd"]])
            h3s.append(h3)
        apps = []
        for hw in range(2):
            apps_h = emit_ln(pr, t1, hw * 512, 512, ps[0][:, :], ps[1][:, :], hbf2, sq2, rstd,
                             c128s, epsc, h3s)
            apps = apps_h if hw == 0 else [b for b in apps_h]
        apps = [pv["ln_rstd"]] * 8
        for o in range(8):
            cb = add("vector", (lambda o, c0: lambda e: e.tensor_copy(
                x3bf[o][:, c0:c0 + TT4], t1[o][:]))(o, c0), deps=[apps[o]])
            st = add("gpsimd", [dma(x3s_d[o * 128:(o + 1) * 128, c0:c0 + TT4], t1[o][:])],
                     deps=[apps[o]], dma=True)
            pv[("x3st", tt, o)] = st
            pv["lastF"] = st

    barrierF = pr.frontier()
    gF6.__exit__(None, None, None)
    for g in reversed(gF5):
        g.__exit__(None, None, None)
    for g in reversed(gF4):
        g.__exit__(None, None, None)
    for g in reversed(gF2):
        g.__exit__(None, None, None)
    for g in reversed(gF):
        g.__exit__(None, None, None)

    ymoe = [A(f"ymoe{k}", [128, TT4], F32) for k in range(8)]

    expert_order = [0, 2, 4, 6, 1, 3, 5, 7]
    for tt in range(SPAN // TT4):
        c0 = tt * TT4
        gA = ([nc.sbuf_tensor(f"af{tt}_{ft}", [128, TT4], BF16) for ft in range(32)]
              + [nc.sbuf_tensor(f"w2st{tt}_{i}", [128, 1024], F32) for i in range(2)]
              + [nc.sbuf_tensor(f"w2bf{tt}_{i}", [128, 1024], BF16) for i in range(2)]
              + [nc.sbuf_tensor(f"w1st{tt}_{i}", [128, 128], F32) for i in range(32)]
              + [nc.sbuf_tensor(f"w1bf{tt}_{i}", [128, 128], BF16) for i in range(32)])
        hA = [g.__enter__() for g in gA]
        a_full = hA[0:32]
        w2st = hA[32:34]
        w2bf = hA[34:36]
        w1st = hA[36:68]
        w1bf = hA[68:100]
        scopeA_gate = pr.frontier() if tt else list(barrierF)
        gts = []
        for g in range(TT4 // 128):
            tg0 = c0 + g * 128
            mmg = add("tensor", (lambda tg0: lambda e: [
                e.matmul(ps[0][:, 0:N_EXP], x3bf[k][:, tg0:tg0 + 128],
                         wg_sb[:, k * N_EXP:(k + 1) * N_EXP],
                         start=(k == 0), stop=False) for k in range(8)][-1])(tg0),
                deps=barrierF + wgh + [psdep(0), ld_c])
            mmg2 = add("tensor", lambda e: e.matmul(
                ps[0][:, 0:N_EXP], ones1_bf[:], bg_bf[:, 0:N_EXP], start=False, stop=True),
                deps=[ones_set, cbg])
            ev = add("scalar", lambda e: e.copy(lg_sb[:], ps[0][0:128, 0:N_EXP]), deps=[mmg2, pv.get("lgwar")])
            psset(0, ev)
            p_ = add("vector", lambda e: e.tensor_max(mx4[:], lg_sb[:, 0:4], lg_sb[:, 4:8]),
                     deps=[ev], force_deps=[pv.get("trp_prev")])
            p_ = add("vector", lambda e: e.tensor_max(mx2[:], mx4[:, 0:2], mx4[:, 2:4]),
                     force_deps=[p_])
            p_ = add("vector", lambda e: e.tensor_max(m1[:], mx2[:, 0:1], mx2[:, 1:2]),
                     force_deps=[p_])
            mk1 = p_ = add("vector", lambda e: e.tensor_scalar(msk1[:], lg_sb[:], m1[:], None, ALU.is_ge),
                           force_deps=[p_])
            p_ = add("vector", lambda e: e.tensor_scalar(msk2[:], msk1[:], 1e30, None, ALU.mult),
                     force_deps=[p_])
            p_ = add("vector", lambda e: e.tensor_sub(msk2[:], lg_sb[:], msk2[:]),
                     force_deps=[p_])
            p_ = add("vector", lambda e: e.tensor_max(mx4[:], msk2[:, 0:4], msk2[:, 4:8]),
                     force_deps=[p_])
            p_ = add("vector", lambda e: e.tensor_max(mx2[:], mx4[:, 0:2], mx4[:, 2:4]),
                     force_deps=[p_])
            p_ = add("vector", lambda e: e.tensor_max(m2[:], mx2[:, 0:1], mx2[:, 1:2]),
                     force_deps=[p_])
            mk2 = p_ = add("vector", lambda e: e.tensor_scalar(msk2[:], msk2[:], m2[:], None, ALU.is_ge),
                           force_deps=[p_])
            d21 = p_ = add("vector", lambda e: e.tensor_sub(s_sc[:], m2[:], m1[:]),
                           force_deps=[p_])
            sg1 = add("scalar", (lambda: lambda e: e.activation(
                w1_sc[:], s_sc[:], AF.Sigmoid, scale=negone_t[:]))(), deps=[d21, csneg])
            sg2 = add("scalar", lambda e: e.activation(w2_sc[:], s_sc[:], AF.Sigmoid),
                      force_deps=[sg1])
            cmb = p_ = add("vector", lambda e: e.tensor_scalar(comb_tm[:], msk1[:], w1_sc[:], None, ALU.mult),
                          deps=[sg2], force_deps=[d21])
            cmb2 = add("vector", lambda e: e.scalar_tensor_tensor(
                comb_tm[:], msk2[:], w2_sc[:], comb_tm[:], ALU.mult, ALU.add),
                force_deps=[p_])
            trp = add("tensor", (lambda: lambda e: e.transpose(
                ps[1][0:8, 0:128], comb_tm[:], id128[:]))(), deps=[cmb2, ld_c, psdep(1)])
            pv["trp_prev"] = trp
            evt = add("scalar", (lambda tg0: lambda e: e.copy(
                comb_fm[:, tg0:tg0 + 128], ps[1][0:8, 0:128]))(tg0), deps=[trp, pv.get("cfwar")])
            psset(1, evt)
            pv["lgwar"] = cmb2
            gts.append(evt)
        gate_bar = pr.frontier()
        for e_i in expert_order:
            rowd = add("gpsimd", [dma(comb_n[:], comb_fm[e_i:e_i + 1, c0:c0 + TT4])],
                       deps=gts + [pv.get("crepmm")] + (gate_bar if e_i == expert_order[0] else []), dma=True)
            crmm1 = add("tensor", (lambda: lambda e: e.matmul(
                ps[0][:, :], ones1_f[:], comb_n[0:1, 0:512], start=True, stop=True))(),
                deps=[rowd, ones_set2, psdep(0)])
            crmm2 = add("tensor", (lambda: lambda e: e.matmul(
                ps[1][:, :], ones1_f[:], comb_n[0:1, 512:1024], start=True, stop=True))(),
                deps=[rowd, ones_set2, psdep(1)])
            ce1 = add("scalar", lambda e: e.copy(crep[:, 0:512], ps[0][:, :]),
                      deps=[crmm1, pv.get("crepw")])
            ce2 = add("scalar", lambda e: e.copy(crep[:, 512:1024], ps[1][:, :]), deps=[crmm2])
            psset(0, ce1)
            psset(1, ce2)
            pv["crepmm"] = crmm2
            for ft in range(32):
                par = ft % 4
                parh = ft % 2
                lw1 = add("sync", [dma(w1st[par * 8 + k][:],
                                       w1_d[e_i, k * 128:(k + 1) * 128, ft * 128:(ft + 1) * 128])
                                   for k in range(8)],
                          deps=[pv.get(("w1c", par))] + scopeA_gate, dma=True)
                cw1 = add("vector", (lambda par: lambda e: [
                    e.tensor_copy(w1bf[par * 8 + k][:], w1st[par * 8 + k][:])
                    for k in range(8)][-1])(par), deps=[lw1])
                mmh = add("tensor", (lambda par, parh, c0: lambda e: [
                    e.matmul(psh[parh][:, i * 512:(i + 1) * 512], w1bf[par * 8 + k][:],
                             x3bf[k][:, c0 + i * 512:c0 + (i + 1) * 512],
                             start=(k == 0), stop=(k == 7))
                    for i in range(2) for k in range(8)][-1])(par, parh, c0),
                    deps=[cw1, pv.get(("pshw", parh))])
                pv[("w1c", par)] = mmh
                actf = AF.Gelu if e_i % 2 == 0 else AF.Silu
                ah = add("scalar", (lambda ft, e_i, actf, parh: lambda e: e.activation(
                    a_full[ft][:], psh[parh][:, :], actf,
                    bias=b1c[:, e_i * 32 + ft:e_i * 32 + ft + 1]))(ft, e_i, actf, parh),
                    deps=[mmh, pv.get(("afw", ft))])
                pv[("pshw", parh)] = ah
            w2r = w2_d.rearrange("e (g a p) d -> e g p a d", p=128, a=8)
            for dc in range(8):
                mmy = None
                for fg in range(4):
                    par = (dc * 4 + fg) % 2
                    lw2 = add("gpsimd", dma(
                        w2st[par][:].rearrange("p (a d) -> p a d", d=128),
                        w2r[e_i, fg][:, :, dc * 128:(dc + 1) * 128]),
                        deps=[pv.get(("w2c", par))] + scopeA_gate, dma=True)
                    cw2 = add("vector", (lambda par: lambda e: e.tensor_copy(
                        w2bf[par][:], w2st[par][:]))(par), deps=[lw2])
                    mmy = add("tensor", (lambda par, fg: lambda e: [
                        e.matmul(psy2[:, i * 512:(i + 1) * 512],
                                 w2bf[par][:, a * 128:(a + 1) * 128],
                                 a_full[fg * 8 + a][:, i * 512:(i + 1) * 512],
                                 start=(fg == 0 and a == 0), stop=(fg == 3 and a == 7))
                        for i in range(2) for a in range(8)][-1])(par, fg),
                        deps=[cw2] + ([pv.get("psy2w"), pv.get(("afw_all",))] if fg == 0 else []))
                    pv[("w2c", par)] = mmy
                wy = add("vector", (lambda: lambda e: e.tensor_tensor(
                    rstd[:, 0:TT4], psy2[:, :], crep[:, 0:TT4], ALU.mult))(),
                    deps=[mmy, ce1, ce2, pv.get("rstdw")])
                pv["psy2w"] = wy
                if e_i == expert_order[0]:
                    acc = add("vector", (lambda dc: lambda e: e.tensor_copy(
                        ymoe[dc][:], rstd[:, 0:TT4]))(dc), deps=[pv.get(("ymwar", dc))])
                else:
                    acc = add("vector", (lambda dc: lambda e: e.tensor_add(
                        ymoe[dc][:], ymoe[dc][:], rstd[:, 0:TT4]))(dc))
                pv["rstdw"] = acc
            pv[("afw_all",)] = mmy
            for ft in range(32):
                pv[("afw", ft)] = mmy
        exp_bar = pr.frontier()
        for dc in range(8):
            mb2 = add("tensor", (lambda dc, c0: lambda e: [
                e.matmul(ps[0][:, 0:512] if i == 0 else ps[1][:, 0:512],
                         b2c[:, dc * 128:(dc + 1) * 128],
                         comb_fm[:, c0 + i * 512:c0 + (i + 1) * 512],
                         start=True, stop=True) for i in range(2)][-1])(dc, c0),
                deps=[psdep(0), psdep(1), ld_c] + gts + (exp_bar if dc == 0 else []))
            ab1 = add("vector", (lambda dc: lambda e: e.tensor_tensor(
                ymoe[dc][:, 0:512], ymoe[dc][:, 0:512], ps[0][:, 0:512], ALU.add))(dc),
                deps=[mb2])
            ab2 = add("vector", (lambda dc: lambda e: e.tensor_tensor(
                ymoe[dc][:, 512:1024], ymoe[dc][:, 512:1024], ps[1][:, 0:512], ALU.add))(dc))
            psset(0, ab1)
            psset(1, ab2)
        if debug and tt == 0:
            for nm, tile, shape, dt_ in [
                ("x3bf", x3bf[0][:], [128, SPAN], BF16),
                ("comb", comb_fm[:], [N_EXP, SPAN], F32),
                ("lg", lg_sb[:], [128, N_EXP], F32),
                ("m1", m1[:], [128, 1], F32),
                ("m2", m2[:], [128, 1], F32),
                ("msk1", msk1[:], [128, N_EXP], F32),
                ("msk2", msk2[:], [128, N_EXP], F32),
                ("s_sc", s_sc[:], [128, 1], F32),
                ("w1s", w1_sc[:], [128, 1], F32),
                ("w2s", w2_sc[:], [128, 1], F32),
                ("ctm", comb_tm[:], [128, N_EXP], F32),
                ("af0", a_full[0][:], [128, TT4], BF16),
                ("ym0", ymoe[0][:], [128, TT4], F32),
                ("crep0", crep[:], [128, TT4], F32),
            ]:
                d_ = nc.dram_tensor("dbg_" + nm, shape, dt_, kind="ExternalOutput").ap()
                add("gpsimd", [dma(d_, tile)], deps=pr.frontier(), dma=True)
        gbar = add("gpsimd", lambda e: e.memset(gbar_t[:], 0.0), deps=pr.frontier())
        frontA = pr.frontier()
        for g in reversed(gA):
            g.__exit__(None, None, None)
        gB = [nc.sbuf_tensor(f"x3re{tt}_{k}", [128, TT4], F32) for k in range(8)]
        x3re = [g.__enter__() for g in gB]
        ldx3 = add("sync", [dma(x3re[k][:], x3s_d[k * 128:(k + 1) * 128, c0:c0 + TT4])
                            for k in range(8)],
                   deps=[pv.get(("x3st", tt, k)) for k in range(8)] + [pv.get("x3rew")]
                   + frontA, dma=True)
        if debug and tt == 0:
            d_ = nc.dram_tensor("dbg_x3re", [128, TT4], F32, kind="ExternalOutput").ap()
            dh = add("gpsimd", [dma(d_, x3re[0][:])], deps=[ldx3], dma=True)
        else:
            dh = None
        h4s = []
        for k in range(8):
            h4 = add("vector", (lambda k: lambda e: e.tensor_tensor(
                x3re[k][:], x3re[k][:], ymoe[k][:], ALU.add))(k),
                deps=[ldx3, pv.get("rstdw"), dh])
            pv[("ymwar", k)] = h4
            h4s.append(h4)
        for hw in range(2):
            apps_h = emit_ln(pr, x3re, hw * 512, 512, ps[0][:, :], ps[1][:, :], hbf2, sq2, rstd,
                             c128s, epsc, h4s + [psdep(0), psdep(1)])
        apps = [pv["ln_rstd"]] * 8
        pv["rstdw"] = None
        for k in range(8):
            st = add("gpsimd", [dma(yout_d[k * 128:(k + 1) * 128, c0:c0 + TT4], x3re[k][:])],
                     deps=[apps[k]], dma=True)
            pv["x3rew"] = st
        for g in reversed(gB):
            g.__exit__(None, None, None)


def build_fused():
    nc = bass.Bass("TRN2", target_bir_lowering=False, detect_race_conditions=False)
    SP = SPAN + 1
    CH = DCH * SP

    E1 = {
        "x": nc.dram_tensor("x", [D, T], F32, kind="ExternalInput").ap(),
        "wx": nc.dram_tensor("wx", [D, DCH], F32, kind="ExternalInput").ap(),
        "wz": nc.dram_tensor("wz", [D, DCH], F32, kind="ExternalInput").ap(),
        "wp": nc.dram_tensor("wp", [D, 96], F32, kind="ExternalInput").ap(),
        "wdt": nc.dram_tensor("wdt", [R_SSM, DCH], F32, kind="ExternalInput").ap(),
        "bdt": nc.dram_tensor("bdt", [2, 128, 1], F32, kind="ExternalInput").ap(),
        "alog": nc.dram_tensor("alog", [2, 128, N_SSM], F32, kind="ExternalInput").ap(),
        "cw": nc.dram_tensor("cw", [2, 128, K_CONV], F32, kind="ExternalInput").ap(),
        "dp": nc.dram_tensor("dp", [2, 128, 1], F32, kind="ExternalInput").ap(),
    }
    y1pad = nc.dram_tensor("y1pad", [DCH, T + 1], BF16, kind="Internal").ap()
    E1["y1"] = y1pad[:, 1:]
    y1_all = nc.dram_tensor("y1_all", [D, T + 1], BF16, kind="Internal").ap()
    y1loc = nc.dram_tensor("y1loc", [D, SP], BF16, kind="Internal").ap()
    kvc = nc.dram_tensor("kvc", [3 * D, SPAN], BF16, kind="Internal").ap()
    kv_all = nc.dram_tensor("kv_all", [4 * 3 * D, SPAN], BF16, kind="Internal").ap()
    x2_i = nc.dram_tensor("x2i", [D, SPAN], F32, kind="Internal").ap()
    qb_i = nc.dram_tensor("qbi", [D, SPAN], BF16, kind="Internal").ap()
    qrb_i = nc.dram_tensor("qrbi", [D, SPAN], BF16, kind="Internal").ap()
    ya_i = nc.dram_tensor("yai", [D, SPAN], BF16, kind="Internal").ap()

    E2 = {
        "xh": nc.dram_tensor("xh", [D, SP], F32, kind="ExternalInput").ap(),
        "y1h": y1loc,
        "woT": nc.dram_tensor("woT", [D, D], F32, kind="ExternalInput").ap(),
        "wrT": nc.dram_tensor("wrT", [D, D], F32, kind="ExternalInput").ap(),
        "wkT": nc.dram_tensor("wkT", [D, D], F32, kind="ExternalInput").ap(),
        "wvT": nc.dram_tensor("wvT", [D, D], F32, kind="ExternalInput").ap(),
        "winT": nc.dram_tensor("winT", [D, 3 * D], F32, kind="ExternalInput").ap(),
        "mur": nc.dram_tensor("mur", [8, 128, 1], F32, kind="ExternalInput").ap(),
        "muk": nc.dram_tensor("muk", [8, 128, 1], F32, kind="ExternalInput").ap(),
        "sinT": nc.dram_tensor("sinT", [128, SPAN], BF16, kind="ExternalInput").ap(),
        "cosT": nc.dram_tensor("cosT", [128, SPAN], BF16, kind="ExternalInput").ap(),
        "x2": x2_i,
        "qb": qb_i,
        "qrb": qrb_i,
        "kb": kvc[0:D, :],
        "krb": kvc[D:2 * D, :],
        "vb": kvc[2 * D:3 * D, :],
        "x2s": nc.dram_tensor("x2s", [D, SPAN], BF16, kind="Internal").ap(),
    }
    E3 = {
        "qb": qb_i,
        "qrb": qrb_i,
        "ya": ya_i,
        "k_parts": lambda r, h: kv_all[r * 3 * D + h * 64:r * 3 * D + (h + 1) * 64, :],
        "kr_parts": lambda r, h: kv_all[r * 3 * D + D + h * 64:r * 3 * D + D + (h + 1) * 64, :],
        "v_slice": lambda h, j: kv_all[(j // 16) * 3 * D + 2 * D + h * 64:
                                       (j // 16) * 3 * D + 2 * D + (h + 1) * 64,
                                       (j % 16) * 128:(j % 16 + 1) * 128],
    }
    E4 = _p4_drams(nc, skip_io=True)
    E4["ya"] = ya_i
    E4["x2"] = x2_i
    ysel_d = nc.dram_tensor("ysel", [128, 4], F32, kind="ExternalInput").ap()

    rg = [[0, 1, 2, 3], [4, 5, 6, 7]]
    pr = Prog(nc)

    sc1 = Scope(nc, "p1_")
    emit_phase1(nc, pr, E1, sc1)
    sc1.close()

    pr.base_deps = pr.frontier()
    with nc.sbuf_tensor("zcol", [128, 1], BF16) as zcol:
        zh = pr.add("vector", lambda e: e.memset(zcol[:], 0.0))
        zd = pr.add("gpsimd", [dma(y1pad[0:128, 0:1], zcol[:], slow=True),
                               dma(y1pad[128:256, 0:1], zcol[:], slow=True)],
                    deps=[zh], dma=True)
        c1 = pr.add("gpsimd", lambda e: e.collective_compute(
            "AllGather", ALU.bypass, rg, [y1pad], [y1_all]), deps=[zd], dma=True)

    with nc.sbuf_tensor("cand0", [128, SP], BF16) as cnd0, \
         nc.sbuf_tensor("cand1", [128, SP], BF16) as cnd1, \
         nc.sbuf_tensor("cand2", [128, SP], BF16) as cnd2, \
         nc.sbuf_tensor("cand3", [128, SP], BF16) as cnd3, \
         nc.sbuf_tensor("yselst", [128, 4], F32) as yselst, \
         nc.sbuf_tensor("yspan", [128, SP], BF16) as yspan:
        cand = [cnd0, cnd1, cnd2, cnd3]
        sl = pr.add("sync", dma(yselst[:], ysel_d), dma=True)
        prev_blend = None
        prev_store = None
        for k in range(8):
            ldh = pr.add("sync", [
                dma(cand[qq][:], y1_all[k * 128:(k + 1) * 128,
                                        qq * SPAN:qq * SPAN + SP]) for qq in range(4)],
                deps=[c1, sl, prev_blend], dma=True)
            b0 = pr.add("vector", (lambda k: lambda e: e.tensor_scalar(
                yspan[:], cnd0[:], yselst[:, 0:1], None, ALU.mult))(k),
                deps=[ldh, prev_store])
            for qq in range(1, 4):
                b0 = pr.add("vector", (lambda qq: lambda e: e.scalar_tensor_tensor(
                    yspan[:], cand[qq][:], yselst[:, qq:qq + 1], yspan[:], ALU.mult, ALU.add))(qq))
            prev_blend = b0
            prev_store = pr.add("gpsimd", dma(y1loc[k * 128:(k + 1) * 128, :], yspan[:]),
                                deps=[b0], dma=True)

    pr.base_deps = pr.frontier()
    sc2 = Scope(nc, "p2_")
    emit_phase2(nc, pr, E2, sc2)
    sc2.close()

    pr.base_deps = pr.frontier()
    c2 = pr.add("gpsimd", lambda e: e.collective_compute(
        "AllGather", ALU.bypass, rg, [kvc], [kv_all]), dma=True)

    pr.base_deps = pr.frontier()
    sc3 = Scope(nc, "p3_")
    emit_phase3(nc, pr, E3, sc3, vt_transposed=False)
    sc3.close()

    pr.base_deps = pr.frontier()
    sc4 = Scope(nc, "p4_")
    emit_phase4(nc, pr, E4, sc4)
    sc4.close()
    pr.base_deps = []

    pr.emit()
    return nc


def _fused_inputs(inputs, core):
    m = _p1_inputs(inputs, core)
    b, q = core // 4, core % 4
    sel = np.zeros((128, 4), dtype=np.float32)
    sel[:, q] = 1.0
    m["ysel"] = sel
    t0 = q * SPAN
    xT = np.ascontiguousarray(np.asarray(inputs["L0_attn_x"])[b].T)
    xh = np.zeros((D, SPAN + 1), dtype=np.float32)
    xh[:, 1:] = xT[:, t0:t0 + SPAN]
    if q > 0:
        xh[:, 0] = xT[:, t0 - 1]
    th = np.repeat(1000.0 ** (-2.0 * np.arange(1, 33, dtype=np.float64) / 64.0), 2)
    pos = np.arange(t0 + 1, t0 + SPAN + 1, dtype=np.float64)
    args = th[:, None] * pos[None, :]
    m.update({
        "xh": xh,
        "woT": np.ascontiguousarray(np.asarray(inputs["L0_attn_W_out"]).T),
        "wrT": np.ascontiguousarray(np.asarray(inputs["L0_ffn_W_r"]).T),
        "wkT": np.ascontiguousarray(np.asarray(inputs["L0_ffn_W_k"]).T),
        "wvT": np.ascontiguousarray(np.asarray(inputs["L0_ffn_W_v"]).T),
        "winT": np.ascontiguousarray(np.asarray(inputs["L1_attn_w_in"]).T),
        "mur": np.ascontiguousarray(np.asarray(inputs["L0_ffn_mu_r"]).reshape(8, 128, 1)),
        "muk": np.ascontiguousarray(np.asarray(inputs["L0_ffn_mu_k"]).reshape(8, 128, 1)),
        "sinT": np.ascontiguousarray(np.tile(np.sin(args).astype(np.float32), (2, 1)).astype(NBF)),
        "cosT": np.ascontiguousarray(np.tile(np.cos(args).astype(np.float32), (2, 1)).astype(NBF)),
    })
    b1 = np.asarray(inputs["L1_ffn_b1"])
    b1h = np.zeros((128, N_EXP * 32), dtype=np.float32)
    for e in range(N_EXP):
        for ft in range(32):
            b1h[:, e * 32 + ft] = b1[e, ft * 128:(ft + 1) * 128]
    m.update({
        "woT4": np.ascontiguousarray(np.asarray(inputs["L1_attn_w_out"]).T),
        "lng": np.ascontiguousarray(np.asarray(inputs["L1_attn_ln_gamma"]).reshape(8, 128, 1)),
        "lnb": np.ascontiguousarray(np.asarray(inputs["L1_attn_ln_beta"]).reshape(8, 128, 1)),
        "wg": np.ascontiguousarray(np.asarray(inputs["L1_ffn_Wg"])),
        "bg": np.ascontiguousarray(np.asarray(inputs["L1_ffn_bg"]).reshape(1, N_EXP)),
        "w1": np.ascontiguousarray(np.asarray(inputs["L1_ffn_W1"])),
        "b1h": b1h,
        "w2": np.ascontiguousarray(np.asarray(inputs["L1_ffn_W2"])),
        "b2mm": np.ascontiguousarray(np.asarray(inputs["L1_ffn_b2"])),
    })
    return m


def kernel_fused(**inputs):
    inputs = {k: np.asarray(v) for k, v in inputs.items()}
    nc = build_fused()
    in_maps = [_fused_inputs(inputs, c) for c in range(NCORES)]
    res = run_bass_kernel_spmd(nc, in_maps, core_ids=list(range(NCORES)))
    out = np.zeros((B, T, D), dtype=np.float32)
    for c in range(NCORES):
        b, q = c // 4, c % 4
        out[b, q * SPAN:(q + 1) * SPAN, :] = res.results[c]["yout"].T
    return out


def _p1_inputs(inputs, core):
    b, q = core // 4, core % 4
    x = np.asarray(inputs["L0_attn_x"])
    sl = slice(q * DCH, (q + 1) * DCH)
    return {
        "x": np.ascontiguousarray(x[b].T),
        "wx": np.ascontiguousarray(np.asarray(inputs["L0_attn_W_x"])[sl].T),
        "wz": np.ascontiguousarray(np.asarray(inputs["L0_attn_W_z"])[sl].T),
        "wp": np.ascontiguousarray(np.asarray(inputs["L0_attn_W_params"]).T),
        "wdt": np.ascontiguousarray(np.asarray(inputs["L0_attn_W_dt"])[sl].T),
        "bdt": np.ascontiguousarray(np.asarray(inputs["L0_attn_b_dt"])[sl].reshape(2, 128, 1)),
        "alog": np.ascontiguousarray(np.asarray(inputs["L0_attn_A_log"])[sl].reshape(2, 128, N_SSM)),
        "cw": np.ascontiguousarray(np.asarray(inputs["L0_attn_conv_w"])[sl, 0, :].reshape(2, 128, K_CONV)),
        "dp": np.ascontiguousarray(np.asarray(inputs["L0_attn_D_param"])[sl].reshape(2, 128, 1)),
    }


def run_phase1(inputs):
    nc = build_phase1()
    in_maps = [_p1_inputs(inputs, c) for c in range(NCORES)]
    res = run_bass_kernel_spmd(nc, in_maps, core_ids=list(range(NCORES)))
    y1 = np.zeros((B, D, T), dtype=NBF)
    for c in range(NCORES):
        b, q = c // 4, c % 4
        y1[b, q * DCH:(q + 1) * DCH, :] = res.results[c]["y1"]
    return y1




def _p2_inputs(inputs, y1, core):
    b, q = core // 4, core % 4
    t0 = q * SPAN
    xT = np.ascontiguousarray(np.asarray(inputs["L0_attn_x"])[b].T)
    xh = np.zeros((D, SPAN + 1), dtype=np.float32)
    xh[:, 1:] = xT[:, t0:t0 + SPAN]
    y1h = np.zeros((D, SPAN + 1), dtype=NBF)
    y1h[:, 1:] = y1[b][:, t0:t0 + SPAN]
    if q > 0:
        xh[:, 0] = xT[:, t0 - 1]
        y1h[:, 0] = y1[b][:, t0 - 1]
    th = np.repeat(1000.0 ** (-2.0 * np.arange(1, 33, dtype=np.float64) / 64.0), 2)
    pos = np.arange(t0 + 1, t0 + SPAN + 1, dtype=np.float64)
    args = th[:, None] * pos[None, :]
    sinT = np.tile(np.sin(args).astype(np.float32), (2, 1)).astype(NBF)
    cosT = np.tile(np.cos(args).astype(np.float32), (2, 1)).astype(NBF)
    return {
        "xh": xh, "y1h": y1h,
        "woT": np.ascontiguousarray(np.asarray(inputs["L0_attn_W_out"]).T),
        "wrT": np.ascontiguousarray(np.asarray(inputs["L0_ffn_W_r"]).T),
        "wkT": np.ascontiguousarray(np.asarray(inputs["L0_ffn_W_k"]).T),
        "wvT": np.ascontiguousarray(np.asarray(inputs["L0_ffn_W_v"]).T),
        "winT": np.ascontiguousarray(np.asarray(inputs["L1_attn_w_in"]).T),
        "mur": np.ascontiguousarray(np.asarray(inputs["L0_ffn_mu_r"]).reshape(8, 128, 1)),
        "muk": np.ascontiguousarray(np.asarray(inputs["L0_ffn_mu_k"]).reshape(8, 128, 1)),
        "sinT": np.ascontiguousarray(sinT), "cosT": np.ascontiguousarray(cosT),
    }


def run_phase2(inputs, y1):
    nc = build_phase2()
    in_maps = [_p2_inputs(inputs, y1, c) for c in range(NCORES)]
    res = run_bass_kernel_spmd(nc, in_maps, core_ids=list(range(NCORES)))
    return [res.results[c] for c in range(NCORES)]


def run_phase3(p2):
    nc = build_phase3()
    in_maps = []
    for c in range(NCORES):
        b = c // 4
        kA = np.concatenate([p2[b * 4 + q]["kb"] for q in range(4)], axis=1)
        krA = np.concatenate([p2[b * 4 + q]["krb"] for q in range(4)], axis=1)
        vT = np.concatenate([np.ascontiguousarray(p2[b * 4 + q]["vb"].T) for q in range(4)], axis=0)
        in_maps.append({"qb": p2[c]["qb"], "qrb": p2[c]["qrb"],
                        "kA": np.ascontiguousarray(kA), "krA": np.ascontiguousarray(krA),
                        "vT": np.ascontiguousarray(vT)})
    res = run_bass_kernel_spmd(nc, in_maps, core_ids=list(range(NCORES)))
    return [res.results[c]["ya"] for c in range(NCORES)]


def _p4_inputs(inputs, p2, p3, core):
    b1 = np.asarray(inputs["L1_ffn_b1"])
    b1h = np.zeros((128, N_EXP * 32), dtype=np.float32)
    for e in range(N_EXP):
        for ft in range(32):
            b1h[:, e * 32 + ft] = b1[e, ft * 128:(ft + 1) * 128]
    return {
        "ya": (p3[core] if p3[core] is not None else np.zeros((D, SPAN), NBF)),
        "x2": p2[core]["x2"],
        "woT4": np.ascontiguousarray(np.asarray(inputs["L1_attn_w_out"]).T),
        "lng": np.ascontiguousarray(np.asarray(inputs["L1_attn_ln_gamma"]).reshape(8, 128, 1)),
        "lnb": np.ascontiguousarray(np.asarray(inputs["L1_attn_ln_beta"]).reshape(8, 128, 1)),
        "wg": np.ascontiguousarray(np.asarray(inputs["L1_ffn_Wg"])),
        "bg": np.ascontiguousarray(np.asarray(inputs["L1_ffn_bg"]).reshape(1, N_EXP)),
        "w1": np.ascontiguousarray(np.asarray(inputs["L1_ffn_W1"])),
        "b1h": b1h,
        "w2": np.ascontiguousarray(np.asarray(inputs["L1_ffn_W2"])),
        "b2mm": np.ascontiguousarray(np.asarray(inputs["L1_ffn_b2"])),
    }


def run_phase4(inputs, p2, p3):
    nc = build_phase4()
    in_maps = [_p4_inputs(inputs, p2, p3, c) for c in range(NCORES)]
    res = run_bass_kernel_spmd(nc, in_maps, core_ids=list(range(NCORES)))
    out = np.zeros((B, T, D), dtype=np.float32)
    for c in range(NCORES):
        b, q = c // 4, c % 4
        out[b, q * SPAN:(q + 1) * SPAN, :] = res.results[c]["yout"].T
    return out




def build_phase34():
    nc = bass.Bass("TRN2", target_bir_lowering=False, detect_race_conditions=False)
    E3 = {
        "qb": nc.dram_tensor("qb", [D, SPAN], BF16, kind="ExternalInput").ap(),
        "qrb": nc.dram_tensor("qrb", [D, SPAN], BF16, kind="ExternalInput").ap(),
        "kA": nc.dram_tensor("kA", [D, T], BF16, kind="ExternalInput").ap(),
        "krA": nc.dram_tensor("krA", [D, T], BF16, kind="ExternalInput").ap(),
        "vT": nc.dram_tensor("vT", [T, D], BF16, kind="ExternalInput").ap(),
        "ya": nc.dram_tensor("ya_i", [D, SPAN], BF16, kind="Internal").ap(),
    }
    E4 = _p4_drams(nc, skip_io=True)
    E4["ya"] = E3["ya"]
    E4["x2"] = nc.dram_tensor("x2", [D, SPAN], F32, kind="ExternalInput").ap()
    pr = Prog(nc)
    sc3 = Scope(nc, "p3_")
    emit_phase3(nc, pr, E3, sc3, vt_transposed=True)
    sc3.close()
    pr.base_deps = pr.frontier()
    sc4 = Scope(nc, "p4_")
    emit_phase4(nc, pr, E4, sc4)
    sc4.close()
    pr.base_deps = []
    pr.emit()
    return nc


def run_phase34(inputs, p2):
    nc = build_phase34()
    in_maps = []
    for c in range(NCORES):
        b = c // 4
        kA = np.concatenate([p2[b * 4 + q]["kb"] for q in range(4)], axis=1)
        krA = np.concatenate([p2[b * 4 + q]["krb"] for q in range(4)], axis=1)
        vT = np.concatenate([np.ascontiguousarray(p2[b * 4 + q]["vb"].T) for q in range(4)], axis=0)
        m = {"qb": p2[c]["qb"], "qrb": p2[c]["qrb"],
             "kA": np.ascontiguousarray(kA), "krA": np.ascontiguousarray(krA),
             "vT": np.ascontiguousarray(vT), "x2": p2[c]["x2"]}
        m.update({k: v for k, v in _p4_inputs(inputs, p2, [None] * 8, c).items()
                  if k not in ("ya", "x2")})
        in_maps.append(m)
    res = run_bass_kernel_spmd(nc, in_maps, core_ids=list(range(NCORES)))
    out = np.zeros((B, T, D), dtype=np.float32)
    for c in range(NCORES):
        b, q = c // 4, c % 4
        out[b, q * SPAN:(q + 1) * SPAN, :] = res.results[c]["yout"].T
    return out


def kernel(**inputs):
    inputs = {k: np.asarray(v) for k, v in inputs.items()}
    y1 = run_phase1(inputs)
    p2 = run_phase2(inputs, y1)
    return run_phase34(inputs, p2)



# version 09
# speedup vs baseline: 1.2156x; 1.2156x over previous
import numpy as np
import ml_dtypes
import concourse.bass as bass
import concourse.mybir as mybir
from concourse.bass_utils import run_bass_kernel_spmd

F32 = mybir.dt.float32
BF16 = mybir.dt.bfloat16
AF = mybir.ActivationFunctionType
ALU = mybir.AluOpType
AX = mybir.AxisListType
NBF = ml_dtypes.bfloat16

B, T, D = 2, 8192, 1024
NCORES = 8
SPAN = T // 4
DCH = D // 4
N_SSM = 16
R_SSM = 64
K_CONV = 4
D_FF = 4096
N_EXP = 8
H = 16
DH = 64


class Prog:

    ENGINES = ("sync", "tensor", "scalar", "vector", "gpsimd")

    def __init__(self, nc):
        self.nc = nc
        self.pv = {}
        self.base_deps = []
        self.ops = {e: [] for e in self.ENGINES}
        self.counts = {}
        self.dpar = {e: 0 for e in self.ENGINES}
        for e in self.ENGINES:
            for kind in ("i", "d0", "d1"):
                self.counts[(e, kind)] = 0

    def add(self, engine, fn, deps=(), dma=False, force_deps=()):
        deps = list(deps) + self.base_deps
        if dma:
            kind = f"d{self.dpar[engine]}"
            self.dpar[engine] ^= 1
            n = len(fn) if isinstance(fn, (list, tuple)) else 1
            self.counts[(engine, kind)] += 16 * n
        else:
            kind = "i"
            self.counts[(engine, kind)] += 1
        handle = (engine, kind, self.counts[(engine, kind)])
        self.ops[engine].append((fn, tuple(d for d in deps if d is not None),
                                 dma, handle, tuple(d for d in force_deps if d is not None)))
        return handle

    def frontier(self):
        out = []
        for e in self.ENGINES:
            for kind in ("i", "d0", "d1"):
                if self.counts[(e, kind)] > 0:
                    out.append((e, kind, self.counts[(e, kind)]))
        return out

    def emit(self):
        nc = self.nc
        sems = {}
        for e in self.ENGINES:
            for kind in ("i", "d0", "d1"):
                if self.counts[(e, kind)] > 0:
                    sems[(e, kind)] = nc.alloc_semaphore(f"s_{e}_{kind}")
        with nc.Block() as block:
            for e in self.ENGINES:
                if not self.ops[e]:
                    continue

                def make(e):
                    def run(eng):
                        watermark = {}
                        mysem = sems.get((e, "i"))
                        for fn, deps, dma, handle, fdeps in self.ops[e]:
                            need = {}
                            for (de, dk, dv) in deps:
                                if de == e and dk == "i":
                                    continue
                                key = (de, dk)
                                need[key] = max(need.get(key, 0), dv)
                            for (de, dk, dv) in fdeps:
                                key = (de, dk)
                                need[key] = max(need.get(key, 0), dv)
                            for key, val in sorted(need.items()):
                                if watermark.get(key, 0) >= val:
                                    continue
                                eng.wait_ge(sems[key], val)
                                watermark[key] = val
                            if dma:
                                thunks = fn if isinstance(fn, (list, tuple)) else [fn]
                                kind = handle[1]
                                dsem = sems[(e, kind)]
                                before = handle[2] - 16 * len(thunks)
                                if before > 0 and watermark.get((e, kind), 0) < before:
                                    eng.wait_ge(dsem, before)
                                    watermark[(e, kind)] = before
                                for th in thunks:
                                    th(eng).then_inc(dsem, 16)
                            else:
                                fn(eng).then_inc(mysem, 1)
                        for kind in ("d0", "d1"):
                            if sems.get((e, kind)) is not None and self.counts[(e, kind)] > 0:
                                eng.wait_ge(sems[(e, kind)], self.counts[(e, kind)])
                    return run

                getattr(block, e)(make(e))


class Scope:

    def __init__(self, nc, prefix):
        self.nc = nc
        self.prefix = prefix
        self.guards = []

    def A(self, name, shape, dtype):
        g = self.nc.sbuf_tensor(self.prefix + name, shape, dtype)
        h = g.__enter__()
        self.guards.append(g)
        return h

    def P(self, name, shape, dtype):
        g = self.nc.psum_tensor(self.prefix + name, shape, dtype)
        h = g.__enter__()
        self.guards.append(g)
        return h

    def close(self):
        for g in reversed(self.guards):
            g.__exit__(None, None, None)
        self.guards = []


def dma(dst, src, slow=False):
    if slow:
        return lambda e: e.dma_start(out=dst, in_=src, allow_slow_non_contiguous=True)
    return lambda e: e.dma_start(out=dst, in_=src)


def _p1_drams(nc, kind_out="ExternalOutput"):
    return {
        "x": nc.dram_tensor("x", [D, T], F32, kind="ExternalInput").ap(),
        "wx": nc.dram_tensor("wx", [D, DCH], F32, kind="ExternalInput").ap(),
        "wz": nc.dram_tensor("wz", [D, DCH], F32, kind="ExternalInput").ap(),
        "wp": nc.dram_tensor("wp", [D, 96], F32, kind="ExternalInput").ap(),
        "wdt": nc.dram_tensor("wdt", [R_SSM, DCH], F32, kind="ExternalInput").ap(),
        "bdt": nc.dram_tensor("bdt", [2, 128, 1], F32, kind="ExternalInput").ap(),
        "alog": nc.dram_tensor("alog", [2, 128, N_SSM], F32, kind="ExternalInput").ap(),
        "cw": nc.dram_tensor("cw", [2, 128, K_CONV], F32, kind="ExternalInput").ap(),
        "dp": nc.dram_tensor("dp", [2, 128, 1], F32, kind="ExternalInput").ap(),
        "y1": nc.dram_tensor("y1", [DCH, T], BF16, kind=kind_out).ap(),
    }


def build_phase1(nchk=None, debug=False):
    nc = bass.Bass("TRN2", target_bir_lowering=False, detect_race_conditions=False)
    E = _p1_drams(nc)
    pr = Prog(nc)
    sc = Scope(nc, "p1_")
    emit_phase1(nc, pr, E, sc, nchk=nchk, debug=debug)
    sc.close()
    pr.emit()
    return nc


def emit_phase1(nc, pr, E, sc, nchk=None, debug=False):
    TC = 2048
    NCHK = T // TC if nchk is None else nchk
    HTC = TC // 2
    x_d = E["x"]
    wx_d = E["wx"]
    wz_d = E["wz"]
    wp_d = E["wp"]
    wdt_d = E["wdt"]
    bdt_d = E["bdt"]
    alog_d = E["alog"]
    cw_d = E["cw"]
    dp_d = E["dp"]
    y1_d = E["y1"]

    A = sc.A
    P = sc.P

    wstage = [A(f"wstage{i}", [128, DCH], F32) for i in range(2)]
    wxb = [A(f"wxb{k}", [128, DCH], BF16) for k in range(8)]
    wzb = [A(f"wzb{k}", [128, DCH], BF16) for k in range(8)]
    wpb = [A(f"wpb{k}", [128, 96], BF16) for k in range(8)]
    wdtb = A("wdtb", [R_SSM, DCH], BF16)
    bdt = [A(f"bdt{m}", [128, 1], F32) for m in range(2)]
    alog = [A(f"alog{m}", [128, N_SSM], F32) for m in range(2)]
    cw = [A(f"cw{m}", [128, K_CONV], F32) for m in range(2)]
    dpar = [A(f"dpar{m}", [128, 1], F32) for m in range(2)]
    Acol = [A(f"Acol{m}", [128, N_SSM], F32) for m in range(2)]
    rcol = [A(f"rcol{m}", [128, N_SSM], F32) for m in range(2)]
    nrcol = [A(f"nrcol{m}", [128, N_SSM], F32) for m in range(2)]
    mcol = [A(f"mcol{m}", [128, N_SSM], F32) for m in range(2)]
    t_ea = [A(f"t_ea{m}", [128, N_SSM], F32) for m in range(2)]
    t_s1 = [A(f"t_s1{m}", [128, N_SSM], F32) for m in range(2)]
    t_s2 = [A(f"t_s2{m}", [128, N_SSM], F32) for m in range(2)]
    t_s3 = [A(f"t_s3{m}", [128, N_SSM], F32) for m in range(2)]
    ones_f = A("ones_f", [1, 128], F32)

    xstage = [A(f"xstage{i}", [128, 512], F32) for i in range(2)]
    xbf = [A(f"xbf{k}", [128, 512], BF16) for k in range(8)]
    xres = [A(f"xres{m}", [128, K_CONV - 1 + TC], F32) for m in range(2)]
    halo = [A(f"halo{m}", [128, K_CONV - 1], F32) for m in range(2)]
    sz_sb = [A(f"sz{m}", [128, TC], F32) for m in range(2)]
    cva = A("cva", [128, TC], F32)
    cvb = A("cvb", [128, TC], F32)
    u_sb = [A(f"u{m}", [128, TC], BF16) for m in range(2)]
    dt_sb = [A(f"dt{m}", [128, TC], BF16) for m in range(2)]
    par_sb = A("par_sb", [96, TC], F32)
    dtu_bf = A("dtu_bf", [R_SSM, TC], BF16)
    sp_r = A("sp_r", [128, 512], F32)
    sp_t = A("sp_t", [128, 512], F32)
    bp_n = A("bp_n", [1, TC], F32)
    cp_n = A("cp_n", [1, TC], F32)
    bp_rep = A("bp_rep", [128, TC], BF16)
    cp_rep = A("cp_rep", [128, TC], BF16)
    at_sb = [A(f"at_sb{i}", [128, TC], BF16) for i in range(2)]
    s1_sb = A("s1_sb", [128, TC], BF16)
    bt_sb = A("bt_sb", [128, TC], BF16)
    w_sb = A("w_sb", [128, TC], BF16)
    y1t_f = A("y1t_f", [128, TC], F32)
    yacc = [A(f"yacc{m}", [128, TC], F32) for m in range(2)]
    state = [A(f"state{m}", [128, N_SSM], F32) for m in range(2)]
    y1bf = A("y1bf", [128, TC], BF16)

    ps = [P(f"ps{i}", [128, 512], F32) for i in range(4)]
    pbp = P("pbp", [128, HTC], F32)
    pcp = P("pcp", [128, HTC], F32)

    pr.pv = {}
    add = pr.add

    wx_r = wx_d.rearrange("(a p) c -> a p c", p=128)
    wz_r = wz_d.rearrange("(a p) c -> a p c", p=128)
    wp_r = wp_d.rearrange("(a p) c -> a p c", p=128)
    srcs = [(wx_r[k], wxb[k], DCH) for k in range(8)] + \
           [(wz_r[k], wzb[k], DCH) for k in range(8)] + \
           [(wp_r[k], wpb[k], 96) for k in range(8)]
    prev_cast = {}
    cast_h = None
    for i, (src, dstb, w) in enumerate(srcs):
        sbuf = wstage[i % 2]
        ldh = add("sync", dma(sbuf[:, 0:w], src), deps=[prev_cast.get(i % 2)], dma=True)
        cast_h = add("vector", (lambda dstb, sbuf, w: lambda e: e.tensor_copy(
            dstb[:, 0:w], sbuf[:, 0:w]))(dstb, sbuf, w), deps=[ldh])
        prev_cast[i % 2] = cast_h
    ld_dt = add("sync", dma(wstage[0][0:R_SSM, 0:DCH], wdt_d), deps=[prev_cast[0]], dma=True)
    cast_w_done = add("vector", lambda e: e.tensor_copy(wdtb[:], wstage[0][0:R_SSM, 0:DCH]),
                      deps=[ld_dt])
    ld_s = add("sync", [dma(bdt[m][:], bdt_d[m]) for m in range(2)]
               + [dma(alog[m][:], alog_d[m]) for m in range(2)]
               + [dma(cw[m][:], cw_d[m]) for m in range(2)]
               + [dma(dpar[m][:], dp_d[m]) for m in range(2)], dma=True)
    ones_set = add("vector", lambda e: e.memset(ones_f[:], 1.0))

    setup_done = []
    for m in range(2):
        ea = add("scalar", (lambda m: lambda e: e.activation(t_ea[m][:], alog[m][:], AF.Exp))(m),
                 deps=[ld_s])
        p_ = add("vector", (lambda m: lambda e: e.tensor_scalar(Acol[m][:], t_ea[m][:], -1.0, None, ALU.mult))(m),
                 deps=[ea])
        p_ = add("vector", (lambda m: lambda e: e.tensor_scalar(mcol[m][:], t_ea[m][:], 1e-8, None, ALU.is_lt))(m),
                 force_deps=[p_])
        h1 = p_ = add("vector", (lambda m: lambda e: e.tensor_scalar(t_s1[m][:], Acol[m][:], 1e-10, None, ALU.add))(m),
                      force_deps=[p_])
        h2 = add("vector", (lambda m: lambda e: e.reciprocal(t_s2[m][:], t_s1[m][:]))(m), force_deps=[h1])
        h3 = add("vector", (lambda m: lambda e: e.tensor_scalar(t_s3[m][:], mcol[m][:], -1.0, 1.0, ALU.mult, ALU.add))(m),
                 force_deps=[h2])
        p_ = add("vector", (lambda m: lambda e: e.tensor_tensor(rcol[m][:], t_s2[m][:], t_s3[m][:], ALU.mult))(m),
                 force_deps=[h3])
        p_ = add("vector", (lambda m: lambda e: e.tensor_scalar(nrcol[m][:], rcol[m][:], -1.0, None, ALU.mult))(m),
                 force_deps=[p_])
        add("vector", (lambda m: lambda e: e.memset(state[m][:], 0.0))(m))
        setup_done.append(add("vector", (lambda m: lambda e: e.memset(halo[m][:], 0.0))(m)))

    pv = {}
    prev_last_acc = None
    prev_hs = [None, None]
    prev_rowdma = None
    prev_st = None
    prev_e1 = [None, None]
    prev_e2 = None

    for c in range(NCHK):
        c0 = c * TC
        par_ev = []
        dt_done = [None, None]
        xres_ev = [[], []]
        sz_ev = [[], []]
        last_xreader = pv.get("last_xreader")
        for tt in range(4):
            t0 = tt * 512
            casts = []
            for k in range(8):
                sbuf = xstage[k % 2]
                ldh = add("sync", dma(sbuf[:], x_d[k * 128:(k + 1) * 128, c0 + t0:c0 + t0 + 512]),
                          deps=[pv.get(("xstage_cast", k % 2))], dma=True)
                ch = add("vector", (lambda k, sbuf: lambda e: e.tensor_copy(xbf[k][:], sbuf[:]))(k, sbuf),
                         deps=[ldh, last_xreader])
                pv[("xstage_cast", k % 2)] = ch
                casts.append(ch)
            mm_deps = [cast_w_done, casts[-1]]

            def mk_acc8(psdst, rows, wb, mslice):
                def f(e):
                    inst = None
                    for k in range(8):
                        inst = e.matmul(psdst[0:rows, :], wb[k][:, mslice],
                                        xbf[k][:], start=(k == 0), stop=(k == 7))
                    return inst
                return f

            par_mm = add("tensor", mk_acc8(ps[0], 96, wpb, slice(0, 96)),
                         deps=mm_deps + [pv.get("ps0"), dt_done[1]])
            ev = add("scalar", (lambda t0: lambda e: e.copy(par_sb[:, t0:t0 + 512], ps[0][0:96, :]))(t0),
                     deps=[par_mm, prev_rowdma])
            par_ev.append(ev)
            dtu_c = add("vector", (lambda t0: lambda e: e.tensor_copy(
                dtu_bf[:, t0:t0 + 512], par_sb[0:64, t0:t0 + 512]))(t0),
                deps=[ev, pv.get("last_dtmm")])
            for m in range(2):
                msl = slice(m * 128, (m + 1) * 128)
                xr_mm = add("tensor", mk_acc8(ps[1 + m], 128, wxb, msl),
                            deps=mm_deps + ([xres_ev[m][-1]] if xres_ev[m] else [pv.get(f"ps{1+m}")]))
                xe = add("scalar", (lambda m, t0: lambda e: e.copy(
                    xres[m][:, K_CONV - 1 + t0:K_CONV - 1 + t0 + 512], ps[1 + m][:, :]))(m, t0),
                    deps=[xr_mm, prev_hs[m]])
                xres_ev[m].append(xe)
                z_mm = add("tensor", mk_acc8(ps[1 + m], 128, wzb, msl), deps=[xe])
                last_xreader = z_mm
                ze = add("scalar", (lambda m, t0: lambda e: e.copy(
                    sz_sb[m][:, t0:t0 + 512], ps[1 + m][:, :]))(m, t0),
                    deps=[z_mm, pv.get(("szwar", m))])
                sz_ev[m].append(ze)
                psi_dt = 3 if m == 0 else 0
                dt_mm = add("tensor", (lambda m, t0, psi_dt: lambda e: e.matmul(
                    ps[psi_dt][:, :], wdtb[:, m * 128:(m + 1) * 128], dtu_bf[:, t0:t0 + 512],
                    start=True, stop=True))(m, t0, psi_dt),
                    deps=[dtu_c, dt_done[m] or pv.get(f"ps{psi_dt}")]
                    + ([par_ev[-1]] if psi_dt == 0 else []))
                pv["last_dtmm"] = dt_mm
                sp1 = add("scalar", (lambda m, psi_dt: lambda e: e.activation(
                    sp_r[:], ps[psi_dt][:, :], AF.Relu, bias=bdt[m][:]))(m, psi_dt),
                    deps=[dt_mm, dt_done[1 - m] or pv.get("last_dtdone")])
                add("scalar", (lambda m, psi_dt: lambda e: e.activation(
                    sp_t[:], ps[psi_dt][:, :], AF.Abs, bias=bdt[m][:]))(m, psi_dt))
                add("scalar", lambda e: e.activation(sp_t[:], sp_t[:], AF.Exp, scale=-1.0))
                sp4 = add("scalar", lambda e: e.activation(sp_t[:], sp_t[:], AF.Ln, bias=1.0))
                dt_done[m] = add("vector", (lambda m, t0: lambda e: e.tensor_add(
                    dt_sb[m][:, t0:t0 + 512], sp_r[:], sp_t[:]))(m, t0), deps=[sp4])
        pv["last_xreader"] = last_xreader
        pv["last_dtdone"] = dt_done[1]

        u_done = [None, None]
        sz_done = [None, None]
        hs_h = [None, None]
        for m in range(2):
            hcopy = add("vector", (lambda m: lambda e: e.tensor_copy(
                xres[m][:, 0:K_CONV - 1], halo[m][:]))(m),
                deps=[setup_done[m]])
            add("vector", (lambda m: lambda e: e.tensor_scalar(
                cva[:], xres[m][:, 0:TC], cw[m][:, 0:1], None, ALU.mult))(m),
                deps=[hcopy, ld_s, u_done[0]] + xres_ev[m])
            add("vector", (lambda m: lambda e: e.scalar_tensor_tensor(
                cvb[:], xres[m][:, 1:1 + TC], cw[m][:, 1:2], cva[:], ALU.mult, ALU.add))(m))
            add("vector", (lambda m: lambda e: e.scalar_tensor_tensor(
                cva[:], xres[m][:, 2:2 + TC], cw[m][:, 2:3], cvb[:], ALU.mult, ALU.add))(m))
            cv4 = add("vector", (lambda m: lambda e: e.scalar_tensor_tensor(
                cvb[:], xres[m][:, 3:3 + TC], cw[m][:, 3:4], cva[:], ALU.mult, ALU.add))(m))
            hs_h[m] = add("vector", (lambda m: lambda e: e.tensor_copy(
                halo[m][:], xres[m][:, TC:TC + K_CONV - 1]))(m))
            u_done[m] = add("scalar", (lambda m: lambda e: e.activation(
                u_sb[m][:], cvb[:], AF.Silu))(m), deps=[cv4, prev_e1[m]])
            sz_done[m] = add("scalar", (lambda m: lambda e: e.activation(
                sz_sb[m][:], sz_sb[m][:], AF.Silu))(m), deps=sz_ev[m] + [prev_e2])
        prev_hs = hs_h

        last_acc = None
        for n in range(N_SSM):
            rowdma = add("gpsimd", [
                dma(bp_n[:], par_sb[64 + n:65 + n, :]),
                dma(cp_n[:], par_sb[80 + n:81 + n, :])],
                deps=par_ev + [pv.get("rowmm")], dma=True)
            prev_rowdma = rowdma
            reps_done = []
            rowmm = None
            for (src, psu, rep, nm) in [(bp_n, pbp, bp_rep, "bp"), (cp_n, pcp, cp_rep, "cp")]:
                for hh in range(2):
                    def mk_b(src, psu, hh):
                        def f(e):
                            inst = None
                            for s4 in range(2):
                                off = hh * HTC + s4 * 512
                                inst = e.matmul(psu[:, s4 * 512:s4 * 512 + 512], ones_f[:],
                                                src[0:1, off:off + 512], start=True, stop=True)
                            return inst
                        return f
                    mm = add("tensor", mk_b(src, psu, hh),
                             deps=[rowdma, ones_set, pv.get(("bcev_last", nm))])
                    rowmm = mm
                    ev = add("scalar", (lambda psu, rep, hh: lambda e: e.copy(
                        rep[:, hh * HTC:(hh + 1) * HTC], psu[:, :]))(psu, rep, hh),
                        deps=[mm, last_acc or prev_last_acc])
                    pv[("bcev_last", nm)] = ev
                    reps_done.append(ev)
            pv["rowmm"] = rowmm
            for m in range(2):
                it = (c * 32 + n * 2 + m) if False else (n * 2 + m)
                par_i = it % 2
                at = add("scalar", (lambda m, n, par_i: lambda e: e.activation(
                    at_sb[par_i][:], dt_sb[m][:], AF.Exp, scale=Acol[m][:, n:n + 1]))(m, n, par_i),
                    deps=dt_done + [setup_done[m], pv.get(("accpar", par_i)) or prev_last_acc])
                s1 = add("vector", (lambda m, n, par_i: lambda e: e.tensor_scalar(
                    s1_sb[:], at_sb[par_i][:], rcol[m][:, n:n + 1], nrcol[m][:, n:n + 1],
                    ALU.mult, ALU.add))(m, n, par_i), deps=[at])
                add("vector", (lambda m, n: lambda e: e.scalar_tensor_tensor(
                    bt_sb[:], dt_sb[m][:], mcol[m][:, n:n + 1], s1_sb[:], ALU.mult, ALU.add))(m, n))
                add("vector", (lambda m: lambda e: e.tensor_tensor(
                    w_sb[:], u_sb[m][:], bp_rep[:], ALU.mult))(m), deps=[u_done[m]] + reps_done)
                Xv = add("vector", lambda e: e.tensor_tensor(s1_sb[:], bt_sb[:], w_sb[:], ALU.mult))
                if c == 0:
                    scn = add("vector", (lambda par_i: lambda e: e.tensor_tensor_scan(
                        bt_sb[:], at_sb[par_i][:], s1_sb[:], 0.0, ALU.mult, ALU.add))(par_i),
                        deps=[at], force_deps=[Xv])
                else:
                    scn = add("vector", (lambda m, n, par_i: lambda e: e.tensor_tensor_scan(
                        bt_sb[:], at_sb[par_i][:], s1_sb[:], state[m][:, n:n + 1], ALU.mult, ALU.add))(m, n, par_i),
                        deps=[at], force_deps=[Xv])
                add("vector", (lambda m, n: lambda e: e.tensor_copy(
                    state[m][:, n:n + 1], bt_sb[:, TC - 1:TC]))(m, n), force_deps=[scn])
                add("vector", (lambda par_i: lambda e: e.tensor_tensor(
                    at_sb[par_i][:], bt_sb[:], cp_rep[:], ALU.mult))(par_i))
                if n == 0:
                    last_acc = add("vector", (lambda m, par_i: lambda e: e.tensor_copy(
                        yacc[m][:], at_sb[par_i][:]))(m, par_i), deps=[prev_e1[m]])
                else:
                    last_acc = add("vector", (lambda m, par_i: lambda e: e.tensor_add(
                        yacc[m][:], yacc[m][:], at_sb[par_i][:]))(m, par_i))
                pv[("lastn", m)] = last_acc
                pv[("accpar", par_i)] = last_acc

        for m in range(2):
            e1 = add("vector", (lambda m: lambda e: e.scalar_tensor_tensor(
                y1t_f[:], u_sb[m][:], dpar[m][:], yacc[m][:], ALU.mult, ALU.add))(m),
                deps=[pv[("lastn", m)], ld_s])
            e2 = add("vector", (lambda m: lambda e: e.tensor_tensor(
                y1bf[:], y1t_f[:], sz_sb[m][:], ALU.mult))(m),
                deps=[sz_done[m], prev_st])
            prev_e2 = e2
            st = add("gpsimd", [dma(y1_d[m * 128:(m + 1) * 128, c0:c0 + TC], y1bf[:])],
                     deps=[e2], dma=True)
            prev_st = st
            prev_e1[m] = e1
            pv[("szwar", m)] = e2
        pv["ps0"] = dt_done[1]
        pv["ps1"] = sz_ev[0][-1]
        pv["ps2"] = sz_ev[1][-1]
        pv["ps3"] = dt_done[0]
        prev_last_acc = pv[("lastn", 1)]

    if debug:
        def dump(name, tile, shape, dtype=F32, dep=None):
            d_ = nc.dram_tensor("dbg_" + name, shape, dtype, kind="ExternalOutput").ap()
            add("gpsimd", [dma(d_, tile)], deps=[dep] if dep else [prev_last_acc], dma=True)
        dump("Acol0", Acol[0][:], [128, N_SSM])
        dump("rcol0", rcol[0][:], [128, N_SSM])
        dump("mcol0", mcol[0][:], [128, N_SSM])
        dump("dt0", dt_sb[0][:], [128, TC])
        dump("u0", u_sb[0][:], [128, TC])
        dump("sz0", sz_sb[0][:], [128, TC])
        dump("par", par_sb[:], [96, TC])
        dump("bprep", bp_rep[:], [128, TC])
        dump("cprep", cp_rep[:], [128, TC])
        dump("yacc0", yacc[0][:], [128, TC])
        dump("state0", state[0][:], [128, N_SSM])
        dump("dtu", dtu_bf[:], [R_SSM, TC], BF16)
        dump("xres0", xres[0][:], [128, K_CONV - 1 + TC])


def emit_ln(pr, tiles, c0, w, ps_m, ps_v, hbf2, sq2, rstd, c128, epsc,
            tile_deps=(), gamma=None, beta=None):
    add = pr.add
    pv = pr.pv
    mms = []
    for k in range(8):
        cast = add("vector", (lambda k: lambda e: e.tensor_copy(
            hbf2[k % 2][:, 0:w], tiles[k][:, c0:c0 + w]))(k),
            deps=list(tile_deps) + [pv.get(("ln_hbf", k % 2))])
        mm = add("tensor", (lambda k: lambda e: e.matmul(
            ps_m[:, 0:w], c128[:], hbf2[k % 2][:, 0:w],
            start=(k == 0), stop=(k == 7)))(k),
            deps=[cast] + ([pv.get("ln_psm")] if k == 0 else []))
        pv[("ln_hbf", k % 2)] = mm
        mms.append(mm)
    xcs = []
    for k in range(8):
        xcs.append(add("vector", (lambda k: lambda e: e.tensor_tensor(
            tiles[k][:, c0:c0 + w], tiles[k][:, c0:c0 + w], ps_m[:, 0:w],
            ALU.subtract))(k), deps=[mms[7]]))
    pv["ln_psm"] = xcs[-1]
    vms = []
    for k in range(8):
        sq = add("scalar", (lambda k: lambda e: e.activation(
            sq2[k % 2][:, 0:w], tiles[k][:, c0:c0 + w], AF.Square))(k),
            deps=[xcs[k], pv.get(("ln_sq", k % 2))])
        vm = add("tensor", (lambda k: lambda e: e.matmul(
            ps_v[:, 0:w], c128[:], sq2[k % 2][:, 0:w],
            start=(k == 0), stop=(k == 7)))(k),
            deps=[sq] + ([pv.get("ln_psv")] if k == 0 else []))
        pv[("ln_sq", k % 2)] = vm
        vms.append(vm)
    eps_ap, nhalf_ap, cdep = epsc
    r1 = add("scalar", lambda e: e.activation(rstd[:, 0:w], ps_v[:, 0:w], AF.Ln, bias=eps_ap),
             deps=[vms[7], pv.get("ln_rstd"), cdep])
    r2 = add("scalar", lambda e: e.activation(rstd[:, 0:w], rstd[:, 0:w], AF.Exp, scale=nhalf_ap))
    pv["ln_psv"] = r2
    apps = []
    for k in range(8):
        a1 = add("vector", (lambda k: lambda e: e.tensor_tensor(
            tiles[k][:, c0:c0 + w], tiles[k][:, c0:c0 + w], rstd[:, 0:w], ALU.mult))(k),
            deps=[r2])
        if gamma is not None:
            a1 = add("vector", (lambda k: lambda e: e.tensor_scalar(
                tiles[k][:, c0:c0 + w], tiles[k][:, c0:c0 + w], gamma[k][:], beta[k][:],
                ALU.mult, ALU.add))(k))
        apps.append(a1)
    pv["ln_rstd"] = apps[-1]
    return apps


def _p2_drams(nc, kind_out="ExternalOutput"):
    SP = SPAN + 1
    return {
        "xh": nc.dram_tensor("xh", [D, SP], F32, kind="ExternalInput").ap(),
        "y1h": nc.dram_tensor("y1h", [D, SP], BF16, kind="ExternalInput").ap(),
        "woT": nc.dram_tensor("woT", [D, D], F32, kind="ExternalInput").ap(),
        "wrT": nc.dram_tensor("wrT", [D, D], F32, kind="ExternalInput").ap(),
        "wkT": nc.dram_tensor("wkT", [D, D], F32, kind="ExternalInput").ap(),
        "wvT": nc.dram_tensor("wvT", [D, D], F32, kind="ExternalInput").ap(),
        "winT": nc.dram_tensor("winT", [D, 3 * D], F32, kind="ExternalInput").ap(),
        "mur": nc.dram_tensor("mur", [8, 128, 1], F32, kind="ExternalInput").ap(),
        "muk": nc.dram_tensor("muk", [8, 128, 1], F32, kind="ExternalInput").ap(),
        "sinT": nc.dram_tensor("sinT", [128, SPAN], BF16, kind="ExternalInput").ap(),
        "cosT": nc.dram_tensor("cosT", [128, SPAN], BF16, kind="ExternalInput").ap(),
        "x2": nc.dram_tensor("x2", [D, SPAN], F32, kind=kind_out).ap(),
        "qb": nc.dram_tensor("qb", [D, SPAN], BF16, kind=kind_out).ap(),
        "qrb": nc.dram_tensor("qrb", [D, SPAN], BF16, kind=kind_out).ap(),
        "kb": nc.dram_tensor("kb", [D, SPAN], BF16, kind=kind_out).ap(),
        "krb": nc.dram_tensor("krb", [D, SPAN], BF16, kind=kind_out).ap(),
        "vb": nc.dram_tensor("vb", [D, SPAN], BF16, kind=kind_out).ap(),
        "x2s": nc.dram_tensor("x2s", [D, SPAN], BF16, kind="Internal").ap(),
    }


def build_phase2():
    nc = bass.Bass("TRN2", target_bir_lowering=False, detect_race_conditions=False)
    E = _p2_drams(nc)
    pr = Prog(nc)
    sc = Scope(nc, "p2_")
    emit_phase2(nc, pr, E, sc)
    sc.close()
    pr.emit()
    return nc


def emit_phase2(nc, pr, E, sc):
    SP = SPAN + 1
    xh_d = E["xh"]
    y1h_d = E["y1h"]
    woT_d = E["woT"]
    wrT_d = E["wrT"]
    wkT_d = E["wkT"]
    wvT_d = E["wvT"]
    winT_d = E["winT"]
    mur_d = E["mur"]
    muk_d = E["muk"]
    sinT_d = E["sinT"]
    cosT_d = E["cosT"]
    x2_d = E["x2"]
    qb_d = E["qb"]
    qrb_d = E["qrb"]
    kb_d = E["kb"]
    krb_d = E["krb"]
    vb_d = E["vb"]
    x2s_d = E["x2s"]

    A = sc.A
    P = sc.P
    x1 = [A(f"x1_{k}", [128, SP], F32) for k in range(8)]
    tx2 = [A(f"tx2_{k}", [128, 512], F32) for k in range(8)]
    wstage = A("wstage", [128, 3 * D], F32)
    wbuf = [A(f"wbuf{k}", [128, 3 * D], BF16) for k in range(8)]
    y1t = [A(f"y1t{k}", [128, 512], BF16) for k in range(8)]
    hbf2 = [A(f"hbf{i}", [128, 512], BF16) for i in range(2)]
    sq2 = [A(f"sq{i}", [128, 512], BF16) for i in range(2)]
    rstd = A("rstd", [128, 512], F32)
    mur = [A(f"mur{k}", [128, 1], F32) for k in range(8)]
    muk = [A(f"muk{k}", [128, 1], F32) for k in range(8)]
    xr_bf = [A(f"xrbf{k}", [128, 512], BF16) for k in range(8)]
    xk_bf = [A(f"xkbf{k}", [128, 512], BF16) for k in range(8)]
    ka_bf = [A(f"kabf{k}", [128, 512], BF16) for k in range(8)]
    r_sb = A("r_sb", [128, 512], F32)
    t_f1 = A("t_f1", [128, 512], F32)
    t_f2 = A("t_f2", [128, 512], F32)
    sinT = A("sinT_s", [128, SPAN], BF16)
    cosT = A("cosT_s", [128, SPAN], BF16)
    x2bf = [A(f"x2bf{k}", [128, 512], BF16) for k in range(8)]
    q_bf = A("q_bf", [128, 512], BF16)
    rot_bf = A("rot_bf", [128, 512], BF16)
    o_bf = A("o_bf", [128, 512], BF16)
    c128s = A("c128s", [128, 128], BF16)
    Rrot = A("Rrot_s", [128, 128], BF16)
    eps_t = A("eps_t", [128, 1], F32)
    nh_t = A("nh_t", [128, 1], F32)
    c128_d = nc.inline_tensor(np.full((128, 128), 1.0 / 1024.0, dtype=NBF), name="c128_p2")
    Rnp = np.zeros((128, 128), dtype=NBF)
    for i in range(64):
        Rnp[2 * i, 2 * i + 1] = 1.0
        Rnp[2 * i + 1, 2 * i] = -1.0
    Rrot_d = nc.inline_tensor(Rnp, name="Rrot")

    ps = [P(f"ps{i}", [128, 512], F32) for i in range(8)]

    pr.pv = {}
    add = pr.add
    pv = pr.pv

    def psdep(i):
        return pv.get(("ps", i))

    def psset(i, h):
        pv[("ps", i)] = h

    cst1 = add("vector", lambda e: e.memset(eps_t[:], 1e-5))
    cst2 = add("vector", lambda e: e.memset(nh_t[:], -0.5))
    epsc = (eps_t[:], nh_t[:], cst2)
    ld_x = add("sync", [dma(x1[k][:], xh_d[k * 128:(k + 1) * 128, :]) for k in range(8)], dma=True)
    ld_c = add("sync", [dma(c128s[:], c128_d.ap()), dma(Rrot[:], Rrot_d.ap()),
                        dma(sinT[:], sinT_d), dma(cosT[:], cosT_d)]
               + [dma(mur[k][:], mur_d[k]) for k in range(8)]
               + [dma(muk[k][:], muk_d[k]) for k in range(8)], dma=True)

    def load_weights(src_d, ncols, col_off, dep_extra=()):
        hs = []
        for k in range(8):
            ldh = add("sync", dma(wstage[:, 0:ncols], src_d[k * 128:(k + 1) * 128, 0:ncols]),
                      deps=[pv.get("wcast")] + list(dep_extra), dma=True)
            ch = add("vector", (lambda k, ncols, col_off: lambda e: e.tensor_copy(
                wbuf[k][:, col_off:col_off + ncols], wstage[:, 0:ncols]))(k, ncols, col_off),
                deps=[ldh] + list(dep_extra))
            pv["wcast"] = ch
            hs.append(ch)
        return hs

    wC = load_weights(woT_d, D, 0)
    tilesC = [(0, 512), (512, 512), (1024, 512), (1536, 512), (2048, 1)]
    for (c0, w) in tilesC:
        ldy = add("sync", [dma(y1t[k][:, 0:w], y1h_d[k * 128:(k + 1) * 128, c0:c0 + w], slow=(w == 1))
                           for k in range(8)], deps=[pv.get("y1done")], dma=True)
        h0s = []
        for o in range(8):
            pso = ps[o % 2]

            def mk8(pso, w, cols, rhs_tiles, rw=None):
                def f(e):
                    inst = None
                    for k in range(8):
                        inst = e.matmul(pso[:, 0:w], wbuf[k][:, cols[0] + 0:cols[1]][:, o * 128:o * 128 + 128] if False else wbuf[k][:, cols[0] + o * 128:cols[0] + (o + 1) * 128],
                                        rhs_tiles[k][:, 0:w] if rw is None else rhs_tiles[k][:, rw[0]:rw[1]],
                                        start=(k == 0), stop=(k == 7))
                    return inst
                return f
            mmo = add("tensor", mk8(pso, w, (0, D), y1t), deps=wC + [ldy, psdep(o % 2)])
            h0 = add("vector", (lambda o, w, pso, c0: lambda e: e.tensor_tensor(
                x1[o][:, c0:c0 + w], pso[:, 0:w], x1[o][:, c0:c0 + w], ALU.add))(o, w, pso, c0),
                deps=[mmo, ld_x])
            psset(o % 2, h0)
            h0s.append(h0)
        pv["y1done"] = mmo
        emit_ln(pr, x1, c0, w, ps[2][:, :], ps[3][:, :], hbf2, sq2, rstd, c128s,
                epsc, h0s + [ld_c, cst1])
        pv["ln_psm_guard"] = None

    wD = load_weights(wrT_d, D, 0, dep_extra=[pv["y1done"]])
    wD += load_weights(wkT_d, D, D)
    wD += load_weights(wvT_d, D, 2 * D)
    x2st = []
    for tt in range(4):
        s = 1 + tt * 512
        c0 = tt * 512
        shfts = []
        for k in range(8):
            add("vector", (lambda k, s: lambda e: e.tensor_tensor(
                t_f1[:], x1[k][:, s:s + 512], x1[k][:, s - 1:s - 1 + 512], ALU.subtract))(k, s),
                deps=[ld_c, pv.get("xrwar")])
            xrh = add("vector", (lambda k, s: lambda e: e.scalar_tensor_tensor(
                xr_bf[k][:], t_f1[:], mur[k][:], x1[k][:, s - 1:s - 1 + 512],
                ALU.mult, ALU.add))(k, s))
            add("vector", (lambda k, s: lambda e: e.tensor_tensor(
                t_f1[:], x1[k][:, s:s + 512], x1[k][:, s - 1:s - 1 + 512], ALU.subtract))(k, s))
            xkh = add("vector", (lambda k, s: lambda e: e.scalar_tensor_tensor(
                xk_bf[k][:], t_f1[:], muk[k][:], x1[k][:, s - 1:s - 1 + 512],
                ALU.mult, ALU.add))(k, s))
            shfts += [xrh, xkh]
        kacts = []
        for o in range(8):
            psi = o % 2
            mmk = add("tensor", (lambda o, psi: lambda e: [
                e.matmul(ps[psi][:, :], wbuf[k][:, D + o * 128:D + (o + 1) * 128],
                         xk_bf[k][:], start=(k == 0), stop=(k == 7)) for k in range(8)][-1])(o, psi),
                deps=wD + shfts + [psdep(psi)])
            a1 = add("scalar", (lambda psi: lambda e: e.activation(
                t_f1[:], ps[psi][:, :], AF.Relu))(psi), deps=[mmk, pv.get("tf1_sc")])
            a2 = add("scalar", (lambda o: lambda e: e.activation(
                ka_bf[o][:], t_f1[:], AF.Square))(o), deps=[pv.get(("kawar", o))])
            pv["tf1_sc"] = a2
            psset(psi, a2)
            kacts.append(a2)
        h1s = []
        for o in range(8):
            mkv = add("tensor", (lambda o: lambda e: [
                e.matmul(ps[2 + o % 2][:, :], wbuf[k][:, 2 * D + o * 128:2 * D + (o + 1) * 128],
                         ka_bf[k][:], start=(k == 0), stop=(k == 7)) for k in range(8)][-1])(o),
                deps=kacts + [psdep(2 + o % 2)])
            pv[("kawar", o)] = mkv
            mr = add("tensor", (lambda o: lambda e: [
                e.matmul(ps[4 + o % 2][:, :], wbuf[k][:, o * 128:(o + 1) * 128],
                         xr_bf[k][:], start=(k == 0), stop=(k == 7)) for k in range(8)][-1])(o),
                deps=shfts + [psdep(4 + o % 2)])
            sg = add("scalar", (lambda o: lambda e: e.activation(
                r_sb[:], ps[4 + o % 2][:, :], AF.Sigmoid))(o), deps=[mr, pv.get("rwar")])
            psset(4 + o % 2, sg)
            of = add("vector", (lambda o: lambda e: e.tensor_tensor(
                t_f2[:], ps[2 + o % 2][:, :], r_sb[:], ALU.mult))(o), deps=[sg, mkv])
            pv["rwar"] = of
            psset(2 + o % 2, of)
            h1 = add("vector", (lambda o, s: lambda e: e.tensor_tensor(
                tx2[o][:], t_f2[:], x1[o][:, s:s + 512], ALU.add))(o, s),
                deps=[pv.get(("tx2war", o))])
            h1s.append(h1)
        apps = emit_ln(pr, tx2, 0, 512, ps[6][:, :], ps[7][:, :], hbf2, sq2, rstd,
                       c128s, epsc, h1s)
        for o in range(8):
            cb = add("vector", (lambda o: lambda e: e.tensor_copy(
                x2bf[o][:], tx2[o][:]))(o), deps=[apps[o], pv.get(("x2bfwar", o))])
            st1 = add("gpsimd", [dma(x2_d[o * 128:(o + 1) * 128, c0:c0 + 512], tx2[o][:]),
                                 dma(x2s_d[o * 128:(o + 1) * 128, c0:c0 + 512], x2bf[o][:])],
                      deps=[apps[o], cb], dma=True)
            pv[("tx2war", o)] = st1
            pv[("x2bfwar", o)] = st1
            x2st.append(st1)

    wE = load_weights(winT_d, 3 * D, 0, dep_extra=[pv["rwar"]])
    for tt in range(4):
        c0 = tt * 512
        casts = []
        for k in range(8):
            ldh = add("gpsimd", dma(x2bf[k][:], x2s_d[k * 128:(k + 1) * 128, c0:c0 + 512]),
                      deps=[pv.get(("x2ld", k))] + x2st, dma=True)
            casts.append(ldh)
        for ot in range(24):
            psi = ot % 2
            mmq = add("tensor", (lambda ot, psi: lambda e: [
                e.matmul(ps[psi][:, :], wbuf[k][:, ot * 128:(ot + 1) * 128],
                         x2bf[k][:], start=(k == 0), stop=(k == 7)) for k in range(8)][-1])(ot, psi),
                deps=wE + casts + [psdep(psi)])
            for k in range(8):
                pv[("x2ld", k)] = mmq
            ev = add("vector", (lambda psi: lambda e: e.tensor_copy(
                q_bf[:], ps[psi][:, :]))(psi), deps=[mmq, pv.get("qbfwar")])
            psset(psi, ev)
            if ot < 16:
                dst, dstr = (qb_d, qrb_d) if ot < 8 else (kb_d, krb_d)
                row = (ot % 8) * 128
                mrot = add("tensor", (lambda psi2: lambda e: e.matmul(
                    ps[2 + psi2][:, :], Rrot[:], q_bf[:], start=True, stop=True))(psi),
                    deps=[ev, ld_c, psdep(2 + psi)])
                rbf = add("vector", (lambda psi2: lambda e: e.tensor_copy(
                    rot_bf[:], ps[2 + psi2][:, :]))(psi), deps=[mrot, pv.get("rotwar")])
                psset(2 + psi, rbf)
                t1h = add("vector", (lambda c0: lambda e: e.tensor_tensor(
                    t_f1[:], q_bf[:], cosT[:, c0:c0 + 512], ALU.mult))(c0),
                    deps=[pv.get("tf1e")])
                t2h = add("vector", (lambda c0: lambda e: e.tensor_tensor(
                    t_f2[:], rot_bf[:], sinT[:, c0:c0 + 512], ALU.mult))(c0))
                pv["rotwar"] = t2h
                ob = add("vector", lambda e: e.tensor_add(o_bf[:], t_f1[:], t_f2[:]))
                pv["tf1e"] = ob
                st = add("gpsimd", [dma(dst[row:row + 128, c0:c0 + 512], q_bf[:]),
                                    dma(dstr[row:row + 128, c0:c0 + 512], o_bf[:])],
                         deps=[ev, ob], dma=True)
                pv["qbfwar"] = st
            else:
                row = (ot - 16) * 128
                st = add("gpsimd", [dma(vb_d[row:row + 128, c0:c0 + 512], q_bf[:])],
                         deps=[ev], dma=True)
                pv["qbfwar"] = st


def build_phase3(debug=False, DBG_H=0):
    nc = bass.Bass("TRN2", target_bir_lowering=False, detect_race_conditions=False)
    E = {
        "qb": nc.dram_tensor("qb", [D, SPAN], BF16, kind="ExternalInput").ap(),
        "qrb": nc.dram_tensor("qrb", [D, SPAN], BF16, kind="ExternalInput").ap(),
        "kA": nc.dram_tensor("kA", [D, T], BF16, kind="ExternalInput").ap(),
        "krA": nc.dram_tensor("krA", [D, T], BF16, kind="ExternalInput").ap(),
        "vT": nc.dram_tensor("vT", [T, D], BF16, kind="ExternalInput").ap(),
        "ya": nc.dram_tensor("ya", [D, SPAN], BF16, kind="ExternalOutput").ap(),
    }
    pr = Prog(nc)
    sc = Scope(nc, "p3_")
    emit_phase3(nc, pr, E, sc, debug=debug, DBG_H=DBG_H, vt_transposed=True)
    sc.close()
    pr.emit()
    return nc


def emit_phase3(nc, pr, E, sc, debug=False, DBG_H=0, vt_transposed=True):
    qb_d = E["qb"]
    qrb_d = E["qrb"]
    kA_d = E.get("kA")
    krA_d = E.get("krA")
    vT_d = E.get("vT")
    ya_d = E["ya"]

    A = sc.A
    P = sc.P
    QT = 512
    ya_sb = [A(f"ya{k}", [128, SPAN], BF16) for k in range(8)]
    kh = [A(f"kh{i}", [64, T], BF16) for i in range(2)]
    krh = [A(f"krh{i}", [64, T], BF16) for i in range(2)]
    vh = [[A(f"vh{i}_{j}", [128, 64], BF16) for j in range(64)] for i in range(2)]
    q_hd = A("q_hd", [64, QT], BF16)
    qr_hd = A("qr_hd", [64, QT], BF16)
    E_sb = [A(f"E_sb{i}", [128, QT], BF16) for i in range(2)]
    Eu_sb = [A(f"Eu_sb{i}", [128, QT], BF16) for i in range(2)]
    den_sb = A("den_sb", [64, QT], F32)
    dacc_sb = A("dacc_sb", [64, QT], F32)
    db_sb = A("db_sb", [64, QT], F32)
    ones_bf = A("ones_bf", [128, 64], BF16)
    scl_t = A("scl_t", [128, 1], F32)
    scl_neg1 = A("scl_neg1", [128, 1], F32)

    psr = [P(f"psr{i}", [128, QT], F32) for i in range(2)]
    psu = [P(f"psu{i}", [128, QT], F32) for i in range(2)]
    psy = P("psy", [64, QT], F32)
    psd = P("psd", [64, QT], F32)

    pr.pv = {}
    add = pr.add
    pv = pr.pv

    ones = add("vector", lambda e: e.memset(ones_bf[:], 1.0))
    sclh = add("vector", lambda e: e.memset(scl_t[:], 0.125))
    sneg = add("vector", lambda e: e.memset(scl_neg1[:], -1.0))

    for h in range(H):
        par = h % 2
        if "k_parts" in E:
            ldk = add("sync", [dma(kh[par][:, r * SPAN:(r + 1) * SPAN], E["k_parts"](r, h))
                               for r in range(4)]
                      + [dma(krh[par][:, r * SPAN:(r + 1) * SPAN], E["kr_parts"](r, h))
                         for r in range(4)],
                      deps=[pv.get(("kdone", par))], dma=True)
            ldv = add("scalar", [
                (lambda j, h2, par2: lambda e: e.dma_start(
                    out=vh[par2][j][:], in_=E["v_slice"](h2, j), transpose=True))(j, h, par)
                for j in range(64)],
                deps=[pv.get(("vdone", par))], dma=True)
        else:
            ldk = add("sync", [dma(kh[par][:], kA_d[h * 64:(h + 1) * 64, :]),
                               dma(krh[par][:], krA_d[h * 64:(h + 1) * 64, :])],
                      deps=[pv.get(("kdone", par))], dma=True)
            ldv = add("gpsimd", [dma(vh[par][j][:], vT_d[j * 128:(j + 1) * 128, h * 64:(h + 1) * 64])
                                 for j in range(64)],
                      deps=[pv.get(("vdone", par))], dma=True)
        qtile = h // 2
        for qt in range(SPAN // QT):
            q0 = qt * QT
            qpart = (h % 2) * 64
            qtl = h // 2
            qcp = add("sync", [dma(q_hd[:], qb_d[h * 64:(h + 1) * 64, q0:q0 + QT]),
                               dma(qr_hd[:], qrb_d[h * 64:(h + 1) * 64, q0:q0 + QT])],
                      deps=[pv.get("qhdw")], dma=True)
            qcp2 = qcp
            if debug and h == DBG_H and qt == 0:
                dbg_qh = nc.dram_tensor("dbg_qh", [64, QT], BF16, kind="ExternalOutput").ap()
                add("gpsimd", [dma(dbg_qh, q_hd[:])], deps=[qcp], dma=True)
            NKT = T // 128
            for kt in range(NKT + 1):
                par2 = kt % 2
                if kt < NKT:
                    k0 = kt * 128
                    mmr = add("tensor", (lambda par, k0, par2: lambda e: [
                        e.matmul(p[par2][:, :], kk[par][:, k0:k0 + 128], qh[:, :],
                                 start=True, stop=True)
                        for (p, kk, qh) in ((psr, krh, qr_hd), (psu, kh, q_hd))][-1])(par, k0, par2),
                        deps=[ldk, qcp2, pv.get(("psrw", par2)), pv.get(("psuw", par2))])
                    er = add("scalar", (lambda par2: lambda e: e.activation(
                        E_sb[par2][:], psr[par2][:, :], AF.Exp, scale=scl_t[:]))(par2),
                        deps=[mmr, sclh, pv.get(("Ew", par2))])
                    pv[("psrw", par2)] = er
                    eu = add("scalar", (lambda par2: lambda e: e.activation(
                        Eu_sb[par2][:], psu[par2][:, :], AF.Exp, scale=scl_t[:]))(par2),
                        deps=[mmr, sclh, pv.get(("Euw", par2))])
                    mmu = mmr
                    pv[("psuw", par2)] = eu
                    pv[("erh", par2)] = er
                    pv[("euh", par2)] = eu
                if kt > 0:
                    kc = kt - 1
                    pc = kc % 2
                    mmy = add("tensor", (lambda par, kc, pc: lambda e: [
                        e.matmul(p[:, :], lh if lh is not None else vh[par][kc][:],
                                 ee[pc][:, :],
                                 start=(kc == 0 if p is psy else kc % 8 == 0),
                                 stop=(kc == NKT - 1 if p is psy else kc % 8 == 7))
                        for (p, lh, ee) in ((psy, None, E_sb), (psd, ones_bf[:], Eu_sb))][-1])(par, kc, pc),
                        deps=[ldv, ones, pv.get(("erh", pc)), pv.get(("euh", pc))]
                        + ([pv.get("psyw")] if kc == 0 else [])
                        + ([pv.get("psdw")] if kc % 8 == 0 else []))
                    pv[("Ew", pc)] = mmy
                    pv[("Euw", pc)] = mmy
                    mmd = mmy
                    if kc % 8 == 7:
                        if kc == 7:
                            dadd = add("vector", (lambda: lambda e: e.tensor_copy(
                                dacc_sb[:], psd[:, :]))(), deps=[mmd, pv.get("daccw")])
                        else:
                            dadd = add("vector", (lambda: lambda e: e.tensor_tensor(
                                dacc_sb[:], dacc_sb[:], psd[:, :], ALU.add))(), deps=[mmd])
                        pv["psdw"] = dadd
            pv[("kdone", par)] = mmu
            pv[("vdone", par)] = mmy
            rcp = add("vector", lambda e: e.reciprocal(db_sb[:], dacc_sb[:]),
                      deps=[pv.get("rcpw")], force_deps=[dadd])
            pv["daccw"] = rcp
            if debug and h == DBG_H and qt == 0:
                dbg_db = nc.dram_tensor("dbg_db", [64, QT], F32, kind="ExternalOutput").ap()
                add("gpsimd", [dma(dbg_db, db_sb[:])], deps=[rcp], dma=True)
            ya1 = add("vector", (lambda qpart, qtl, q0: lambda e: e.tensor_tensor(
                ya_sb[qtl][qpart:qpart + 64, q0:q0 + QT], psy[:, :], db_sb[:], ALU.mult))(qpart, qtl, q0),
                deps=[mmy, pv.get(("yawar", h // 2))], force_deps=[rcp])
            pv["rcpw"] = ya1
            pv["psyw"] = ya1
            pv["qhdw"] = mmu
    st = add("gpsimd", [dma(ya_d[k * 128:(k + 1) * 128, :], ya_sb[k][:]) for k in range(8)],
             deps=pr.frontier(), dma=True)


def _p4_drams(nc, skip_io=False):
    d = {}
    if not skip_io:
        d["ya"] = nc.dram_tensor("ya", [D, SPAN], BF16, kind="ExternalInput").ap()
        d["x2"] = nc.dram_tensor("x2", [D, SPAN], F32, kind="ExternalInput").ap()
    d.update({
        "woT4": nc.dram_tensor("woT4", [D, D], F32, kind="ExternalInput").ap(),
        "lng": nc.dram_tensor("lng", [8, 128, 1], F32, kind="ExternalInput").ap(),
        "lnb": nc.dram_tensor("lnb", [8, 128, 1], F32, kind="ExternalInput").ap(),
        "wg": nc.dram_tensor("wg", [D, N_EXP], F32, kind="ExternalInput").ap(),
        "bg": nc.dram_tensor("bg", [1, N_EXP], F32, kind="ExternalInput").ap(),
        "w1": nc.dram_tensor("w1", [N_EXP, D, D_FF], F32, kind="ExternalInput").ap(),
        "b1h": nc.dram_tensor("b1h", [128, N_EXP * 32], F32, kind="ExternalInput").ap(),
        "w2": nc.dram_tensor("w2", [N_EXP, D_FF, D], F32, kind="ExternalInput").ap(),
        "b2mm": nc.dram_tensor("b2mm", [N_EXP, D], F32, kind="ExternalInput").ap(),
        "yout": nc.dram_tensor("yout", [D, SPAN], F32, kind="ExternalOutput").ap(),
        "x3s": nc.dram_tensor("x3s", [D, SPAN], F32, kind="Internal").ap(),
    })
    return d


def build_phase4(debug=False):
    nc = bass.Bass("TRN2", target_bir_lowering=False, detect_race_conditions=False)
    E = _p4_drams(nc)
    pr = Prog(nc)
    sc = Scope(nc, "p4_")
    emit_phase4(nc, pr, E, sc, debug=debug)
    sc.close()
    pr.emit()
    return nc


def emit_phase4(nc, pr, E, sc, debug=False):
    ya_d = E["ya"]
    x2_d = E["x2"]
    woT_d = E["woT4"]
    lng_d = E["lng"]
    lnb_d = E["lnb"]
    wg_d = E["wg"]
    bg_d = E["bg"]
    w1_d = E["w1"]
    b1_d = E["b1h"]
    w2_d = E["w2"]
    b2_d = E["b2mm"]
    yout_d = E["yout"]
    x3s_d = E["x3s"]

    A = sc.A
    P = sc.P
    TT4 = 1024

    x3bf = [A(f"x3bf{k}", [128, SPAN], BF16) for k in range(8)]
    hbf2 = [A(f"hbf{i}", [128, TT4], BF16) for i in range(2)]
    sq2 = [A(f"sq{i}", [128, TT4], BF16) for i in range(2)]
    rstd = A("rstd", [128, TT4], F32)
    lng = [A(f"lng{k}", [128, 1], F32) for k in range(8)]
    lnb = [A(f"lnb{k}", [128, 1], F32) for k in range(8)]
    c128s = A("c128s", [128, 128], BF16)
    id128 = A("id128", [128, 128], F32)
    ones1_bf = A("ones1_bf", [1, 128], BF16)
    ones1_f = A("ones1_f", [1, 128], F32)
    wg_sb = A("wg_sb", [128, 8 * N_EXP], BF16)
    bg_bf = A("bg_bf", [1, N_EXP], BF16)
    b1c = A("b1c", [128, N_EXP * 32], F32)
    b2c = A("b2c", [N_EXP, D], F32)
    lg_sb = A("lg_sb", [128, N_EXP], F32)
    gbar_t = A("gbar_t", [1, 1], F32)
    mx4 = A("mx4", [128, 4], F32)
    mx2 = A("mx2", [128, 2], F32)
    negone_t = A("negone_t", [128, 1], F32)
    m1 = A("m1", [128, 1], F32)
    m2 = A("m2", [128, 1], F32)
    msk1 = A("msk1", [128, N_EXP], F32)
    msk2 = A("msk2", [128, N_EXP], F32)
    s_sc = A("s_sc", [128, 1], F32)
    w1_sc = A("w1_sc", [128, 1], F32)
    w2_sc = A("w2_sc", [128, 1], F32)
    comb_tm = A("comb_tm", [128, N_EXP], F32)
    comb_fm = A("comb_fm", [N_EXP, SPAN], F32)
    comb_n = A("comb_n", [1, TT4], F32)
    crep = A("crep", [128, TT4], F32)
    eps_t = A("eps_t", [128, 1], F32)
    nh_t = A("nh_t", [128, 1], F32)
    c128_d = nc.inline_tensor(np.full((128, 128), 1.0 / 1024.0, dtype=NBF), name="c128_p4")
    id_d = nc.inline_tensor(np.eye(128, dtype=np.float32), name="id128_c")

    ps = [P(f"ps{i}", [128, 512], F32) for i in range(2)]
    psh = [P(f"psh{i}", [128, TT4], F32) for i in range(2)]
    psy2 = P("psy2", [128, TT4], F32)

    pr.pv = {}
    add = pr.add
    pv = pr.pv

    def psdep(i):
        return pv.get(("ps", i))

    def psset(i, h):
        pv[("ps", i)] = h

    cst1 = add("vector", lambda e: e.memset(eps_t[:], 1e-5))
    cst2 = add("vector", lambda e: e.memset(nh_t[:], -0.5))
    csneg = add("vector", lambda e: e.memset(negone_t[:], -1.0))
    epsc = (eps_t[:], nh_t[:], cst2)
    ld_c = add("sync", [dma(c128s[:], c128_d.ap()), dma(id128[:], id_d.ap()),
                        dma(b1c[:], b1_d), dma(b2c[:], b2_d)]
               + [dma(lng[k][:], lng_d[k]) for k in range(8)]
               + [dma(lnb[k][:], lnb_d[k]) for k in range(8)], dma=True)
    wgst = A("wgst", [128, N_EXP], F32)
    wgh = []
    for k in range(8):
        l1 = add("sync", dma(wgst[:], wg_d[k * 128:(k + 1) * 128, :]),
                 deps=[pv.get("wgc")], dma=True)
        c1 = add("vector", (lambda k: lambda e: e.tensor_copy(
            wg_sb[:, k * N_EXP:(k + 1) * N_EXP], wgst[:]))(k), deps=[l1])
        pv["wgc"] = c1
        wgh.append(c1)
    ones_set = add("vector", lambda e: e.memset(ones1_bf[:], 1.0))
    ones_set2 = add("vector", lambda e: e.memset(ones1_f[:], 1.0))
    lbg = add("sync", dma(wgst[0:1, 0:N_EXP], bg_d), deps=[pv.get("wgc")], dma=True)
    cbg = add("vector", lambda e: e.tensor_copy(bg_bf[:, 0:N_EXP], wgst[0:1, 0:N_EXP]), deps=[lbg])
    pv["wgc"] = cbg

    gF = [nc.sbuf_tensor(f"x2f_{k}", [128, SPAN], F32) for k in range(8)]
    x2f = [g.__enter__() for g in gF]
    gF2 = [nc.sbuf_tensor(f"t1_{k}", [128, TT4], F32) for k in range(8)]
    t1 = [g.__enter__() for g in gF2]
    gF4 = [nc.sbuf_tensor(f"yat_{k}", [128, TT4], BF16) for k in range(8)]
    ya_t = [g.__enter__() for g in gF4]
    gF5 = [nc.sbuf_tensor(f"wo_{k}", [128, D], BF16) for k in range(8)]
    wo_b = [g.__enter__() for g in gF5]
    gF6 = nc.sbuf_tensor("wstF", [128, D], F32)
    wstF = gF6.__enter__()

    ld_x2 = add("sync", [dma(x2f[k][:], x2_d[k * 128:(k + 1) * 128, :]) for k in range(8)], dma=True)
    woh = []
    for k in range(8):
        l1 = add("sync", dma(wstF[:], woT_d[k * 128:(k + 1) * 128, :]),
                 deps=[pv.get("woc")], dma=True)
        c1 = add("vector", (lambda k: lambda e: e.tensor_copy(wo_b[k][:], wstF[:]))(k), deps=[l1])
        pv["woc"] = c1
        woh.append(c1)

    for tt in range(SPAN // TT4):
        c0 = tt * TT4
        ldy = add("sync", [dma(ya_t[k][:], ya_d[k * 128:(k + 1) * 128, c0:c0 + TT4])
                           for k in range(8)], deps=[pv.get("yald")], dma=True)
        h2s = []
        for o in range(8):
            mmo = add("tensor", (lambda o: lambda e: [
                e.matmul(psh[o % 2][:, i * 512:(i + 1) * 512], wo_b[k][:, o * 128:(o + 1) * 128],
                         ya_t[k][:, i * 512:(i + 1) * 512],
                         start=(k == 0), stop=(k == 7))
                for i in range(2) for k in range(8)][-1])(o),
                deps=woh + [ldy, pv.get(("pshw", o % 2))])
            h2 = add("vector", (lambda o, c0: lambda e: e.tensor_tensor(
                t1[o][:], psh[o % 2][:, :], x2f[o][:, c0:c0 + TT4], ALU.add))(o, c0),
                deps=[mmo, ld_x2, pv.get("lastF")])
            pv[("pshw", o % 2)] = h2
            h2s.append(h2)
        pv["yald"] = mmo
        for hw in range(2):
            emit_ln(pr, t1, hw * 512, 512, ps[0][:, :], ps[1][:, :], hbf2, sq2, rstd, c128s,
                    epsc, h2s + [ld_c, cst1], gamma=lng, beta=lnb)
        h3s = []
        for o in range(8):
            h3 = add("vector", (lambda o, c0: lambda e: e.tensor_tensor(
                t1[o][:], t1[o][:], x2f[o][:, c0:c0 + TT4], ALU.add))(o, c0),
                deps=[pv["ln_rstd"]])
            h3s.append(h3)
        apps = []
        for hw in range(2):
            apps_h = emit_ln(pr, t1, hw * 512, 512, ps[0][:, :], ps[1][:, :], hbf2, sq2, rstd,
                             c128s, epsc, h3s)
            apps = apps_h if hw == 0 else [b for b in apps_h]
        apps = [pv["ln_rstd"]] * 8
        for o in range(8):
            cb = add("vector", (lambda o, c0: lambda e: e.tensor_copy(
                x3bf[o][:, c0:c0 + TT4], t1[o][:]))(o, c0), deps=[apps[o]])
            st = add("gpsimd", [dma(x3s_d[o * 128:(o + 1) * 128, c0:c0 + TT4], t1[o][:])],
                     deps=[apps[o]], dma=True)
            pv[("x3st", tt, o)] = st
            pv["lastF"] = st

    barrierF = pr.frontier()
    gF6.__exit__(None, None, None)
    for g in reversed(gF5):
        g.__exit__(None, None, None)
    for g in reversed(gF4):
        g.__exit__(None, None, None)
    for g in reversed(gF2):
        g.__exit__(None, None, None)
    for g in reversed(gF):
        g.__exit__(None, None, None)

    ymoe = [A(f"ymoe{k}", [128, TT4], F32) for k in range(8)]

    expert_order = [0, 2, 4, 6, 1, 3, 5, 7]
    for tt in range(SPAN // TT4):
        c0 = tt * TT4
        gA = ([nc.sbuf_tensor(f"af{tt}_{ft}", [128, TT4], BF16) for ft in range(32)]
              + [nc.sbuf_tensor(f"w2st{tt}_{i}", [128, 1024], F32) for i in range(2)]
              + [nc.sbuf_tensor(f"w2bf{tt}_{i}", [128, 1024], BF16) for i in range(2)]
              + [nc.sbuf_tensor(f"w1st{tt}_{i}", [128, 128], F32) for i in range(32)]
              + [nc.sbuf_tensor(f"w1bf{tt}_{i}", [128, 128], BF16) for i in range(32)])
        hA = [g.__enter__() for g in gA]
        a_full = hA[0:32]
        w2st = hA[32:34]
        w2bf = hA[34:36]
        w1st = hA[36:68]
        w1bf = hA[68:100]
        scopeA_gate = pr.frontier() if tt else list(barrierF)
        gts = []
        for g in range(TT4 // 128):
            tg0 = c0 + g * 128
            mmg = add("tensor", (lambda tg0: lambda e: [
                e.matmul(ps[0][:, 0:N_EXP], x3bf[k][:, tg0:tg0 + 128],
                         wg_sb[:, k * N_EXP:(k + 1) * N_EXP],
                         start=(k == 0), stop=False) for k in range(8)][-1])(tg0),
                deps=barrierF + wgh + [psdep(0), ld_c])
            mmg2 = add("tensor", lambda e: e.matmul(
                ps[0][:, 0:N_EXP], ones1_bf[:], bg_bf[:, 0:N_EXP], start=False, stop=True),
                deps=[ones_set, cbg])
            ev = add("scalar", lambda e: e.copy(lg_sb[:], ps[0][0:128, 0:N_EXP]), deps=[mmg2, pv.get("lgwar")])
            psset(0, ev)
            p_ = add("vector", lambda e: e.tensor_max(mx4[:], lg_sb[:, 0:4], lg_sb[:, 4:8]),
                     deps=[ev], force_deps=[pv.get("trp_prev")])
            p_ = add("vector", lambda e: e.tensor_max(mx2[:], mx4[:, 0:2], mx4[:, 2:4]),
                     force_deps=[p_])
            p_ = add("vector", lambda e: e.tensor_max(m1[:], mx2[:, 0:1], mx2[:, 1:2]),
                     force_deps=[p_])
            mk1 = p_ = add("vector", lambda e: e.tensor_scalar(msk1[:], lg_sb[:], m1[:], None, ALU.is_ge),
                           force_deps=[p_])
            p_ = add("vector", lambda e: e.tensor_scalar(msk2[:], msk1[:], 1e30, None, ALU.mult),
                     force_deps=[p_])
            p_ = add("vector", lambda e: e.tensor_sub(msk2[:], lg_sb[:], msk2[:]),
                     force_deps=[p_])
            p_ = add("vector", lambda e: e.tensor_max(mx4[:], msk2[:, 0:4], msk2[:, 4:8]),
                     force_deps=[p_])
            p_ = add("vector", lambda e: e.tensor_max(mx2[:], mx4[:, 0:2], mx4[:, 2:4]),
                     force_deps=[p_])
            p_ = add("vector", lambda e: e.tensor_max(m2[:], mx2[:, 0:1], mx2[:, 1:2]),
                     force_deps=[p_])
            mk2 = p_ = add("vector", lambda e: e.tensor_scalar(msk2[:], msk2[:], m2[:], None, ALU.is_ge),
                           force_deps=[p_])
            d21 = p_ = add("vector", lambda e: e.tensor_sub(s_sc[:], m2[:], m1[:]),
                           force_deps=[p_])
            sg1 = add("scalar", (lambda: lambda e: e.activation(
                w1_sc[:], s_sc[:], AF.Sigmoid, scale=negone_t[:]))(), deps=[d21, csneg])
            sg2 = add("scalar", lambda e: e.activation(w2_sc[:], s_sc[:], AF.Sigmoid),
                      force_deps=[sg1])
            cmb = p_ = add("vector", lambda e: e.tensor_scalar(comb_tm[:], msk1[:], w1_sc[:], None, ALU.mult),
                          deps=[sg2], force_deps=[d21])
            cmb2 = add("vector", lambda e: e.scalar_tensor_tensor(
                comb_tm[:], msk2[:], w2_sc[:], comb_tm[:], ALU.mult, ALU.add),
                force_deps=[p_])
            trp = add("tensor", (lambda: lambda e: e.transpose(
                ps[1][0:8, 0:128], comb_tm[:], id128[:]))(), deps=[cmb2, ld_c, psdep(1)])
            pv["trp_prev"] = trp
            evt = add("scalar", (lambda tg0: lambda e: e.copy(
                comb_fm[:, tg0:tg0 + 128], ps[1][0:8, 0:128]))(tg0), deps=[trp, pv.get("cfwar")])
            psset(1, evt)
            pv["lgwar"] = cmb2
            gts.append(evt)
        gate_bar = pr.frontier()
        for e_i in expert_order:
            rowd = add("gpsimd", [dma(comb_n[:], comb_fm[e_i:e_i + 1, c0:c0 + TT4])],
                       deps=gts + [pv.get("crepmm")] + (gate_bar if e_i == expert_order[0] else []), dma=True)
            crmm1 = add("tensor", (lambda: lambda e: e.matmul(
                ps[0][:, :], ones1_f[:], comb_n[0:1, 0:512], start=True, stop=True))(),
                deps=[rowd, ones_set2, psdep(0)])
            crmm2 = add("tensor", (lambda: lambda e: e.matmul(
                ps[1][:, :], ones1_f[:], comb_n[0:1, 512:1024], start=True, stop=True))(),
                deps=[rowd, ones_set2, psdep(1)])
            ce1 = add("scalar", lambda e: e.copy(crep[:, 0:512], ps[0][:, :]),
                      deps=[crmm1, pv.get("crepw")])
            ce2 = add("scalar", lambda e: e.copy(crep[:, 512:1024], ps[1][:, :]), deps=[crmm2])
            psset(0, ce1)
            psset(1, ce2)
            pv["crepmm"] = crmm2
            for ft in range(32):
                par = ft % 4
                parh = ft % 2
                lw1 = add("sync", [dma(w1st[par * 8 + k][:],
                                       w1_d[e_i, k * 128:(k + 1) * 128, ft * 128:(ft + 1) * 128])
                                   for k in range(8)],
                          deps=[pv.get(("w1c", par))] + scopeA_gate, dma=True)
                cw1 = add("vector", (lambda par: lambda e: [
                    e.tensor_copy(w1bf[par * 8 + k][:], w1st[par * 8 + k][:])
                    for k in range(8)][-1])(par), deps=[lw1])
                mmh = add("tensor", (lambda par, parh, c0: lambda e: [
                    e.matmul(psh[parh][:, i * 512:(i + 1) * 512], w1bf[par * 8 + k][:],
                             x3bf[k][:, c0 + i * 512:c0 + (i + 1) * 512],
                             start=(k == 0), stop=(k == 7))
                    for i in range(2) for k in range(8)][-1])(par, parh, c0),
                    deps=[cw1, pv.get(("pshw", parh))])
                pv[("w1c", par)] = mmh
                actf = AF.Gelu if e_i % 2 == 0 else AF.Silu
                ah = add("scalar", (lambda ft, e_i, actf, parh: lambda e: e.activation(
                    a_full[ft][:], psh[parh][:, :], actf,
                    bias=b1c[:, e_i * 32 + ft:e_i * 32 + ft + 1]))(ft, e_i, actf, parh),
                    deps=[mmh, pv.get(("afw", ft))])
                pv[("pshw", parh)] = ah
            w2r = w2_d.rearrange("e (g a p) d -> e g p a d", p=128, a=8)
            for dc in range(8):
                mmy = None
                for fg in range(4):
                    par = (dc * 4 + fg) % 2
                    lw2 = add("gpsimd", dma(
                        w2st[par][:].rearrange("p (a d) -> p a d", d=128),
                        w2r[e_i, fg][:, :, dc * 128:(dc + 1) * 128]),
                        deps=[pv.get(("w2c", par))] + scopeA_gate, dma=True)
                    cw2 = add("vector", (lambda par: lambda e: e.tensor_copy(
                        w2bf[par][:], w2st[par][:]))(par), deps=[lw2])
                    mmy = add("tensor", (lambda par, fg: lambda e: [
                        e.matmul(psy2[:, i * 512:(i + 1) * 512],
                                 w2bf[par][:, a * 128:(a + 1) * 128],
                                 a_full[fg * 8 + a][:, i * 512:(i + 1) * 512],
                                 start=(fg == 0 and a == 0), stop=(fg == 3 and a == 7))
                        for i in range(2) for a in range(8)][-1])(par, fg),
                        deps=[cw2] + ([pv.get("psy2w"), pv.get(("afw_all",))] if fg == 0 else []))
                    pv[("w2c", par)] = mmy
                wy = add("vector", (lambda: lambda e: e.tensor_tensor(
                    rstd[:, 0:TT4], psy2[:, :], crep[:, 0:TT4], ALU.mult))(),
                    deps=[mmy, ce1, ce2, pv.get("rstdw")])
                pv["psy2w"] = wy
                if e_i == expert_order[0]:
                    acc = add("vector", (lambda dc: lambda e: e.tensor_copy(
                        ymoe[dc][:], rstd[:, 0:TT4]))(dc), deps=[pv.get(("ymwar", dc))])
                else:
                    acc = add("vector", (lambda dc: lambda e: e.tensor_add(
                        ymoe[dc][:], ymoe[dc][:], rstd[:, 0:TT4]))(dc))
                pv["rstdw"] = acc
            pv[("afw_all",)] = mmy
            for ft in range(32):
                pv[("afw", ft)] = mmy
        exp_bar = pr.frontier()
        for dc in range(8):
            mb2 = add("tensor", (lambda dc, c0: lambda e: [
                e.matmul(ps[0][:, 0:512] if i == 0 else ps[1][:, 0:512],
                         b2c[:, dc * 128:(dc + 1) * 128],
                         comb_fm[:, c0 + i * 512:c0 + (i + 1) * 512],
                         start=True, stop=True) for i in range(2)][-1])(dc, c0),
                deps=[psdep(0), psdep(1), ld_c] + gts + (exp_bar if dc == 0 else []))
            ab1 = add("vector", (lambda dc: lambda e: e.tensor_tensor(
                ymoe[dc][:, 0:512], ymoe[dc][:, 0:512], ps[0][:, 0:512], ALU.add))(dc),
                deps=[mb2])
            ab2 = add("vector", (lambda dc: lambda e: e.tensor_tensor(
                ymoe[dc][:, 512:1024], ymoe[dc][:, 512:1024], ps[1][:, 0:512], ALU.add))(dc))
            psset(0, ab1)
            psset(1, ab2)
        if debug and tt == 0:
            for nm, tile, shape, dt_ in [
                ("x3bf", x3bf[0][:], [128, SPAN], BF16),
                ("comb", comb_fm[:], [N_EXP, SPAN], F32),
                ("lg", lg_sb[:], [128, N_EXP], F32),
                ("m1", m1[:], [128, 1], F32),
                ("m2", m2[:], [128, 1], F32),
                ("msk1", msk1[:], [128, N_EXP], F32),
                ("msk2", msk2[:], [128, N_EXP], F32),
                ("s_sc", s_sc[:], [128, 1], F32),
                ("w1s", w1_sc[:], [128, 1], F32),
                ("w2s", w2_sc[:], [128, 1], F32),
                ("ctm", comb_tm[:], [128, N_EXP], F32),
                ("af0", a_full[0][:], [128, TT4], BF16),
                ("ym0", ymoe[0][:], [128, TT4], F32),
                ("crep0", crep[:], [128, TT4], F32),
            ]:
                d_ = nc.dram_tensor("dbg_" + nm, shape, dt_, kind="ExternalOutput").ap()
                add("gpsimd", [dma(d_, tile)], deps=pr.frontier(), dma=True)
        gbar = add("gpsimd", lambda e: e.memset(gbar_t[:], 0.0), deps=pr.frontier())
        frontA = pr.frontier()
        for g in reversed(gA):
            g.__exit__(None, None, None)
        gB = [nc.sbuf_tensor(f"x3re{tt}_{k}", [128, TT4], F32) for k in range(8)]
        x3re = [g.__enter__() for g in gB]
        ldx3 = add("sync", [dma(x3re[k][:], x3s_d[k * 128:(k + 1) * 128, c0:c0 + TT4])
                            for k in range(8)],
                   deps=[pv.get(("x3st", tt, k)) for k in range(8)] + [pv.get("x3rew")]
                   + frontA, dma=True)
        if debug and tt == 0:
            d_ = nc.dram_tensor("dbg_x3re", [128, TT4], F32, kind="ExternalOutput").ap()
            dh = add("gpsimd", [dma(d_, x3re[0][:])], deps=[ldx3], dma=True)
        else:
            dh = None
        h4s = []
        for k in range(8):
            h4 = add("vector", (lambda k: lambda e: e.tensor_tensor(
                x3re[k][:], x3re[k][:], ymoe[k][:], ALU.add))(k),
                deps=[ldx3, pv.get("rstdw"), dh])
            pv[("ymwar", k)] = h4
            h4s.append(h4)
        for hw in range(2):
            apps_h = emit_ln(pr, x3re, hw * 512, 512, ps[0][:, :], ps[1][:, :], hbf2, sq2, rstd,
                             c128s, epsc, h4s + [psdep(0), psdep(1)])
        apps = [pv["ln_rstd"]] * 8
        pv["rstdw"] = None
        for k in range(8):
            st = add("gpsimd", [dma(yout_d[k * 128:(k + 1) * 128, c0:c0 + TT4], x3re[k][:])],
                     deps=[apps[k]], dma=True)
            pv["x3rew"] = st
        for g in reversed(gB):
            g.__exit__(None, None, None)


def build_fused():
    nc = bass.Bass("TRN2", target_bir_lowering=False, detect_race_conditions=False)
    SP = SPAN + 1
    CH = DCH * SP

    E1 = {
        "x": nc.dram_tensor("x", [D, T], F32, kind="ExternalInput").ap(),
        "wx": nc.dram_tensor("wx", [D, DCH], F32, kind="ExternalInput").ap(),
        "wz": nc.dram_tensor("wz", [D, DCH], F32, kind="ExternalInput").ap(),
        "wp": nc.dram_tensor("wp", [D, 96], F32, kind="ExternalInput").ap(),
        "wdt": nc.dram_tensor("wdt", [R_SSM, DCH], F32, kind="ExternalInput").ap(),
        "bdt": nc.dram_tensor("bdt", [2, 128, 1], F32, kind="ExternalInput").ap(),
        "alog": nc.dram_tensor("alog", [2, 128, N_SSM], F32, kind="ExternalInput").ap(),
        "cw": nc.dram_tensor("cw", [2, 128, K_CONV], F32, kind="ExternalInput").ap(),
        "dp": nc.dram_tensor("dp", [2, 128, 1], F32, kind="ExternalInput").ap(),
    }
    y1pad = nc.dram_tensor("y1pad", [DCH, T + 1], BF16, kind="Internal").ap()
    E1["y1"] = y1pad[:, 1:]
    y1_all = nc.dram_tensor("y1_all", [D, T + 1], BF16, kind="Internal").ap()
    y1loc = nc.dram_tensor("y1loc", [D, SP], BF16, kind="Internal").ap()
    kvc = nc.dram_tensor("kvc", [3 * D, SPAN], BF16, kind="Internal").ap()
    kv_all = nc.dram_tensor("kv_all", [4 * 3 * D, SPAN], BF16, kind="Internal").ap()
    x2_i = nc.dram_tensor("x2i", [D, SPAN], F32, kind="Internal").ap()
    qb_i = nc.dram_tensor("qbi", [D, SPAN], BF16, kind="Internal").ap()
    qrb_i = nc.dram_tensor("qrbi", [D, SPAN], BF16, kind="Internal").ap()
    ya_i = nc.dram_tensor("yai", [D, SPAN], BF16, kind="Internal").ap()

    E2 = {
        "xh": nc.dram_tensor("xh", [D, SP], F32, kind="ExternalInput").ap(),
        "y1h": y1loc,
        "woT": nc.dram_tensor("woT", [D, D], F32, kind="ExternalInput").ap(),
        "wrT": nc.dram_tensor("wrT", [D, D], F32, kind="ExternalInput").ap(),
        "wkT": nc.dram_tensor("wkT", [D, D], F32, kind="ExternalInput").ap(),
        "wvT": nc.dram_tensor("wvT", [D, D], F32, kind="ExternalInput").ap(),
        "winT": nc.dram_tensor("winT", [D, 3 * D], F32, kind="ExternalInput").ap(),
        "mur": nc.dram_tensor("mur", [8, 128, 1], F32, kind="ExternalInput").ap(),
        "muk": nc.dram_tensor("muk", [8, 128, 1], F32, kind="ExternalInput").ap(),
        "sinT": nc.dram_tensor("sinT", [128, SPAN], BF16, kind="ExternalInput").ap(),
        "cosT": nc.dram_tensor("cosT", [128, SPAN], BF16, kind="ExternalInput").ap(),
        "x2": x2_i,
        "qb": qb_i,
        "qrb": qrb_i,
        "kb": kvc[0:D, :],
        "krb": kvc[D:2 * D, :],
        "vb": kvc[2 * D:3 * D, :],
        "x2s": nc.dram_tensor("x2s", [D, SPAN], BF16, kind="Internal").ap(),
    }
    E3 = {
        "qb": qb_i,
        "qrb": qrb_i,
        "ya": ya_i,
        "k_parts": lambda r, h: kv_all[r * 3 * D + h * 64:r * 3 * D + (h + 1) * 64, :],
        "kr_parts": lambda r, h: kv_all[r * 3 * D + D + h * 64:r * 3 * D + D + (h + 1) * 64, :],
        "v_slice": lambda h, j: kv_all[(j // 16) * 3 * D + 2 * D + h * 64:
                                       (j // 16) * 3 * D + 2 * D + (h + 1) * 64,
                                       (j % 16) * 128:(j % 16 + 1) * 128],
    }
    E4 = _p4_drams(nc, skip_io=True)
    E4["ya"] = ya_i
    E4["x2"] = x2_i
    ysel_d = nc.dram_tensor("ysel", [128, 4], F32, kind="ExternalInput").ap()

    rg = [[0, 1, 2, 3], [4, 5, 6, 7]]
    pr = Prog(nc)

    sc1 = Scope(nc, "p1_")
    emit_phase1(nc, pr, E1, sc1)
    sc1.close()

    pr.base_deps = pr.frontier()
    with nc.sbuf_tensor("zcol", [128, 1], BF16) as zcol:
        zh = pr.add("vector", lambda e: e.memset(zcol[:], 0.0))
        zd = pr.add("gpsimd", [dma(y1pad[0:128, 0:1], zcol[:], slow=True),
                               dma(y1pad[128:256, 0:1], zcol[:], slow=True)],
                    deps=[zh], dma=True)
        c1 = pr.add("gpsimd", lambda e: e.collective_compute(
            "AllGather", ALU.bypass, rg, [y1pad], [y1_all]), deps=[zd], dma=True)

    with nc.sbuf_tensor("cand0", [128, SP], BF16) as cnd0, \
         nc.sbuf_tensor("cand1", [128, SP], BF16) as cnd1, \
         nc.sbuf_tensor("cand2", [128, SP], BF16) as cnd2, \
         nc.sbuf_tensor("cand3", [128, SP], BF16) as cnd3, \
         nc.sbuf_tensor("yselst", [128, 4], F32) as yselst, \
         nc.sbuf_tensor("yspan", [128, SP], BF16) as yspan:
        cand = [cnd0, cnd1, cnd2, cnd3]
        sl = pr.add("sync", dma(yselst[:], ysel_d), dma=True)
        prev_blend = None
        prev_store = None
        for k in range(8):
            ldh = pr.add("sync", [
                dma(cand[qq][:], y1_all[k * 128:(k + 1) * 128,
                                        qq * SPAN:qq * SPAN + SP]) for qq in range(4)],
                deps=[c1, sl, prev_blend], dma=True)
            b0 = pr.add("vector", (lambda k: lambda e: e.tensor_scalar(
                yspan[:], cnd0[:], yselst[:, 0:1], None, ALU.mult))(k),
                deps=[ldh, prev_store])
            for qq in range(1, 4):
                b0 = pr.add("vector", (lambda qq: lambda e: e.scalar_tensor_tensor(
                    yspan[:], cand[qq][:], yselst[:, qq:qq + 1], yspan[:], ALU.mult, ALU.add))(qq))
            prev_blend = b0
            prev_store = pr.add("gpsimd", dma(y1loc[k * 128:(k + 1) * 128, :], yspan[:]),
                                deps=[b0], dma=True)

    pr.base_deps = pr.frontier()
    sc2 = Scope(nc, "p2_")
    emit_phase2(nc, pr, E2, sc2)
    sc2.close()

    pr.base_deps = pr.frontier()
    c2 = pr.add("gpsimd", lambda e: e.collective_compute(
        "AllGather", ALU.bypass, rg, [kvc], [kv_all]), dma=True)

    pr.base_deps = pr.frontier()
    sc3 = Scope(nc, "p3_")
    emit_phase3(nc, pr, E3, sc3, vt_transposed=False)
    sc3.close()

    pr.base_deps = pr.frontier()
    sc4 = Scope(nc, "p4_")
    emit_phase4(nc, pr, E4, sc4)
    sc4.close()
    pr.base_deps = []

    pr.emit()
    return nc


def _fused_inputs(inputs, core):
    m = _p1_inputs(inputs, core)
    b, q = core // 4, core % 4
    sel = np.zeros((128, 4), dtype=np.float32)
    sel[:, q] = 1.0
    m["ysel"] = sel
    t0 = q * SPAN
    xT = np.ascontiguousarray(np.asarray(inputs["L0_attn_x"])[b].T)
    xh = np.zeros((D, SPAN + 1), dtype=np.float32)
    xh[:, 1:] = xT[:, t0:t0 + SPAN]
    if q > 0:
        xh[:, 0] = xT[:, t0 - 1]
    th = np.repeat(1000.0 ** (-2.0 * np.arange(1, 33, dtype=np.float64) / 64.0), 2)
    pos = np.arange(t0 + 1, t0 + SPAN + 1, dtype=np.float64)
    args = th[:, None] * pos[None, :]
    m.update({
        "xh": xh,
        "woT": np.ascontiguousarray(np.asarray(inputs["L0_attn_W_out"]).T),
        "wrT": np.ascontiguousarray(np.asarray(inputs["L0_ffn_W_r"]).T),
        "wkT": np.ascontiguousarray(np.asarray(inputs["L0_ffn_W_k"]).T),
        "wvT": np.ascontiguousarray(np.asarray(inputs["L0_ffn_W_v"]).T),
        "winT": np.ascontiguousarray(np.asarray(inputs["L1_attn_w_in"]).T),
        "mur": np.ascontiguousarray(np.asarray(inputs["L0_ffn_mu_r"]).reshape(8, 128, 1)),
        "muk": np.ascontiguousarray(np.asarray(inputs["L0_ffn_mu_k"]).reshape(8, 128, 1)),
        "sinT": np.ascontiguousarray(np.tile(np.sin(args).astype(np.float32), (2, 1)).astype(NBF)),
        "cosT": np.ascontiguousarray(np.tile(np.cos(args).astype(np.float32), (2, 1)).astype(NBF)),
    })
    b1 = np.asarray(inputs["L1_ffn_b1"])
    b1h = np.zeros((128, N_EXP * 32), dtype=np.float32)
    for e in range(N_EXP):
        for ft in range(32):
            b1h[:, e * 32 + ft] = b1[e, ft * 128:(ft + 1) * 128]
    m.update({
        "woT4": np.ascontiguousarray(np.asarray(inputs["L1_attn_w_out"]).T),
        "lng": np.ascontiguousarray(np.asarray(inputs["L1_attn_ln_gamma"]).reshape(8, 128, 1)),
        "lnb": np.ascontiguousarray(np.asarray(inputs["L1_attn_ln_beta"]).reshape(8, 128, 1)),
        "wg": np.ascontiguousarray(np.asarray(inputs["L1_ffn_Wg"])),
        "bg": np.ascontiguousarray(np.asarray(inputs["L1_ffn_bg"]).reshape(1, N_EXP)),
        "w1": np.ascontiguousarray(np.asarray(inputs["L1_ffn_W1"])),
        "b1h": b1h,
        "w2": np.ascontiguousarray(np.asarray(inputs["L1_ffn_W2"])),
        "b2mm": np.ascontiguousarray(np.asarray(inputs["L1_ffn_b2"])),
    })
    return m


def kernel_fused(**inputs):
    inputs = {k: np.asarray(v) for k, v in inputs.items()}
    nc = build_fused()
    in_maps = [_fused_inputs(inputs, c) for c in range(NCORES)]
    res = run_bass_kernel_spmd(nc, in_maps, core_ids=list(range(NCORES)))
    out = np.zeros((B, T, D), dtype=np.float32)
    for c in range(NCORES):
        b, q = c // 4, c % 4
        out[b, q * SPAN:(q + 1) * SPAN, :] = res.results[c]["yout"].T
    return out


def _p1_inputs(inputs, core):
    b, q = core // 4, core % 4
    x = np.asarray(inputs["L0_attn_x"])
    sl = slice(q * DCH, (q + 1) * DCH)
    return {
        "x": np.ascontiguousarray(x[b].T),
        "wx": np.ascontiguousarray(np.asarray(inputs["L0_attn_W_x"])[sl].T),
        "wz": np.ascontiguousarray(np.asarray(inputs["L0_attn_W_z"])[sl].T),
        "wp": np.ascontiguousarray(np.asarray(inputs["L0_attn_W_params"]).T),
        "wdt": np.ascontiguousarray(np.asarray(inputs["L0_attn_W_dt"])[sl].T),
        "bdt": np.ascontiguousarray(np.asarray(inputs["L0_attn_b_dt"])[sl].reshape(2, 128, 1)),
        "alog": np.ascontiguousarray(np.asarray(inputs["L0_attn_A_log"])[sl].reshape(2, 128, N_SSM)),
        "cw": np.ascontiguousarray(np.asarray(inputs["L0_attn_conv_w"])[sl, 0, :].reshape(2, 128, K_CONV)),
        "dp": np.ascontiguousarray(np.asarray(inputs["L0_attn_D_param"])[sl].reshape(2, 128, 1)),
    }


def run_phase1(inputs):
    nc = build_phase1()
    in_maps = [_p1_inputs(inputs, c) for c in range(NCORES)]
    res = run_bass_kernel_spmd(nc, in_maps, core_ids=list(range(NCORES)))
    y1 = np.zeros((B, D, T), dtype=NBF)
    for c in range(NCORES):
        b, q = c // 4, c % 4
        y1[b, q * DCH:(q + 1) * DCH, :] = res.results[c]["y1"]
    return y1




def _p2_inputs(inputs, y1, core):
    b, q = core // 4, core % 4
    t0 = q * SPAN
    xT = np.ascontiguousarray(np.asarray(inputs["L0_attn_x"])[b].T)
    xh = np.zeros((D, SPAN + 1), dtype=np.float32)
    xh[:, 1:] = xT[:, t0:t0 + SPAN]
    y1h = np.zeros((D, SPAN + 1), dtype=NBF)
    y1h[:, 1:] = y1[b][:, t0:t0 + SPAN]
    if q > 0:
        xh[:, 0] = xT[:, t0 - 1]
        y1h[:, 0] = y1[b][:, t0 - 1]
    th = np.repeat(1000.0 ** (-2.0 * np.arange(1, 33, dtype=np.float64) / 64.0), 2)
    pos = np.arange(t0 + 1, t0 + SPAN + 1, dtype=np.float64)
    args = th[:, None] * pos[None, :]
    sinT = np.tile(np.sin(args).astype(np.float32), (2, 1)).astype(NBF)
    cosT = np.tile(np.cos(args).astype(np.float32), (2, 1)).astype(NBF)
    return {
        "xh": xh, "y1h": y1h,
        "woT": np.ascontiguousarray(np.asarray(inputs["L0_attn_W_out"]).T),
        "wrT": np.ascontiguousarray(np.asarray(inputs["L0_ffn_W_r"]).T),
        "wkT": np.ascontiguousarray(np.asarray(inputs["L0_ffn_W_k"]).T),
        "wvT": np.ascontiguousarray(np.asarray(inputs["L0_ffn_W_v"]).T),
        "winT": np.ascontiguousarray(np.asarray(inputs["L1_attn_w_in"]).T),
        "mur": np.ascontiguousarray(np.asarray(inputs["L0_ffn_mu_r"]).reshape(8, 128, 1)),
        "muk": np.ascontiguousarray(np.asarray(inputs["L0_ffn_mu_k"]).reshape(8, 128, 1)),
        "sinT": np.ascontiguousarray(sinT), "cosT": np.ascontiguousarray(cosT),
    }


def run_phase2(inputs, y1):
    nc = build_phase2()
    in_maps = [_p2_inputs(inputs, y1, c) for c in range(NCORES)]
    res = run_bass_kernel_spmd(nc, in_maps, core_ids=list(range(NCORES)))
    return [res.results[c] for c in range(NCORES)]


def run_phase3(p2):
    nc = build_phase3()
    in_maps = []
    for c in range(NCORES):
        b = c // 4
        kA = np.concatenate([p2[b * 4 + q]["kb"] for q in range(4)], axis=1)
        krA = np.concatenate([p2[b * 4 + q]["krb"] for q in range(4)], axis=1)
        vT = np.concatenate([np.ascontiguousarray(p2[b * 4 + q]["vb"].T) for q in range(4)], axis=0)
        in_maps.append({"qb": p2[c]["qb"], "qrb": p2[c]["qrb"],
                        "kA": np.ascontiguousarray(kA), "krA": np.ascontiguousarray(krA),
                        "vT": np.ascontiguousarray(vT)})
    res = run_bass_kernel_spmd(nc, in_maps, core_ids=list(range(NCORES)))
    return [res.results[c]["ya"] for c in range(NCORES)]


def _p4_inputs(inputs, p2, p3, core):
    b1 = np.asarray(inputs["L1_ffn_b1"])
    b1h = np.zeros((128, N_EXP * 32), dtype=np.float32)
    for e in range(N_EXP):
        for ft in range(32):
            b1h[:, e * 32 + ft] = b1[e, ft * 128:(ft + 1) * 128]
    return {
        "ya": (p3[core] if p3[core] is not None else np.zeros((D, SPAN), NBF)),
        "x2": p2[core]["x2"],
        "woT4": np.ascontiguousarray(np.asarray(inputs["L1_attn_w_out"]).T),
        "lng": np.ascontiguousarray(np.asarray(inputs["L1_attn_ln_gamma"]).reshape(8, 128, 1)),
        "lnb": np.ascontiguousarray(np.asarray(inputs["L1_attn_ln_beta"]).reshape(8, 128, 1)),
        "wg": np.ascontiguousarray(np.asarray(inputs["L1_ffn_Wg"])),
        "bg": np.ascontiguousarray(np.asarray(inputs["L1_ffn_bg"]).reshape(1, N_EXP)),
        "w1": np.ascontiguousarray(np.asarray(inputs["L1_ffn_W1"])),
        "b1h": b1h,
        "w2": np.ascontiguousarray(np.asarray(inputs["L1_ffn_W2"])),
        "b2mm": np.ascontiguousarray(np.asarray(inputs["L1_ffn_b2"])),
    }


def run_phase4(inputs, p2, p3):
    nc = build_phase4()
    in_maps = [_p4_inputs(inputs, p2, p3, c) for c in range(NCORES)]
    res = run_bass_kernel_spmd(nc, in_maps, core_ids=list(range(NCORES)))
    out = np.zeros((B, T, D), dtype=np.float32)
    for c in range(NCORES):
        b, q = c // 4, c % 4
        out[b, q * SPAN:(q + 1) * SPAN, :] = res.results[c]["yout"].T
    return out




def build_phase34():
    nc = bass.Bass("TRN2", target_bir_lowering=False, detect_race_conditions=False)
    E3 = {
        "qb": nc.dram_tensor("qb", [D, SPAN], BF16, kind="ExternalInput").ap(),
        "qrb": nc.dram_tensor("qrb", [D, SPAN], BF16, kind="ExternalInput").ap(),
        "kA": nc.dram_tensor("kA", [D, T], BF16, kind="ExternalInput").ap(),
        "krA": nc.dram_tensor("krA", [D, T], BF16, kind="ExternalInput").ap(),
        "vT": nc.dram_tensor("vT", [T, D], BF16, kind="ExternalInput").ap(),
        "ya": nc.dram_tensor("ya_i", [D, SPAN], BF16, kind="Internal").ap(),
    }
    E4 = _p4_drams(nc, skip_io=True)
    E4["ya"] = E3["ya"]
    E4["x2"] = nc.dram_tensor("x2", [D, SPAN], F32, kind="ExternalInput").ap()
    pr = Prog(nc)
    sc3 = Scope(nc, "p3_")
    emit_phase3(nc, pr, E3, sc3, vt_transposed=True)
    sc3.close()
    pr.base_deps = pr.frontier()
    sc4 = Scope(nc, "p4_")
    emit_phase4(nc, pr, E4, sc4)
    sc4.close()
    pr.base_deps = []
    pr.emit()
    return nc


def run_phase34(inputs, p2):
    nc = build_phase34()
    in_maps = []
    for c in range(NCORES):
        b = c // 4
        kA = np.concatenate([p2[b * 4 + q]["kb"] for q in range(4)], axis=1)
        krA = np.concatenate([p2[b * 4 + q]["krb"] for q in range(4)], axis=1)
        vT = np.concatenate([np.ascontiguousarray(p2[b * 4 + q]["vb"].T) for q in range(4)], axis=0)
        m = {"qb": p2[c]["qb"], "qrb": p2[c]["qrb"],
             "kA": np.ascontiguousarray(kA), "krA": np.ascontiguousarray(krA),
             "vT": np.ascontiguousarray(vT), "x2": p2[c]["x2"]}
        m.update({k: v for k, v in _p4_inputs(inputs, p2, [None] * 8, c).items()
                  if k not in ("ya", "x2")})
        in_maps.append(m)
    res = run_bass_kernel_spmd(nc, in_maps, core_ids=list(range(NCORES)))
    out = np.zeros((B, T, D), dtype=np.float32)
    for c in range(NCORES):
        b, q = c // 4, c % 4
        out[b, q * SPAN:(q + 1) * SPAN, :] = res.results[c]["yout"].T
    return out


def kernel(**inputs):
    inputs = {k: np.asarray(v) for k, v in inputs.items()}
    y1 = run_phase1(inputs)
    p2 = run_phase2(inputs, y1)
    return run_phase34(inputs, p2)



# version 10
# speedup vs baseline: 1.2908x; 1.0619x over previous
import numpy as np
import ml_dtypes
import concourse.bass as bass
import concourse.mybir as mybir
from concourse.bass_utils import run_bass_kernel_spmd

F32 = mybir.dt.float32
BF16 = mybir.dt.bfloat16
AF = mybir.ActivationFunctionType
ALU = mybir.AluOpType
AX = mybir.AxisListType
NBF = ml_dtypes.bfloat16

B, T, D = 2, 8192, 1024
NCORES = 8
SPAN = T // 4
DCH = D // 4
N_SSM = 16
R_SSM = 64
K_CONV = 4
D_FF = 4096
N_EXP = 8
H = 16
DH = 64


class Prog:

    ENGINES = ("sync", "tensor", "scalar", "vector", "gpsimd")

    def __init__(self, nc):
        self.nc = nc
        self.pv = {}
        self.base_deps = []
        self.ops = {e: [] for e in self.ENGINES}
        self.counts = {}
        self.dpar = {e: 0 for e in self.ENGINES}
        for e in self.ENGINES:
            for kind in ("i", "d0", "d1"):
                self.counts[(e, kind)] = 0

    def add(self, engine, fn, deps=(), dma=False, force_deps=()):
        deps = list(deps) + self.base_deps
        if dma:
            kind = f"d{self.dpar[engine]}"
            self.dpar[engine] ^= 1
            n = len(fn) if isinstance(fn, (list, tuple)) else 1
            self.counts[(engine, kind)] += 16 * n
        else:
            kind = "i"
            self.counts[(engine, kind)] += 1
        handle = (engine, kind, self.counts[(engine, kind)])
        self.ops[engine].append((fn, tuple(d for d in deps if d is not None),
                                 dma, handle, tuple(d for d in force_deps if d is not None)))
        return handle

    def frontier(self):
        out = []
        for e in self.ENGINES:
            for kind in ("i", "d0", "d1"):
                if self.counts[(e, kind)] > 0:
                    out.append((e, kind, self.counts[(e, kind)]))
        return out

    def emit(self):
        nc = self.nc
        sems = {}
        for e in self.ENGINES:
            for kind in ("i", "d0", "d1"):
                if self.counts[(e, kind)] > 0:
                    sems[(e, kind)] = nc.alloc_semaphore(f"s_{e}_{kind}")
        with nc.Block() as block:
            for e in self.ENGINES:
                if not self.ops[e]:
                    continue

                def make(e):
                    def run(eng):
                        watermark = {}
                        mysem = sems.get((e, "i"))
                        for fn, deps, dma, handle, fdeps in self.ops[e]:
                            need = {}
                            for (de, dk, dv) in deps:
                                if de == e and dk == "i":
                                    continue
                                key = (de, dk)
                                need[key] = max(need.get(key, 0), dv)
                            for (de, dk, dv) in fdeps:
                                key = (de, dk)
                                need[key] = max(need.get(key, 0), dv)
                            for key, val in sorted(need.items()):
                                if watermark.get(key, 0) >= val:
                                    continue
                                eng.wait_ge(sems[key], val)
                                watermark[key] = val
                            if dma:
                                thunks = fn if isinstance(fn, (list, tuple)) else [fn]
                                kind = handle[1]
                                dsem = sems[(e, kind)]
                                before = handle[2] - 16 * len(thunks)
                                if before > 0 and watermark.get((e, kind), 0) < before:
                                    eng.wait_ge(dsem, before)
                                    watermark[(e, kind)] = before
                                for th in thunks:
                                    th(eng).then_inc(dsem, 16)
                            else:
                                fn(eng).then_inc(mysem, 1)
                        for kind in ("d0", "d1"):
                            if sems.get((e, kind)) is not None and self.counts[(e, kind)] > 0:
                                eng.wait_ge(sems[(e, kind)], self.counts[(e, kind)])
                    return run

                getattr(block, e)(make(e))


class Scope:

    def __init__(self, nc, prefix):
        self.nc = nc
        self.prefix = prefix
        self.guards = []

    def A(self, name, shape, dtype):
        g = self.nc.sbuf_tensor(self.prefix + name, shape, dtype)
        h = g.__enter__()
        self.guards.append(g)
        return h

    def P(self, name, shape, dtype):
        g = self.nc.psum_tensor(self.prefix + name, shape, dtype)
        h = g.__enter__()
        self.guards.append(g)
        return h

    def close(self):
        for g in reversed(self.guards):
            g.__exit__(None, None, None)
        self.guards = []


def dma(dst, src, slow=False):
    if slow:
        return lambda e: e.dma_start(out=dst, in_=src, allow_slow_non_contiguous=True)
    return lambda e: e.dma_start(out=dst, in_=src)


def _p1_drams(nc, kind_out="ExternalOutput"):
    return {
        "x": nc.dram_tensor("x", [D, T], F32, kind="ExternalInput").ap(),
        "wx": nc.dram_tensor("wx", [D, DCH], F32, kind="ExternalInput").ap(),
        "wz": nc.dram_tensor("wz", [D, DCH], F32, kind="ExternalInput").ap(),
        "wp": nc.dram_tensor("wp", [D, 96], F32, kind="ExternalInput").ap(),
        "wdt": nc.dram_tensor("wdt", [R_SSM, DCH], F32, kind="ExternalInput").ap(),
        "bdt": nc.dram_tensor("bdt", [2, 128, 1], F32, kind="ExternalInput").ap(),
        "alog": nc.dram_tensor("alog", [2, 128, N_SSM], F32, kind="ExternalInput").ap(),
        "cw": nc.dram_tensor("cw", [2, 128, K_CONV], F32, kind="ExternalInput").ap(),
        "dp": nc.dram_tensor("dp", [2, 128, 1], F32, kind="ExternalInput").ap(),
        "y1": nc.dram_tensor("y1", [DCH, T], BF16, kind=kind_out).ap(),
    }


def build_phase1(nchk=None, debug=False):
    nc = bass.Bass("TRN2", target_bir_lowering=False, detect_race_conditions=False)
    E = _p1_drams(nc)
    pr = Prog(nc)
    sc = Scope(nc, "p1_")
    emit_phase1(nc, pr, E, sc, nchk=nchk, debug=debug)
    sc.close()
    pr.emit()
    return nc


def emit_phase1(nc, pr, E, sc, nchk=None, debug=False):
    TC = 2048
    NCHK = T // TC if nchk is None else nchk
    HTC = TC // 2
    x_d = E["x"]
    wx_d = E["wx"]
    wz_d = E["wz"]
    wp_d = E["wp"]
    wdt_d = E["wdt"]
    bdt_d = E["bdt"]
    alog_d = E["alog"]
    cw_d = E["cw"]
    dp_d = E["dp"]
    y1_d = E["y1"]

    A = sc.A
    P = sc.P

    wstage = [A(f"wstage{i}", [128, DCH], F32) for i in range(2)]
    wxb = [A(f"wxb{k}", [128, DCH], BF16) for k in range(8)]
    wzb = [A(f"wzb{k}", [128, DCH], BF16) for k in range(8)]
    wpb = [A(f"wpb{k}", [128, 96], BF16) for k in range(8)]
    wdtb = A("wdtb", [R_SSM, DCH], BF16)
    bdt = [A(f"bdt{m}", [128, 1], F32) for m in range(2)]
    alog = [A(f"alog{m}", [128, N_SSM], F32) for m in range(2)]
    cw = [A(f"cw{m}", [128, K_CONV], F32) for m in range(2)]
    dpar = [A(f"dpar{m}", [128, 1], F32) for m in range(2)]
    Acol = [A(f"Acol{m}", [128, N_SSM], F32) for m in range(2)]
    rcol = [A(f"rcol{m}", [128, N_SSM], F32) for m in range(2)]
    nrcol = [A(f"nrcol{m}", [128, N_SSM], F32) for m in range(2)]
    mcol = [A(f"mcol{m}", [128, N_SSM], F32) for m in range(2)]
    t_ea = [A(f"t_ea{m}", [128, N_SSM], F32) for m in range(2)]
    t_s1 = [A(f"t_s1{m}", [128, N_SSM], F32) for m in range(2)]
    t_s2 = [A(f"t_s2{m}", [128, N_SSM], F32) for m in range(2)]
    t_s3 = [A(f"t_s3{m}", [128, N_SSM], F32) for m in range(2)]
    ones_f = A("ones_f", [1, 128], F32)

    xstage = [A(f"xstage{i}", [128, 512], F32) for i in range(2)]
    xbf = [A(f"xbf{k}", [128, 512], BF16) for k in range(8)]
    xres = [A(f"xres{m}", [128, K_CONV - 1 + TC], F32) for m in range(2)]
    halo = [A(f"halo{m}", [128, K_CONV - 1], F32) for m in range(2)]
    sz_sb = [A(f"sz{m}", [128, TC], F32) for m in range(2)]
    cva = A("cva", [128, TC], F32)
    cvb = A("cvb", [128, TC], F32)
    u_sb = [A(f"u{m}", [128, TC], BF16) for m in range(2)]
    dt_sb = [A(f"dt{m}", [128, TC], BF16) for m in range(2)]
    par_sb = A("par_sb", [96, TC], F32)
    dtu_bf = A("dtu_bf", [R_SSM, TC], BF16)
    sp_r = A("sp_r", [128, 512], F32)
    sp_t = A("sp_t", [128, 512], F32)
    bp_n = A("bp_n", [1, TC], F32)
    cp_n = A("cp_n", [1, TC], F32)
    bp_rep = [A(f"bp_rep{i}", [128, TC], BF16) for i in range(2)]
    cp_rep = [A(f"cp_rep{i}", [128, TC], BF16) for i in range(2)]
    mcol_bf = [A(f"mcolbf{m}", [128, 16], BF16) for m in range(2)]
    at_sb = [A(f"at_sb{i}", [128, TC], BF16) for i in range(2)]
    s1_sb = A("s1_sb", [128, TC], BF16)
    bt_sb = A("bt_sb", [128, TC], BF16)
    w_sb = A("w_sb", [128, TC], BF16)
    y1t_f = A("y1t_f", [128, TC], F32)
    yacc = [A(f"yacc{m}", [128, TC], F32) for m in range(2)]
    state = [A(f"state{m}", [128, N_SSM], F32) for m in range(2)]
    y1bf = A("y1bf", [128, TC], BF16)

    ps = [P(f"ps{i}", [128, 512], F32) for i in range(4)]
    pbp = P("pbp", [128, HTC], F32)
    pcp = P("pcp", [128, HTC], F32)

    pr.pv = {}
    add = pr.add

    wx_r = wx_d.rearrange("(a p) c -> a p c", p=128)
    wz_r = wz_d.rearrange("(a p) c -> a p c", p=128)
    wp_r = wp_d.rearrange("(a p) c -> a p c", p=128)
    srcs = [(wx_r[k], wxb[k], DCH) for k in range(8)] + \
           [(wz_r[k], wzb[k], DCH) for k in range(8)] + \
           [(wp_r[k], wpb[k], 96) for k in range(8)]
    prev_cast = {}
    cast_h = None
    for i, (src, dstb, w) in enumerate(srcs):
        sbuf = wstage[i % 2]
        ldh = add("sync", dma(sbuf[:, 0:w], src), deps=[prev_cast.get(i % 2)], dma=True)
        cast_h = add("vector", (lambda dstb, sbuf, w: lambda e: e.tensor_copy(
            dstb[:, 0:w], sbuf[:, 0:w]))(dstb, sbuf, w), deps=[ldh])
        prev_cast[i % 2] = cast_h
    ld_dt = add("sync", dma(wstage[0][0:R_SSM, 0:DCH], wdt_d), deps=[prev_cast[0]], dma=True)
    cast_w_done = add("vector", lambda e: e.tensor_copy(wdtb[:], wstage[0][0:R_SSM, 0:DCH]),
                      deps=[ld_dt])
    ld_s = add("sync", [dma(bdt[m][:], bdt_d[m]) for m in range(2)]
               + [dma(alog[m][:], alog_d[m]) for m in range(2)]
               + [dma(cw[m][:], cw_d[m]) for m in range(2)]
               + [dma(dpar[m][:], dp_d[m]) for m in range(2)], dma=True)
    ones_set = add("vector", lambda e: e.memset(ones_f[:], 1.0))

    setup_done = []
    for m in range(2):
        ea = add("scalar", (lambda m: lambda e: e.activation(t_ea[m][:], alog[m][:], AF.Exp))(m),
                 deps=[ld_s])
        p_ = add("vector", (lambda m: lambda e: e.tensor_scalar(Acol[m][:], t_ea[m][:], -1.0, None, ALU.mult))(m),
                 deps=[ea])
        p_ = add("vector", (lambda m: lambda e: e.tensor_scalar(mcol[m][:], t_ea[m][:], 1e-8, None, ALU.is_lt))(m),
                 force_deps=[p_])
        h1 = p_ = add("vector", (lambda m: lambda e: e.tensor_scalar(t_s1[m][:], Acol[m][:], 1e-10, None, ALU.add))(m),
                      force_deps=[p_])
        h2 = add("vector", (lambda m: lambda e: e.reciprocal(t_s2[m][:], t_s1[m][:]))(m), force_deps=[h1])
        h3 = add("vector", (lambda m: lambda e: e.tensor_scalar(t_s3[m][:], mcol[m][:], -1.0, 1.0, ALU.mult, ALU.add))(m),
                 force_deps=[h2])
        p_ = add("vector", (lambda m: lambda e: e.tensor_tensor(rcol[m][:], t_s2[m][:], t_s3[m][:], ALU.mult))(m),
                 force_deps=[h3])
        p_ = add("vector", (lambda m: lambda e: e.tensor_scalar(nrcol[m][:], rcol[m][:], -1.0, None, ALU.mult))(m),
                 force_deps=[p_])
        mbf_done = [add("scalar", (lambda m: lambda e: e.copy(mcol_bf[m][:], mcol[m][:]))(m)) for m in range(2)]
        add("vector", (lambda m: lambda e: e.memset(state[m][:], 0.0))(m))
        setup_done.append(add("vector", (lambda m: lambda e: e.memset(halo[m][:], 0.0))(m)))

    pv = {}
    prev_last_acc = None
    prev_hs = [None, None]
    prev_rowdma = None
    prev_st = None
    prev_e1 = [None, None]
    prev_e2 = None

    for c in range(NCHK):
        c0 = c * TC
        par_ev = []
        dt_done = [None, None]
        xres_ev = [[], []]
        sz_ev = [[], []]
        last_xreader = pv.get("last_xreader")
        for tt in range(4):
            t0 = tt * 512
            casts = []
            for k in range(8):
                sbuf = xstage[k % 2]
                ldh = add("sync", dma(sbuf[:], x_d[k * 128:(k + 1) * 128, c0 + t0:c0 + t0 + 512]),
                          deps=[pv.get(("xstage_cast", k % 2))], dma=True)
                ch = add("vector", (lambda k, sbuf: lambda e: e.tensor_copy(xbf[k][:], sbuf[:]))(k, sbuf),
                         deps=[ldh, last_xreader])
                pv[("xstage_cast", k % 2)] = ch
                casts.append(ch)
            mm_deps = [cast_w_done, casts[-1]]

            def mk_acc8(psdst, rows, wb, mslice):
                def f(e):
                    inst = None
                    for k in range(8):
                        inst = e.matmul(psdst[0:rows, :], wb[k][:, mslice],
                                        xbf[k][:], start=(k == 0), stop=(k == 7))
                    return inst
                return f

            par_mm = add("tensor", mk_acc8(ps[0], 96, wpb, slice(0, 96)),
                         deps=mm_deps + [pv.get("ps0"), dt_done[1]])
            ev = add("scalar", (lambda t0: lambda e: e.copy(par_sb[:, t0:t0 + 512], ps[0][0:96, :]))(t0),
                     deps=[par_mm, prev_rowdma])
            par_ev.append(ev)
            dtu_c = add("vector", (lambda t0: lambda e: e.tensor_copy(
                dtu_bf[:, t0:t0 + 512], par_sb[0:64, t0:t0 + 512]))(t0),
                deps=[ev, pv.get("last_dtmm")])
            for m in range(2):
                msl = slice(m * 128, (m + 1) * 128)
                xr_mm = add("tensor", mk_acc8(ps[1 + m], 128, wxb, msl),
                            deps=mm_deps + ([xres_ev[m][-1]] if xres_ev[m] else [pv.get(f"ps{1+m}")]))
                xe = add("scalar", (lambda m, t0: lambda e: e.copy(
                    xres[m][:, K_CONV - 1 + t0:K_CONV - 1 + t0 + 512], ps[1 + m][:, :]))(m, t0),
                    deps=[xr_mm, prev_hs[m]])
                xres_ev[m].append(xe)
                z_mm = add("tensor", mk_acc8(ps[1 + m], 128, wzb, msl), deps=[xe])
                last_xreader = z_mm
                ze = add("scalar", (lambda m, t0: lambda e: e.copy(
                    sz_sb[m][:, t0:t0 + 512], ps[1 + m][:, :]))(m, t0),
                    deps=[z_mm, pv.get(("szwar", m))])
                sz_ev[m].append(ze)
                psi_dt = 3 if m == 0 else 0
                dt_mm = add("tensor", (lambda m, t0, psi_dt: lambda e: e.matmul(
                    ps[psi_dt][:, :], wdtb[:, m * 128:(m + 1) * 128], dtu_bf[:, t0:t0 + 512],
                    start=True, stop=True))(m, t0, psi_dt),
                    deps=[dtu_c, dt_done[m] or pv.get(f"ps{psi_dt}")]
                    + ([par_ev[-1]] if psi_dt == 0 else []))
                pv["last_dtmm"] = dt_mm
                sp1 = add("scalar", (lambda m, psi_dt: lambda e: e.activation(
                    sp_r[:], ps[psi_dt][:, :], AF.Relu, bias=bdt[m][:]))(m, psi_dt),
                    deps=[dt_mm, dt_done[1 - m] or pv.get("last_dtdone")])
                add("scalar", (lambda m, psi_dt: lambda e: e.activation(
                    sp_t[:], ps[psi_dt][:, :], AF.Abs, bias=bdt[m][:]))(m, psi_dt))
                add("scalar", lambda e: e.activation(sp_t[:], sp_t[:], AF.Exp, scale=-1.0))
                sp4 = add("scalar", lambda e: e.activation(sp_t[:], sp_t[:], AF.Ln, bias=1.0))
                dt_done[m] = add("vector", (lambda m, t0: lambda e: e.tensor_add(
                    dt_sb[m][:, t0:t0 + 512], sp_r[:], sp_t[:]))(m, t0), deps=[sp4])
        pv["last_xreader"] = last_xreader
        pv["last_dtdone"] = dt_done[1]

        u_done = [None, None]
        sz_done = [None, None]
        hs_h = [None, None]
        for m in range(2):
            hcopy = add("vector", (lambda m: lambda e: e.tensor_copy(
                xres[m][:, 0:K_CONV - 1], halo[m][:]))(m),
                deps=[setup_done[m]])
            add("vector", (lambda m: lambda e: e.tensor_scalar(
                cva[:], xres[m][:, 0:TC], cw[m][:, 0:1], None, ALU.mult))(m),
                deps=[hcopy, ld_s, u_done[0]] + xres_ev[m])
            add("vector", (lambda m: lambda e: e.scalar_tensor_tensor(
                cvb[:], xres[m][:, 1:1 + TC], cw[m][:, 1:2], cva[:], ALU.mult, ALU.add))(m))
            add("vector", (lambda m: lambda e: e.scalar_tensor_tensor(
                cva[:], xres[m][:, 2:2 + TC], cw[m][:, 2:3], cvb[:], ALU.mult, ALU.add))(m))
            cv4 = add("vector", (lambda m: lambda e: e.scalar_tensor_tensor(
                cvb[:], xres[m][:, 3:3 + TC], cw[m][:, 3:4], cva[:], ALU.mult, ALU.add))(m))
            hs_h[m] = add("vector", (lambda m: lambda e: e.tensor_copy(
                halo[m][:], xres[m][:, TC:TC + K_CONV - 1]))(m))
            u_done[m] = add("scalar", (lambda m: lambda e: e.activation(
                u_sb[m][:], cvb[:], AF.Silu))(m), deps=[cv4, prev_e1[m]])
            sz_done[m] = add("scalar", (lambda m: lambda e: e.activation(
                sz_sb[m][:], sz_sb[m][:], AF.Silu))(m), deps=sz_ev[m] + [prev_e2])
        prev_hs = hs_h

        last_acc = None
        for n in range(N_SSM):
            rowdma = add("gpsimd", [
                dma(bp_n[:], par_sb[64 + n:65 + n, :]),
                dma(cp_n[:], par_sb[80 + n:81 + n, :])],
                deps=par_ev + [pv.get("rowmm")], dma=True)
            prev_rowdma = rowdma
            reps_done = []
            rowmm = None
            pn = n % 2
            for (src, psu, rep, nm) in [(bp_n, pbp, bp_rep[pn], "bp"), (cp_n, pcp, cp_rep[pn], "cp")]:
                for hh in range(2):
                    def mk_b(src, psu, hh):
                        def f(e):
                            inst = None
                            for s4 in range(2):
                                off = hh * HTC + s4 * 512
                                inst = e.matmul(psu[:, s4 * 512:s4 * 512 + 512], ones_f[:],
                                                src[0:1, off:off + 512], start=True, stop=True)
                            return inst
                        return f
                    mm = add("tensor", mk_b(src, psu, hh),
                             deps=[rowdma, ones_set, pv.get(("bcev_last", nm))])
                    rowmm = mm
                    ev = add("scalar", (lambda psu, rep, hh: lambda e: e.copy(
                        rep[:, hh * HTC:(hh + 1) * HTC], psu[:, :]))(psu, rep, hh),
                        deps=[mm, pv.get(("bcacc", pn)) or prev_last_acc])
                    pv[("bcev_last", nm)] = ev
                    reps_done.append(ev)
            pv["rowmm"] = rowmm
            for m in range(2):
                it = (c * 32 + n * 2 + m) if False else (n * 2 + m)
                par_i = it % 2
                at = add("scalar", (lambda m, n, par_i: lambda e: e.activation(
                    at_sb[par_i][:], dt_sb[m][:], AF.Exp, scale=Acol[m][:, n:n + 1]))(m, n, par_i),
                    deps=dt_done + [setup_done[m], pv.get(("accpar", par_i)) or prev_last_acc])
                s1 = add("vector", (lambda m, n, par_i: lambda e: e.tensor_scalar(
                    s1_sb[:], at_sb[par_i][:], rcol[m][:, n:n + 1], nrcol[m][:, n:n + 1],
                    ALU.mult, ALU.add))(m, n, par_i), deps=[at])
                add("vector", (lambda m, n: lambda e: e.scalar_tensor_tensor(
                    bt_sb[:], dt_sb[m][:], mcol_bf[m][:, n:n + 1], s1_sb[:], ALU.mult, ALU.add))(m, n),
                    deps=[mbf_done[m]])
                add("vector", (lambda m, pn: lambda e: e.tensor_tensor(
                    w_sb[:], u_sb[m][:], bp_rep[pn][:], ALU.mult))(m, pn), deps=[u_done[m]] + reps_done)
                Xv = add("vector", lambda e: e.tensor_tensor(s1_sb[:], bt_sb[:], w_sb[:], ALU.mult))
                if c == 0:
                    scn = add("vector", (lambda par_i: lambda e: e.tensor_tensor_scan(
                        bt_sb[:], at_sb[par_i][:], s1_sb[:], 0.0, ALU.mult, ALU.add))(par_i),
                        deps=[at], force_deps=[Xv])
                else:
                    scn = add("vector", (lambda m, n, par_i: lambda e: e.tensor_tensor_scan(
                        bt_sb[:], at_sb[par_i][:], s1_sb[:], state[m][:, n:n + 1], ALU.mult, ALU.add))(m, n, par_i),
                        deps=[at], force_deps=[Xv])
                add("vector", (lambda m, n: lambda e: e.tensor_copy(
                    state[m][:, n:n + 1], bt_sb[:, TC - 1:TC]))(m, n), force_deps=[scn])
                add("vector", (lambda par_i, pn: lambda e: e.tensor_tensor(
                    at_sb[par_i][:], bt_sb[:], cp_rep[pn][:], ALU.mult))(par_i, pn))
                if n == 0:
                    last_acc = add("vector", (lambda m, par_i: lambda e: e.tensor_copy(
                        yacc[m][:], at_sb[par_i][:]))(m, par_i), deps=[prev_e1[m]])
                else:
                    last_acc = add("vector", (lambda m, par_i: lambda e: e.tensor_add(
                        yacc[m][:], yacc[m][:], at_sb[par_i][:]))(m, par_i))
                pv[("lastn", m)] = last_acc
                pv[("accpar", par_i)] = last_acc
            pv[("bcacc", pn)] = last_acc

        for m in range(2):
            e1 = add("vector", (lambda m: lambda e: e.scalar_tensor_tensor(
                y1t_f[:], u_sb[m][:], dpar[m][:], yacc[m][:], ALU.mult, ALU.add))(m),
                deps=[pv[("lastn", m)], ld_s])
            e2 = add("vector", (lambda m: lambda e: e.tensor_tensor(
                y1bf[:], y1t_f[:], sz_sb[m][:], ALU.mult))(m),
                deps=[sz_done[m], prev_st])
            prev_e2 = e2
            st = add("gpsimd", [dma(y1_d[m * 128:(m + 1) * 128, c0:c0 + TC], y1bf[:])],
                     deps=[e2], dma=True)
            prev_st = st
            prev_e1[m] = e1
            pv[("szwar", m)] = e2
        pv["ps0"] = dt_done[1]
        pv["ps1"] = sz_ev[0][-1]
        pv["ps2"] = sz_ev[1][-1]
        pv["ps3"] = dt_done[0]
        prev_last_acc = pv[("lastn", 1)]

    if debug:
        def dump(name, tile, shape, dtype=F32, dep=None):
            d_ = nc.dram_tensor("dbg_" + name, shape, dtype, kind="ExternalOutput").ap()
            add("gpsimd", [dma(d_, tile)], deps=[dep] if dep else [prev_last_acc], dma=True)
        dump("Acol0", Acol[0][:], [128, N_SSM])
        dump("rcol0", rcol[0][:], [128, N_SSM])
        dump("mcol0", mcol[0][:], [128, N_SSM])
        dump("dt0", dt_sb[0][:], [128, TC])
        dump("u0", u_sb[0][:], [128, TC])
        dump("sz0", sz_sb[0][:], [128, TC])
        dump("par", par_sb[:], [96, TC])
        dump("bprep", bp_rep[0][:], [128, TC])
        dump("cprep", cp_rep[0][:], [128, TC])
        dump("yacc0", yacc[0][:], [128, TC])
        dump("state0", state[0][:], [128, N_SSM])
        dump("dtu", dtu_bf[:], [R_SSM, TC], BF16)
        dump("xres0", xres[0][:], [128, K_CONV - 1 + TC])


def emit_ln(pr, tiles, c0, w, ps_m, ps_v, hbf2, sq2, rstd, c128, epsc,
            tile_deps=(), gamma=None, beta=None):
    add = pr.add
    pv = pr.pv
    mms = []
    for k in range(8):
        cast = add("vector", (lambda k: lambda e: e.tensor_copy(
            hbf2[k % 2][:, 0:w], tiles[k][:, c0:c0 + w]))(k),
            deps=list(tile_deps) + [pv.get(("ln_hbf", k % 2))])
        mm = add("tensor", (lambda k: lambda e: e.matmul(
            ps_m[:, 0:w], c128[:], hbf2[k % 2][:, 0:w],
            start=(k == 0), stop=(k == 7)))(k),
            deps=[cast] + ([pv.get("ln_psm")] if k == 0 else []))
        pv[("ln_hbf", k % 2)] = mm
        mms.append(mm)
    xcs = []
    for k in range(8):
        xcs.append(add("vector", (lambda k: lambda e: e.tensor_tensor(
            tiles[k][:, c0:c0 + w], tiles[k][:, c0:c0 + w], ps_m[:, 0:w],
            ALU.subtract))(k), deps=[mms[7]]))
    pv["ln_psm"] = xcs[-1]
    vms = []
    for k in range(8):
        sq = add("scalar", (lambda k: lambda e: e.activation(
            sq2[k % 2][:, 0:w], tiles[k][:, c0:c0 + w], AF.Square))(k),
            deps=[xcs[k], pv.get(("ln_sq", k % 2))])
        vm = add("tensor", (lambda k: lambda e: e.matmul(
            ps_v[:, 0:w], c128[:], sq2[k % 2][:, 0:w],
            start=(k == 0), stop=(k == 7)))(k),
            deps=[sq] + ([pv.get("ln_psv")] if k == 0 else []))
        pv[("ln_sq", k % 2)] = vm
        vms.append(vm)
    eps_ap, nhalf_ap, cdep = epsc
    r1 = add("scalar", lambda e: e.activation(rstd[:, 0:w], ps_v[:, 0:w], AF.Ln, bias=eps_ap),
             deps=[vms[7], pv.get("ln_rstd"), cdep])
    r2 = add("scalar", lambda e: e.activation(rstd[:, 0:w], rstd[:, 0:w], AF.Exp, scale=nhalf_ap))
    pv["ln_psv"] = r2
    apps = []
    for k in range(8):
        a1 = add("vector", (lambda k: lambda e: e.tensor_tensor(
            tiles[k][:, c0:c0 + w], tiles[k][:, c0:c0 + w], rstd[:, 0:w], ALU.mult))(k),
            deps=[r2])
        if gamma is not None:
            a1 = add("vector", (lambda k: lambda e: e.tensor_scalar(
                tiles[k][:, c0:c0 + w], tiles[k][:, c0:c0 + w], gamma[k][:], beta[k][:],
                ALU.mult, ALU.add))(k))
        apps.append(a1)
    pv["ln_rstd"] = apps[-1]
    return apps


def _p2_drams(nc, kind_out="ExternalOutput"):
    SP = SPAN + 1
    return {
        "xh": nc.dram_tensor("xh", [D, SP], F32, kind="ExternalInput").ap(),
        "y1h": nc.dram_tensor("y1h", [D, SP], BF16, kind="ExternalInput").ap(),
        "woT": nc.dram_tensor("woT", [D, D], F32, kind="ExternalInput").ap(),
        "wrT": nc.dram_tensor("wrT", [D, D], F32, kind="ExternalInput").ap(),
        "wkT": nc.dram_tensor("wkT", [D, D], F32, kind="ExternalInput").ap(),
        "wvT": nc.dram_tensor("wvT", [D, D], F32, kind="ExternalInput").ap(),
        "winT": nc.dram_tensor("winT", [D, 3 * D], F32, kind="ExternalInput").ap(),
        "mur": nc.dram_tensor("mur", [8, 128, 1], F32, kind="ExternalInput").ap(),
        "muk": nc.dram_tensor("muk", [8, 128, 1], F32, kind="ExternalInput").ap(),
        "sinT": nc.dram_tensor("sinT", [128, SPAN], BF16, kind="ExternalInput").ap(),
        "cosT": nc.dram_tensor("cosT", [128, SPAN], BF16, kind="ExternalInput").ap(),
        "x2": nc.dram_tensor("x2", [D, SPAN], F32, kind=kind_out).ap(),
        "qb": nc.dram_tensor("qb", [D, SPAN], BF16, kind=kind_out).ap(),
        "qrb": nc.dram_tensor("qrb", [D, SPAN], BF16, kind=kind_out).ap(),
        "kb": nc.dram_tensor("kb", [D, SPAN], BF16, kind=kind_out).ap(),
        "krb": nc.dram_tensor("krb", [D, SPAN], BF16, kind=kind_out).ap(),
        "vb": nc.dram_tensor("vb", [D, SPAN], BF16, kind=kind_out).ap(),
        "x2s": nc.dram_tensor("x2s", [D, SPAN], BF16, kind="Internal").ap(),
    }


def build_phase2():
    nc = bass.Bass("TRN2", target_bir_lowering=False, detect_race_conditions=False)
    E = _p2_drams(nc)
    pr = Prog(nc)
    sc = Scope(nc, "p2_")
    emit_phase2(nc, pr, E, sc)
    sc.close()
    pr.emit()
    return nc


def emit_phase2(nc, pr, E, sc):
    SP = SPAN + 1
    xh_d = E["xh"]
    y1h_d = E["y1h"]
    woT_d = E["woT"]
    wrT_d = E["wrT"]
    wkT_d = E["wkT"]
    wvT_d = E["wvT"]
    winT_d = E["winT"]
    mur_d = E["mur"]
    muk_d = E["muk"]
    sinT_d = E["sinT"]
    cosT_d = E["cosT"]
    x2_d = E["x2"]
    qb_d = E["qb"]
    qrb_d = E["qrb"]
    kb_d = E["kb"]
    krb_d = E["krb"]
    vb_d = E["vb"]
    x2s_d = E["x2s"]

    A = sc.A
    P = sc.P
    x1 = [A(f"x1_{k}", [128, SP], F32) for k in range(8)]
    tx2 = [A(f"tx2_{k}", [128, 512], F32) for k in range(8)]
    wstage = A("wstage", [128, 3 * D], F32)
    wbuf = [A(f"wbuf{k}", [128, 3 * D], BF16) for k in range(8)]
    y1t = [A(f"y1t{k}", [128, 512], BF16) for k in range(8)]
    hbf2 = [A(f"hbf{i}", [128, 512], BF16) for i in range(2)]
    sq2 = [A(f"sq{i}", [128, 512], BF16) for i in range(2)]
    rstd = A("rstd", [128, 512], F32)
    mur = [A(f"mur{k}", [128, 1], F32) for k in range(8)]
    muk = [A(f"muk{k}", [128, 1], F32) for k in range(8)]
    xr_bf = [A(f"xrbf{k}", [128, 512], BF16) for k in range(8)]
    xk_bf = [A(f"xkbf{k}", [128, 512], BF16) for k in range(8)]
    ka_bf = [A(f"kabf{k}", [128, 512], BF16) for k in range(8)]
    r_sb = A("r_sb", [128, 512], F32)
    t_f1 = A("t_f1", [128, 512], F32)
    t_f2 = A("t_f2", [128, 512], F32)
    sinT = A("sinT_s", [128, SPAN], BF16)
    cosT = A("cosT_s", [128, SPAN], BF16)
    x2bf = [A(f"x2bf{k}", [128, 512], BF16) for k in range(8)]
    q_bf = A("q_bf", [128, 512], BF16)
    rot_bf = A("rot_bf", [128, 512], BF16)
    o_bf = A("o_bf", [128, 512], BF16)
    c128s = A("c128s", [128, 128], BF16)
    Rrot = A("Rrot_s", [128, 128], BF16)
    eps_t = A("eps_t", [128, 1], F32)
    nh_t = A("nh_t", [128, 1], F32)
    c128_d = nc.inline_tensor(np.full((128, 128), 1.0 / 1024.0, dtype=NBF), name="c128_p2")
    Rnp = np.zeros((128, 128), dtype=NBF)
    for i in range(64):
        Rnp[2 * i, 2 * i + 1] = 1.0
        Rnp[2 * i + 1, 2 * i] = -1.0
    Rrot_d = nc.inline_tensor(Rnp, name="Rrot")

    ps = [P(f"ps{i}", [128, 512], F32) for i in range(8)]

    pr.pv = {}
    add = pr.add
    pv = pr.pv

    def psdep(i):
        return pv.get(("ps", i))

    def psset(i, h):
        pv[("ps", i)] = h

    cst1 = add("vector", lambda e: e.memset(eps_t[:], 1e-5))
    cst2 = add("vector", lambda e: e.memset(nh_t[:], -0.5))
    epsc = (eps_t[:], nh_t[:], cst2)
    ld_x = add("sync", [dma(x1[k][:], xh_d[k * 128:(k + 1) * 128, :]) for k in range(8)], dma=True)
    ld_c = add("sync", [dma(c128s[:], c128_d.ap()), dma(Rrot[:], Rrot_d.ap()),
                        dma(sinT[:], sinT_d), dma(cosT[:], cosT_d)]
               + [dma(mur[k][:], mur_d[k]) for k in range(8)]
               + [dma(muk[k][:], muk_d[k]) for k in range(8)], dma=True)

    def load_weights(src_d, ncols, col_off, dep_extra=()):
        hs = []
        for k in range(8):
            ldh = add("sync", dma(wstage[:, 0:ncols], src_d[k * 128:(k + 1) * 128, 0:ncols]),
                      deps=[pv.get("wcast")] + list(dep_extra), dma=True)
            ch = add("vector", (lambda k, ncols, col_off: lambda e: e.tensor_copy(
                wbuf[k][:, col_off:col_off + ncols], wstage[:, 0:ncols]))(k, ncols, col_off),
                deps=[ldh] + list(dep_extra))
            pv["wcast"] = ch
            hs.append(ch)
        return hs

    wC = load_weights(woT_d, D, 0)
    tilesC = [(0, 512), (512, 512), (1024, 512), (1536, 512), (2048, 1)]
    for (c0, w) in tilesC:
        ldy = add("sync", [dma(y1t[k][:, 0:w], y1h_d[k * 128:(k + 1) * 128, c0:c0 + w], slow=(w == 1))
                           for k in range(8)], deps=[pv.get("y1done")], dma=True)
        h0s = []
        for o in range(8):
            pso = ps[o % 2]

            def mk8(pso, w, cols, rhs_tiles, rw=None):
                def f(e):
                    inst = None
                    for k in range(8):
                        inst = e.matmul(pso[:, 0:w], wbuf[k][:, cols[0] + 0:cols[1]][:, o * 128:o * 128 + 128] if False else wbuf[k][:, cols[0] + o * 128:cols[0] + (o + 1) * 128],
                                        rhs_tiles[k][:, 0:w] if rw is None else rhs_tiles[k][:, rw[0]:rw[1]],
                                        start=(k == 0), stop=(k == 7))
                    return inst
                return f
            mmo = add("tensor", mk8(pso, w, (0, D), y1t), deps=wC + [ldy, psdep(o % 2)])
            h0 = add("vector", (lambda o, w, pso, c0: lambda e: e.tensor_tensor(
                x1[o][:, c0:c0 + w], pso[:, 0:w], x1[o][:, c0:c0 + w], ALU.add))(o, w, pso, c0),
                deps=[mmo, ld_x])
            psset(o % 2, h0)
            h0s.append(h0)
        pv["y1done"] = mmo
        emit_ln(pr, x1, c0, w, ps[2][:, :], ps[3][:, :], hbf2, sq2, rstd, c128s,
                epsc, h0s + [ld_c, cst1])
        pv["ln_psm_guard"] = None

    wD = load_weights(wrT_d, D, 0, dep_extra=[pv["y1done"]])
    wD += load_weights(wkT_d, D, D)
    wD += load_weights(wvT_d, D, 2 * D)
    x2st = []
    for tt in range(4):
        s = 1 + tt * 512
        c0 = tt * 512
        shfts = []
        for k in range(8):
            add("vector", (lambda k, s: lambda e: e.tensor_tensor(
                t_f1[:], x1[k][:, s:s + 512], x1[k][:, s - 1:s - 1 + 512], ALU.subtract))(k, s),
                deps=[ld_c, pv.get("xrwar")])
            xrh = add("vector", (lambda k, s: lambda e: e.scalar_tensor_tensor(
                xr_bf[k][:], t_f1[:], mur[k][:], x1[k][:, s - 1:s - 1 + 512],
                ALU.mult, ALU.add))(k, s))
            add("vector", (lambda k, s: lambda e: e.tensor_tensor(
                t_f1[:], x1[k][:, s:s + 512], x1[k][:, s - 1:s - 1 + 512], ALU.subtract))(k, s))
            xkh = add("vector", (lambda k, s: lambda e: e.scalar_tensor_tensor(
                xk_bf[k][:], t_f1[:], muk[k][:], x1[k][:, s - 1:s - 1 + 512],
                ALU.mult, ALU.add))(k, s))
            shfts += [xrh, xkh]
        kacts = []
        for o in range(8):
            psi = o % 2
            mmk = add("tensor", (lambda o, psi: lambda e: [
                e.matmul(ps[psi][:, :], wbuf[k][:, D + o * 128:D + (o + 1) * 128],
                         xk_bf[k][:], start=(k == 0), stop=(k == 7)) for k in range(8)][-1])(o, psi),
                deps=wD + shfts + [psdep(psi)])
            a1 = add("scalar", (lambda psi: lambda e: e.activation(
                t_f1[:], ps[psi][:, :], AF.Relu))(psi), deps=[mmk, pv.get("tf1_sc")])
            a2 = add("scalar", (lambda o: lambda e: e.activation(
                ka_bf[o][:], t_f1[:], AF.Square))(o), deps=[pv.get(("kawar", o))])
            pv["tf1_sc"] = a2
            psset(psi, a2)
            kacts.append(a2)
        h1s = []
        for o in range(8):
            mkv = add("tensor", (lambda o: lambda e: [
                e.matmul(ps[2 + o % 2][:, :], wbuf[k][:, 2 * D + o * 128:2 * D + (o + 1) * 128],
                         ka_bf[k][:], start=(k == 0), stop=(k == 7)) for k in range(8)][-1])(o),
                deps=kacts + [psdep(2 + o % 2)])
            pv[("kawar", o)] = mkv
            mr = add("tensor", (lambda o: lambda e: [
                e.matmul(ps[4 + o % 2][:, :], wbuf[k][:, o * 128:(o + 1) * 128],
                         xr_bf[k][:], start=(k == 0), stop=(k == 7)) for k in range(8)][-1])(o),
                deps=shfts + [psdep(4 + o % 2)])
            sg = add("scalar", (lambda o: lambda e: e.activation(
                r_sb[:], ps[4 + o % 2][:, :], AF.Sigmoid))(o), deps=[mr, pv.get("rwar")])
            psset(4 + o % 2, sg)
            of = add("vector", (lambda o: lambda e: e.tensor_tensor(
                t_f2[:], ps[2 + o % 2][:, :], r_sb[:], ALU.mult))(o), deps=[sg, mkv])
            pv["rwar"] = of
            psset(2 + o % 2, of)
            h1 = add("vector", (lambda o, s: lambda e: e.tensor_tensor(
                tx2[o][:], t_f2[:], x1[o][:, s:s + 512], ALU.add))(o, s),
                deps=[pv.get(("tx2war", o))])
            h1s.append(h1)
        apps = emit_ln(pr, tx2, 0, 512, ps[6][:, :], ps[7][:, :], hbf2, sq2, rstd,
                       c128s, epsc, h1s)
        for o in range(8):
            cb = add("vector", (lambda o: lambda e: e.tensor_copy(
                x2bf[o][:], tx2[o][:]))(o), deps=[apps[o], pv.get(("x2bfwar", o))])
            st1 = add("gpsimd", [dma(x2_d[o * 128:(o + 1) * 128, c0:c0 + 512], tx2[o][:]),
                                 dma(x2s_d[o * 128:(o + 1) * 128, c0:c0 + 512], x2bf[o][:])],
                      deps=[apps[o], cb], dma=True)
            pv[("tx2war", o)] = st1
            pv[("x2bfwar", o)] = st1
            x2st.append(st1)

    wE = load_weights(winT_d, 3 * D, 0, dep_extra=[pv["rwar"]])
    for tt in range(4):
        c0 = tt * 512
        casts = []
        for k in range(8):
            ldh = add("gpsimd", dma(x2bf[k][:], x2s_d[k * 128:(k + 1) * 128, c0:c0 + 512]),
                      deps=[pv.get(("x2ld", k))] + x2st, dma=True)
            casts.append(ldh)
        for ot in range(24):
            psi = ot % 2
            mmq = add("tensor", (lambda ot, psi: lambda e: [
                e.matmul(ps[psi][:, :], wbuf[k][:, ot * 128:(ot + 1) * 128],
                         x2bf[k][:], start=(k == 0), stop=(k == 7)) for k in range(8)][-1])(ot, psi),
                deps=wE + casts + [psdep(psi)])
            for k in range(8):
                pv[("x2ld", k)] = mmq
            ev = add("vector", (lambda psi: lambda e: e.tensor_copy(
                q_bf[:], ps[psi][:, :]))(psi), deps=[mmq, pv.get("qbfwar")])
            psset(psi, ev)
            if ot < 16:
                dst, dstr = (qb_d, qrb_d) if ot < 8 else (kb_d, krb_d)
                row = (ot % 8) * 128
                mrot = add("tensor", (lambda psi2: lambda e: e.matmul(
                    ps[2 + psi2][:, :], Rrot[:], q_bf[:], start=True, stop=True))(psi),
                    deps=[ev, ld_c, psdep(2 + psi)])
                rbf = add("vector", (lambda psi2: lambda e: e.tensor_copy(
                    rot_bf[:], ps[2 + psi2][:, :]))(psi), deps=[mrot, pv.get("rotwar")])
                psset(2 + psi, rbf)
                t1h = add("vector", (lambda c0: lambda e: e.tensor_tensor(
                    t_f1[:], q_bf[:], cosT[:, c0:c0 + 512], ALU.mult))(c0),
                    deps=[pv.get("tf1e")])
                t2h = add("vector", (lambda c0: lambda e: e.tensor_tensor(
                    t_f2[:], rot_bf[:], sinT[:, c0:c0 + 512], ALU.mult))(c0))
                pv["rotwar"] = t2h
                ob = add("vector", lambda e: e.tensor_add(o_bf[:], t_f1[:], t_f2[:]))
                pv["tf1e"] = ob
                st = add("gpsimd", [dma(dst[row:row + 128, c0:c0 + 512], q_bf[:]),
                                    dma(dstr[row:row + 128, c0:c0 + 512], o_bf[:])],
                         deps=[ev, ob], dma=True)
                pv["qbfwar"] = st
            else:
                row = (ot - 16) * 128
                st = add("gpsimd", [dma(vb_d[row:row + 128, c0:c0 + 512], q_bf[:])],
                         deps=[ev], dma=True)
                pv["qbfwar"] = st


def build_phase3(debug=False, DBG_H=0):
    nc = bass.Bass("TRN2", target_bir_lowering=False, detect_race_conditions=False)
    E = {
        "qb": nc.dram_tensor("qb", [D, SPAN], BF16, kind="ExternalInput").ap(),
        "qrb": nc.dram_tensor("qrb", [D, SPAN], BF16, kind="ExternalInput").ap(),
        "kA": nc.dram_tensor("kA", [D, T], BF16, kind="ExternalInput").ap(),
        "krA": nc.dram_tensor("krA", [D, T], BF16, kind="ExternalInput").ap(),
        "vT": nc.dram_tensor("vT", [T, D], BF16, kind="ExternalInput").ap(),
        "ya": nc.dram_tensor("ya", [D, SPAN], BF16, kind="ExternalOutput").ap(),
    }
    pr = Prog(nc)
    sc = Scope(nc, "p3_")
    emit_phase3(nc, pr, E, sc, debug=debug, DBG_H=DBG_H, vt_transposed=True)
    sc.close()
    pr.emit()
    return nc


def emit_phase3(nc, pr, E, sc, debug=False, DBG_H=0, vt_transposed=True):
    qb_d = E["qb"]
    qrb_d = E["qrb"]
    kA_d = E.get("kA")
    krA_d = E.get("krA")
    vT_d = E.get("vT")
    ya_d = E["ya"]

    A = sc.A
    P = sc.P
    QT = 512
    ya_sb = [A(f"ya{k}", [128, SPAN], BF16) for k in range(8)]
    kh = [A(f"kh{i}", [64, T], BF16) for i in range(2)]
    krh = [A(f"krh{i}", [64, T], BF16) for i in range(2)]
    vh = [[A(f"vh{i}_{j}", [128, 64], BF16) for j in range(64)] for i in range(2)]
    q_hd = A("q_hd", [64, QT], BF16)
    qr_hd = A("qr_hd", [64, QT], BF16)
    E_sb = [A(f"E_sb{i}", [128, QT], BF16) for i in range(2)]
    Eu_sb = [A(f"Eu_sb{i}", [128, QT], BF16) for i in range(2)]
    den_sb = A("den_sb", [64, QT], F32)
    dacc_sb = A("dacc_sb", [64, QT], F32)
    db_sb = A("db_sb", [64, QT], F32)
    ones_bf = A("ones_bf", [128, 64], BF16)
    scl_t = A("scl_t", [128, 1], F32)
    scl_neg1 = A("scl_neg1", [128, 1], F32)

    psr = [P(f"psr{i}", [128, QT], F32) for i in range(2)]
    psu = [P(f"psu{i}", [128, QT], F32) for i in range(2)]
    psy = P("psy", [64, QT], F32)
    psd = P("psd", [64, QT], F32)

    pr.pv = {}
    add = pr.add
    pv = pr.pv

    ones = add("vector", lambda e: e.memset(ones_bf[:], 1.0))
    sclh = add("vector", lambda e: e.memset(scl_t[:], 0.125))
    sneg = add("vector", lambda e: e.memset(scl_neg1[:], -1.0))

    for h in range(H):
        par = h % 2
        if "k_parts" in E:
            ldk = add("sync", [dma(kh[par][:, r * SPAN:(r + 1) * SPAN], E["k_parts"](r, h))
                               for r in range(4)]
                      + [dma(krh[par][:, r * SPAN:(r + 1) * SPAN], E["kr_parts"](r, h))
                         for r in range(4)],
                      deps=[pv.get(("kdone", par))], dma=True)
            ldv = add("scalar", [
                (lambda j, h2, par2: lambda e: e.dma_start(
                    out=vh[par2][j][:], in_=E["v_slice"](h2, j), transpose=True))(j, h, par)
                for j in range(64)],
                deps=[pv.get(("vdone", par))], dma=True)
        else:
            ldk = add("sync", [dma(kh[par][:], kA_d[h * 64:(h + 1) * 64, :]),
                               dma(krh[par][:], krA_d[h * 64:(h + 1) * 64, :])],
                      deps=[pv.get(("kdone", par))], dma=True)
            ldv = add("gpsimd", [dma(vh[par][j][:], vT_d[j * 128:(j + 1) * 128, h * 64:(h + 1) * 64])
                                 for j in range(64)],
                      deps=[pv.get(("vdone", par))], dma=True)
        qtile = h // 2
        for qt in range(SPAN // QT):
            q0 = qt * QT
            qpart = (h % 2) * 64
            qtl = h // 2
            qcp = add("sync", [dma(q_hd[:], qb_d[h * 64:(h + 1) * 64, q0:q0 + QT]),
                               dma(qr_hd[:], qrb_d[h * 64:(h + 1) * 64, q0:q0 + QT])],
                      deps=[pv.get("qhdw")], dma=True)
            qcp2 = qcp
            if debug and h == DBG_H and qt == 0:
                dbg_qh = nc.dram_tensor("dbg_qh", [64, QT], BF16, kind="ExternalOutput").ap()
                add("gpsimd", [dma(dbg_qh, q_hd[:])], deps=[qcp], dma=True)
            NKT = T // 128
            for kt in range(NKT + 1):
                par2 = kt % 2
                if kt < NKT:
                    k0 = kt * 128
                    mmr = add("tensor", (lambda par, k0, par2: lambda e: [
                        e.matmul(p[par2][:, :], kk[par][:, k0:k0 + 128], qh[:, :],
                                 start=True, stop=True)
                        for (p, kk, qh) in ((psr, krh, qr_hd), (psu, kh, q_hd))][-1])(par, k0, par2),
                        deps=[ldk, qcp2, pv.get(("psrw", par2)), pv.get(("psuw", par2))])
                    er = add("scalar", (lambda par2: lambda e: e.activation(
                        E_sb[par2][:], psr[par2][:, :], AF.Exp, scale=scl_t[:]))(par2),
                        deps=[mmr, sclh, pv.get(("Ew", par2))])
                    pv[("psrw", par2)] = er
                    eu = add("scalar", (lambda par2: lambda e: e.activation(
                        Eu_sb[par2][:], psu[par2][:, :], AF.Exp, scale=scl_t[:]))(par2),
                        deps=[mmr, sclh, pv.get(("Euw", par2))])
                    mmu = mmr
                    pv[("psuw", par2)] = eu
                    pv[("erh", par2)] = er
                    pv[("euh", par2)] = eu
                if kt > 0:
                    kc = kt - 1
                    pc = kc % 2
                    mmy = add("tensor", (lambda par, kc, pc: lambda e: [
                        e.matmul(p[:, :], lh if lh is not None else vh[par][kc][:],
                                 ee[pc][:, :],
                                 start=(kc == 0 if p is psy else kc % 8 == 0),
                                 stop=(kc == NKT - 1 if p is psy else kc % 8 == 7))
                        for (p, lh, ee) in ((psy, None, E_sb), (psd, ones_bf[:], Eu_sb))][-1])(par, kc, pc),
                        deps=[ldv, ones, pv.get(("erh", pc)), pv.get(("euh", pc))]
                        + ([pv.get("psyw")] if kc == 0 else [])
                        + ([pv.get("psdw")] if kc % 8 == 0 else []))
                    pv[("Ew", pc)] = mmy
                    pv[("Euw", pc)] = mmy
                    mmd = mmy
                    if kc % 8 == 7:
                        if kc == 7:
                            dadd = add("vector", (lambda: lambda e: e.tensor_copy(
                                dacc_sb[:], psd[:, :]))(), deps=[mmd, pv.get("daccw")])
                        else:
                            dadd = add("vector", (lambda: lambda e: e.tensor_tensor(
                                dacc_sb[:], dacc_sb[:], psd[:, :], ALU.add))(), deps=[mmd])
                        pv["psdw"] = dadd
            pv[("kdone", par)] = mmu
            pv[("vdone", par)] = mmy
            rcp = add("vector", lambda e: e.reciprocal(db_sb[:], dacc_sb[:]),
                      deps=[pv.get("rcpw")], force_deps=[dadd])
            pv["daccw"] = rcp
            if debug and h == DBG_H and qt == 0:
                dbg_db = nc.dram_tensor("dbg_db", [64, QT], F32, kind="ExternalOutput").ap()
                add("gpsimd", [dma(dbg_db, db_sb[:])], deps=[rcp], dma=True)
            ya1 = add("vector", (lambda qpart, qtl, q0: lambda e: e.tensor_tensor(
                ya_sb[qtl][qpart:qpart + 64, q0:q0 + QT], psy[:, :], db_sb[:], ALU.mult))(qpart, qtl, q0),
                deps=[mmy, pv.get(("yawar", h // 2))], force_deps=[rcp])
            pv["rcpw"] = ya1
            pv["psyw"] = ya1
            pv["qhdw"] = mmu
    st = add("gpsimd", [dma(ya_d[k * 128:(k + 1) * 128, :], ya_sb[k][:]) for k in range(8)],
             deps=pr.frontier(), dma=True)


def _p4_drams(nc, skip_io=False):
    d = {}
    if not skip_io:
        d["ya"] = nc.dram_tensor("ya", [D, SPAN], BF16, kind="ExternalInput").ap()
        d["x2"] = nc.dram_tensor("x2", [D, SPAN], F32, kind="ExternalInput").ap()
    d.update({
        "woT4": nc.dram_tensor("woT4", [D, D], F32, kind="ExternalInput").ap(),
        "lng": nc.dram_tensor("lng", [8, 128, 1], F32, kind="ExternalInput").ap(),
        "lnb": nc.dram_tensor("lnb", [8, 128, 1], F32, kind="ExternalInput").ap(),
        "wg": nc.dram_tensor("wg", [D, N_EXP], F32, kind="ExternalInput").ap(),
        "bg": nc.dram_tensor("bg", [1, N_EXP], F32, kind="ExternalInput").ap(),
        "w1": nc.dram_tensor("w1", [N_EXP, D, D_FF], F32, kind="ExternalInput").ap(),
        "b1h": nc.dram_tensor("b1h", [128, N_EXP * 32], F32, kind="ExternalInput").ap(),
        "w2": nc.dram_tensor("w2", [N_EXP, D_FF, D], F32, kind="ExternalInput").ap(),
        "b2mm": nc.dram_tensor("b2mm", [N_EXP, D], F32, kind="ExternalInput").ap(),
        "yout": nc.dram_tensor("yout", [D, SPAN], F32, kind="ExternalOutput").ap(),
        "x3s": nc.dram_tensor("x3s", [D, SPAN], F32, kind="Internal").ap(),
    })
    return d


def build_phase4(debug=False):
    nc = bass.Bass("TRN2", target_bir_lowering=False, detect_race_conditions=False)
    E = _p4_drams(nc)
    pr = Prog(nc)
    sc = Scope(nc, "p4_")
    emit_phase4(nc, pr, E, sc, debug=debug)
    sc.close()
    pr.emit()
    return nc


def emit_phase4(nc, pr, E, sc, debug=False):
    ya_d = E["ya"]
    x2_d = E["x2"]
    woT_d = E["woT4"]
    lng_d = E["lng"]
    lnb_d = E["lnb"]
    wg_d = E["wg"]
    bg_d = E["bg"]
    w1_d = E["w1"]
    b1_d = E["b1h"]
    w2_d = E["w2"]
    b2_d = E["b2mm"]
    yout_d = E["yout"]
    x3s_d = E["x3s"]

    A = sc.A
    P = sc.P
    TT4 = 1024

    x3bf = [A(f"x3bf{k}", [128, SPAN], BF16) for k in range(8)]
    hbf2 = [A(f"hbf{i}", [128, TT4], BF16) for i in range(2)]
    sq2 = [A(f"sq{i}", [128, TT4], BF16) for i in range(2)]
    rstd = A("rstd", [128, TT4], F32)
    lng = [A(f"lng{k}", [128, 1], F32) for k in range(8)]
    lnb = [A(f"lnb{k}", [128, 1], F32) for k in range(8)]
    c128s = A("c128s", [128, 128], BF16)
    id128 = A("id128", [128, 128], F32)
    ones1_bf = A("ones1_bf", [1, 128], BF16)
    ones1_f = A("ones1_f", [1, 128], F32)
    wg_sb = A("wg_sb", [128, 8 * N_EXP], BF16)
    bg_bf = A("bg_bf", [1, N_EXP], BF16)
    b1c = A("b1c", [128, N_EXP * 32], F32)
    b2c = A("b2c", [N_EXP, D], F32)
    lg_sb = A("lg_sb", [128, N_EXP], F32)
    gbar_t = A("gbar_t", [1, 1], F32)
    mx4 = A("mx4", [128, 4], F32)
    mx2 = A("mx2", [128, 2], F32)
    negone_t = A("negone_t", [128, 1], F32)
    m1 = A("m1", [128, 1], F32)
    m2 = A("m2", [128, 1], F32)
    msk1 = A("msk1", [128, N_EXP], F32)
    msk2 = A("msk2", [128, N_EXP], F32)
    s_sc = A("s_sc", [128, 1], F32)
    w1_sc = A("w1_sc", [128, 1], F32)
    w2_sc = A("w2_sc", [128, 1], F32)
    comb_tm = A("comb_tm", [128, N_EXP], F32)
    comb_fm = A("comb_fm", [N_EXP, SPAN], F32)
    comb_n = A("comb_n", [1, TT4], F32)
    crep = A("crep", [128, TT4], F32)
    eps_t = A("eps_t", [128, 1], F32)
    nh_t = A("nh_t", [128, 1], F32)
    c128_d = nc.inline_tensor(np.full((128, 128), 1.0 / 1024.0, dtype=NBF), name="c128_p4")
    id_d = nc.inline_tensor(np.eye(128, dtype=np.float32), name="id128_c")

    ps = [P(f"ps{i}", [128, 512], F32) for i in range(2)]
    psh = [P(f"psh{i}", [128, TT4], F32) for i in range(2)]
    psy2 = P("psy2", [128, TT4], F32)

    pr.pv = {}
    add = pr.add
    pv = pr.pv

    def psdep(i):
        return pv.get(("ps", i))

    def psset(i, h):
        pv[("ps", i)] = h

    cst1 = add("vector", lambda e: e.memset(eps_t[:], 1e-5))
    cst2 = add("vector", lambda e: e.memset(nh_t[:], -0.5))
    csneg = add("vector", lambda e: e.memset(negone_t[:], -1.0))
    epsc = (eps_t[:], nh_t[:], cst2)
    ld_c = add("sync", [dma(c128s[:], c128_d.ap()), dma(id128[:], id_d.ap()),
                        dma(b1c[:], b1_d), dma(b2c[:], b2_d)]
               + [dma(lng[k][:], lng_d[k]) for k in range(8)]
               + [dma(lnb[k][:], lnb_d[k]) for k in range(8)], dma=True)
    wgst = A("wgst", [128, N_EXP], F32)
    wgh = []
    for k in range(8):
        l1 = add("sync", dma(wgst[:], wg_d[k * 128:(k + 1) * 128, :]),
                 deps=[pv.get("wgc")], dma=True)
        c1 = add("vector", (lambda k: lambda e: e.tensor_copy(
            wg_sb[:, k * N_EXP:(k + 1) * N_EXP], wgst[:]))(k), deps=[l1])
        pv["wgc"] = c1
        wgh.append(c1)
    ones_set = add("vector", lambda e: e.memset(ones1_bf[:], 1.0))
    ones_set2 = add("vector", lambda e: e.memset(ones1_f[:], 1.0))
    lbg = add("sync", dma(wgst[0:1, 0:N_EXP], bg_d), deps=[pv.get("wgc")], dma=True)
    cbg = add("vector", lambda e: e.tensor_copy(bg_bf[:, 0:N_EXP], wgst[0:1, 0:N_EXP]), deps=[lbg])
    pv["wgc"] = cbg

    gF = [nc.sbuf_tensor(f"x2f_{k}", [128, SPAN], F32) for k in range(8)]
    x2f = [g.__enter__() for g in gF]
    gF2 = [nc.sbuf_tensor(f"t1_{k}", [128, TT4], F32) for k in range(8)]
    t1 = [g.__enter__() for g in gF2]
    gF4 = [nc.sbuf_tensor(f"yat_{k}", [128, TT4], BF16) for k in range(8)]
    ya_t = [g.__enter__() for g in gF4]
    gF5 = [nc.sbuf_tensor(f"wo_{k}", [128, D], BF16) for k in range(8)]
    wo_b = [g.__enter__() for g in gF5]
    gF6 = nc.sbuf_tensor("wstF", [128, D], F32)
    wstF = gF6.__enter__()

    ld_x2 = add("sync", [dma(x2f[k][:], x2_d[k * 128:(k + 1) * 128, :]) for k in range(8)], dma=True)
    woh = []
    for k in range(8):
        l1 = add("sync", dma(wstF[:], woT_d[k * 128:(k + 1) * 128, :]),
                 deps=[pv.get("woc")], dma=True)
        c1 = add("vector", (lambda k: lambda e: e.tensor_copy(wo_b[k][:], wstF[:]))(k), deps=[l1])
        pv["woc"] = c1
        woh.append(c1)

    for tt in range(SPAN // TT4):
        c0 = tt * TT4
        ldy = add("sync", [dma(ya_t[k][:], ya_d[k * 128:(k + 1) * 128, c0:c0 + TT4])
                           for k in range(8)], deps=[pv.get("yald")], dma=True)
        h2s = []
        for o in range(8):
            mmo = add("tensor", (lambda o: lambda e: [
                e.matmul(psh[o % 2][:, i * 512:(i + 1) * 512], wo_b[k][:, o * 128:(o + 1) * 128],
                         ya_t[k][:, i * 512:(i + 1) * 512],
                         start=(k == 0), stop=(k == 7))
                for i in range(2) for k in range(8)][-1])(o),
                deps=woh + [ldy, pv.get(("pshw", o % 2))])
            h2 = add("vector", (lambda o, c0: lambda e: e.tensor_tensor(
                t1[o][:], psh[o % 2][:, :], x2f[o][:, c0:c0 + TT4], ALU.add))(o, c0),
                deps=[mmo, ld_x2, pv.get("lastF")])
            pv[("pshw", o % 2)] = h2
            h2s.append(h2)
        pv["yald"] = mmo
        for hw in range(2):
            emit_ln(pr, t1, hw * 512, 512, ps[0][:, :], ps[1][:, :], hbf2, sq2, rstd, c128s,
                    epsc, h2s + [ld_c, cst1], gamma=lng, beta=lnb)
        h3s = []
        for o in range(8):
            h3 = add("vector", (lambda o, c0: lambda e: e.tensor_tensor(
                t1[o][:], t1[o][:], x2f[o][:, c0:c0 + TT4], ALU.add))(o, c0),
                deps=[pv["ln_rstd"]])
            h3s.append(h3)
        apps = []
        for hw in range(2):
            apps_h = emit_ln(pr, t1, hw * 512, 512, ps[0][:, :], ps[1][:, :], hbf2, sq2, rstd,
                             c128s, epsc, h3s)
            apps = apps_h if hw == 0 else [b for b in apps_h]
        apps = [pv["ln_rstd"]] * 8
        for o in range(8):
            cb = add("vector", (lambda o, c0: lambda e: e.tensor_copy(
                x3bf[o][:, c0:c0 + TT4], t1[o][:]))(o, c0), deps=[apps[o]])
            st = add("gpsimd", [dma(x3s_d[o * 128:(o + 1) * 128, c0:c0 + TT4], t1[o][:])],
                     deps=[apps[o]], dma=True)
            pv[("x3st", tt, o)] = st
            pv["lastF"] = st

    barrierF = pr.frontier()
    gF6.__exit__(None, None, None)
    for g in reversed(gF5):
        g.__exit__(None, None, None)
    for g in reversed(gF4):
        g.__exit__(None, None, None)
    for g in reversed(gF2):
        g.__exit__(None, None, None)
    for g in reversed(gF):
        g.__exit__(None, None, None)

    ymoe = [A(f"ymoe{k}", [128, TT4], F32) for k in range(8)]

    expert_order = [0, 2, 4, 6, 1, 3, 5, 7]
    for tt in range(SPAN // TT4):
        c0 = tt * TT4
        gA = ([nc.sbuf_tensor(f"af{tt}_{ft}", [128, TT4], BF16) for ft in range(32)]
              + [nc.sbuf_tensor(f"w2st{tt}_{i}", [128, 1024], F32) for i in range(2)]
              + [nc.sbuf_tensor(f"w2bf{tt}_{i}", [128, 1024], BF16) for i in range(2)]
              + [nc.sbuf_tensor(f"w1st{tt}_{i}", [128, 128], F32) for i in range(32)]
              + [nc.sbuf_tensor(f"w1bf{tt}_{i}", [128, 128], BF16) for i in range(32)])
        hA = [g.__enter__() for g in gA]
        a_full = hA[0:32]
        w2st = hA[32:34]
        w2bf = hA[34:36]
        w1st = hA[36:68]
        w1bf = hA[68:100]
        scopeA_gate = pr.frontier() if tt else list(barrierF)
        gts = []
        for g in range(TT4 // 128):
            tg0 = c0 + g * 128
            mmg = add("tensor", (lambda tg0: lambda e: [
                e.matmul(ps[0][:, 0:N_EXP], x3bf[k][:, tg0:tg0 + 128],
                         wg_sb[:, k * N_EXP:(k + 1) * N_EXP],
                         start=(k == 0), stop=False) for k in range(8)][-1])(tg0),
                deps=barrierF + wgh + [psdep(0), ld_c])
            mmg2 = add("tensor", lambda e: e.matmul(
                ps[0][:, 0:N_EXP], ones1_bf[:], bg_bf[:, 0:N_EXP], start=False, stop=True),
                deps=[ones_set, cbg])
            ev = add("scalar", lambda e: e.copy(lg_sb[:], ps[0][0:128, 0:N_EXP]), deps=[mmg2, pv.get("lgwar")])
            psset(0, ev)
            p_ = add("vector", lambda e: e.tensor_max(mx4[:], lg_sb[:, 0:4], lg_sb[:, 4:8]),
                     deps=[ev], force_deps=[pv.get("trp_prev")])
            p_ = add("vector", lambda e: e.tensor_max(mx2[:], mx4[:, 0:2], mx4[:, 2:4]),
                     force_deps=[p_])
            p_ = add("vector", lambda e: e.tensor_max(m1[:], mx2[:, 0:1], mx2[:, 1:2]),
                     force_deps=[p_])
            mk1 = p_ = add("vector", lambda e: e.tensor_scalar(msk1[:], lg_sb[:], m1[:], None, ALU.is_ge),
                           force_deps=[p_])
            p_ = add("vector", lambda e: e.tensor_scalar(msk2[:], msk1[:], 1e30, None, ALU.mult),
                     force_deps=[p_])
            p_ = add("vector", lambda e: e.tensor_sub(msk2[:], lg_sb[:], msk2[:]),
                     force_deps=[p_])
            p_ = add("vector", lambda e: e.tensor_max(mx4[:], msk2[:, 0:4], msk2[:, 4:8]),
                     force_deps=[p_])
            p_ = add("vector", lambda e: e.tensor_max(mx2[:], mx4[:, 0:2], mx4[:, 2:4]),
                     force_deps=[p_])
            p_ = add("vector", lambda e: e.tensor_max(m2[:], mx2[:, 0:1], mx2[:, 1:2]),
                     force_deps=[p_])
            mk2 = p_ = add("vector", lambda e: e.tensor_scalar(msk2[:], msk2[:], m2[:], None, ALU.is_ge),
                           force_deps=[p_])
            d21 = p_ = add("vector", lambda e: e.tensor_sub(s_sc[:], m2[:], m1[:]),
                           force_deps=[p_])
            sg1 = add("scalar", (lambda: lambda e: e.activation(
                w1_sc[:], s_sc[:], AF.Sigmoid, scale=negone_t[:]))(), deps=[d21, csneg])
            sg2 = add("scalar", lambda e: e.activation(w2_sc[:], s_sc[:], AF.Sigmoid),
                      force_deps=[sg1])
            cmb = p_ = add("vector", lambda e: e.tensor_scalar(comb_tm[:], msk1[:], w1_sc[:], None, ALU.mult),
                          deps=[sg2], force_deps=[d21])
            cmb2 = add("vector", lambda e: e.scalar_tensor_tensor(
                comb_tm[:], msk2[:], w2_sc[:], comb_tm[:], ALU.mult, ALU.add),
                force_deps=[p_])
            trp = add("tensor", (lambda: lambda e: e.transpose(
                ps[1][0:8, 0:128], comb_tm[:], id128[:]))(), deps=[cmb2, ld_c, psdep(1)])
            pv["trp_prev"] = trp
            evt = add("scalar", (lambda tg0: lambda e: e.copy(
                comb_fm[:, tg0:tg0 + 128], ps[1][0:8, 0:128]))(tg0), deps=[trp, pv.get("cfwar")])
            psset(1, evt)
            pv["lgwar"] = cmb2
            gts.append(evt)
        gate_bar = pr.frontier()
        for e_i in expert_order:
            rowd = add("gpsimd", [dma(comb_n[:], comb_fm[e_i:e_i + 1, c0:c0 + TT4])],
                       deps=gts + [pv.get("crepmm")] + (gate_bar if e_i == expert_order[0] else []), dma=True)
            crmm1 = add("tensor", (lambda: lambda e: e.matmul(
                ps[0][:, :], ones1_f[:], comb_n[0:1, 0:512], start=True, stop=True))(),
                deps=[rowd, ones_set2, psdep(0)])
            crmm2 = add("tensor", (lambda: lambda e: e.matmul(
                ps[1][:, :], ones1_f[:], comb_n[0:1, 512:1024], start=True, stop=True))(),
                deps=[rowd, ones_set2, psdep(1)])
            ce1 = add("scalar", lambda e: e.copy(crep[:, 0:512], ps[0][:, :]),
                      deps=[crmm1, pv.get("crepw")])
            ce2 = add("scalar", lambda e: e.copy(crep[:, 512:1024], ps[1][:, :]), deps=[crmm2])
            psset(0, ce1)
            psset(1, ce2)
            pv["crepmm"] = crmm2
            for ft in range(32):
                par = ft % 4
                parh = ft % 2
                lw1 = add("sync", [dma(w1st[par * 8 + k][:],
                                       w1_d[e_i, k * 128:(k + 1) * 128, ft * 128:(ft + 1) * 128])
                                   for k in range(8)],
                          deps=[pv.get(("w1c", par))] + scopeA_gate, dma=True)
                cw1 = add("vector", (lambda par: lambda e: [
                    e.tensor_copy(w1bf[par * 8 + k][:], w1st[par * 8 + k][:])
                    for k in range(8)][-1])(par), deps=[lw1])
                mmh = add("tensor", (lambda par, parh, c0: lambda e: [
                    e.matmul(psh[parh][:, i * 512:(i + 1) * 512], w1bf[par * 8 + k][:],
                             x3bf[k][:, c0 + i * 512:c0 + (i + 1) * 512],
                             start=(k == 0), stop=(k == 7))
                    for i in range(2) for k in range(8)][-1])(par, parh, c0),
                    deps=[cw1, pv.get(("pshw", parh))])
                pv[("w1c", par)] = mmh
                actf = AF.Gelu if e_i % 2 == 0 else AF.Silu
                ah = add("scalar", (lambda ft, e_i, actf, parh: lambda e: e.activation(
                    a_full[ft][:], psh[parh][:, :], actf,
                    bias=b1c[:, e_i * 32 + ft:e_i * 32 + ft + 1]))(ft, e_i, actf, parh),
                    deps=[mmh, pv.get(("afw", ft))])
                pv[("pshw", parh)] = ah
            w2r = w2_d.rearrange("e (g a p) d -> e g p a d", p=128, a=8)
            for dc in range(8):
                mmy = None
                for fg in range(4):
                    par = (dc * 4 + fg) % 2
                    lw2 = add("gpsimd", dma(
                        w2st[par][:].rearrange("p (a d) -> p a d", d=128),
                        w2r[e_i, fg][:, :, dc * 128:(dc + 1) * 128]),
                        deps=[pv.get(("w2c", par))] + scopeA_gate, dma=True)
                    cw2 = add("vector", (lambda par: lambda e: e.tensor_copy(
                        w2bf[par][:], w2st[par][:]))(par), deps=[lw2])
                    mmy = add("tensor", (lambda par, fg: lambda e: [
                        e.matmul(psy2[:, i * 512:(i + 1) * 512],
                                 w2bf[par][:, a * 128:(a + 1) * 128],
                                 a_full[fg * 8 + a][:, i * 512:(i + 1) * 512],
                                 start=(fg == 0 and a == 0), stop=(fg == 3 and a == 7))
                        for i in range(2) for a in range(8)][-1])(par, fg),
                        deps=[cw2] + ([pv.get("psy2w"), pv.get(("afw_all",))] if fg == 0 else []))
                    pv[("w2c", par)] = mmy
                wy = add("vector", (lambda: lambda e: e.tensor_tensor(
                    rstd[:, 0:TT4], psy2[:, :], crep[:, 0:TT4], ALU.mult))(),
                    deps=[mmy, ce1, ce2, pv.get("rstdw")])
                pv["psy2w"] = wy
                if e_i == expert_order[0]:
                    acc = add("vector", (lambda dc: lambda e: e.tensor_copy(
                        ymoe[dc][:], rstd[:, 0:TT4]))(dc), deps=[pv.get(("ymwar", dc))])
                else:
                    acc = add("vector", (lambda dc: lambda e: e.tensor_add(
                        ymoe[dc][:], ymoe[dc][:], rstd[:, 0:TT4]))(dc))
                pv["rstdw"] = acc
            pv[("afw_all",)] = mmy
            for ft in range(32):
                pv[("afw", ft)] = mmy
        exp_bar = pr.frontier()
        for dc in range(8):
            mb2 = add("tensor", (lambda dc, c0: lambda e: [
                e.matmul(ps[0][:, 0:512] if i == 0 else ps[1][:, 0:512],
                         b2c[:, dc * 128:(dc + 1) * 128],
                         comb_fm[:, c0 + i * 512:c0 + (i + 1) * 512],
                         start=True, stop=True) for i in range(2)][-1])(dc, c0),
                deps=[psdep(0), psdep(1), ld_c] + gts + (exp_bar if dc == 0 else []))
            ab1 = add("vector", (lambda dc: lambda e: e.tensor_tensor(
                ymoe[dc][:, 0:512], ymoe[dc][:, 0:512], ps[0][:, 0:512], ALU.add))(dc),
                deps=[mb2])
            ab2 = add("vector", (lambda dc: lambda e: e.tensor_tensor(
                ymoe[dc][:, 512:1024], ymoe[dc][:, 512:1024], ps[1][:, 0:512], ALU.add))(dc))
            psset(0, ab1)
            psset(1, ab2)
        if debug and tt == 0:
            for nm, tile, shape, dt_ in [
                ("x3bf", x3bf[0][:], [128, SPAN], BF16),
                ("comb", comb_fm[:], [N_EXP, SPAN], F32),
                ("lg", lg_sb[:], [128, N_EXP], F32),
                ("m1", m1[:], [128, 1], F32),
                ("m2", m2[:], [128, 1], F32),
                ("msk1", msk1[:], [128, N_EXP], F32),
                ("msk2", msk2[:], [128, N_EXP], F32),
                ("s_sc", s_sc[:], [128, 1], F32),
                ("w1s", w1_sc[:], [128, 1], F32),
                ("w2s", w2_sc[:], [128, 1], F32),
                ("ctm", comb_tm[:], [128, N_EXP], F32),
                ("af0", a_full[0][:], [128, TT4], BF16),
                ("ym0", ymoe[0][:], [128, TT4], F32),
                ("crep0", crep[:], [128, TT4], F32),
            ]:
                d_ = nc.dram_tensor("dbg_" + nm, shape, dt_, kind="ExternalOutput").ap()
                add("gpsimd", [dma(d_, tile)], deps=pr.frontier(), dma=True)
        gbar = add("gpsimd", lambda e: e.memset(gbar_t[:], 0.0), deps=pr.frontier())
        frontA = pr.frontier()
        for g in reversed(gA):
            g.__exit__(None, None, None)
        gB = [nc.sbuf_tensor(f"x3re{tt}_{k}", [128, TT4], F32) for k in range(8)]
        x3re = [g.__enter__() for g in gB]
        ldx3 = add("sync", [dma(x3re[k][:], x3s_d[k * 128:(k + 1) * 128, c0:c0 + TT4])
                            for k in range(8)],
                   deps=[pv.get(("x3st", tt, k)) for k in range(8)] + [pv.get("x3rew")]
                   + frontA, dma=True)
        if debug and tt == 0:
            d_ = nc.dram_tensor("dbg_x3re", [128, TT4], F32, kind="ExternalOutput").ap()
            dh = add("gpsimd", [dma(d_, x3re[0][:])], deps=[ldx3], dma=True)
        else:
            dh = None
        h4s = []
        for k in range(8):
            h4 = add("vector", (lambda k: lambda e: e.tensor_tensor(
                x3re[k][:], x3re[k][:], ymoe[k][:], ALU.add))(k),
                deps=[ldx3, pv.get("rstdw"), dh])
            pv[("ymwar", k)] = h4
            h4s.append(h4)
        for hw in range(2):
            apps_h = emit_ln(pr, x3re, hw * 512, 512, ps[0][:, :], ps[1][:, :], hbf2, sq2, rstd,
                             c128s, epsc, h4s + [psdep(0), psdep(1)])
        apps = [pv["ln_rstd"]] * 8
        pv["rstdw"] = None
        for k in range(8):
            st = add("gpsimd", [dma(yout_d[k * 128:(k + 1) * 128, c0:c0 + TT4], x3re[k][:])],
                     deps=[apps[k]], dma=True)
            pv["x3rew"] = st
        for g in reversed(gB):
            g.__exit__(None, None, None)


def build_fused():
    nc = bass.Bass("TRN2", target_bir_lowering=False, detect_race_conditions=False)
    SP = SPAN + 1
    CH = DCH * SP

    E1 = {
        "x": nc.dram_tensor("x", [D, T], F32, kind="ExternalInput").ap(),
        "wx": nc.dram_tensor("wx", [D, DCH], F32, kind="ExternalInput").ap(),
        "wz": nc.dram_tensor("wz", [D, DCH], F32, kind="ExternalInput").ap(),
        "wp": nc.dram_tensor("wp", [D, 96], F32, kind="ExternalInput").ap(),
        "wdt": nc.dram_tensor("wdt", [R_SSM, DCH], F32, kind="ExternalInput").ap(),
        "bdt": nc.dram_tensor("bdt", [2, 128, 1], F32, kind="ExternalInput").ap(),
        "alog": nc.dram_tensor("alog", [2, 128, N_SSM], F32, kind="ExternalInput").ap(),
        "cw": nc.dram_tensor("cw", [2, 128, K_CONV], F32, kind="ExternalInput").ap(),
        "dp": nc.dram_tensor("dp", [2, 128, 1], F32, kind="ExternalInput").ap(),
    }
    y1pad = nc.dram_tensor("y1pad", [DCH, T + 1], BF16, kind="Internal").ap()
    E1["y1"] = y1pad[:, 1:]
    y1_all = nc.dram_tensor("y1_all", [D, T + 1], BF16, kind="Internal").ap()
    y1loc = nc.dram_tensor("y1loc", [D, SP], BF16, kind="Internal").ap()
    kvc = nc.dram_tensor("kvc", [3 * D, SPAN], BF16, kind="Internal").ap()
    kv_all = nc.dram_tensor("kv_all", [4 * 3 * D, SPAN], BF16, kind="Internal").ap()
    x2_i = nc.dram_tensor("x2i", [D, SPAN], F32, kind="Internal").ap()
    qb_i = nc.dram_tensor("qbi", [D, SPAN], BF16, kind="Internal").ap()
    qrb_i = nc.dram_tensor("qrbi", [D, SPAN], BF16, kind="Internal").ap()
    ya_i = nc.dram_tensor("yai", [D, SPAN], BF16, kind="Internal").ap()

    E2 = {
        "xh": nc.dram_tensor("xh", [D, SP], F32, kind="ExternalInput").ap(),
        "y1h": y1loc,
        "woT": nc.dram_tensor("woT", [D, D], F32, kind="ExternalInput").ap(),
        "wrT": nc.dram_tensor("wrT", [D, D], F32, kind="ExternalInput").ap(),
        "wkT": nc.dram_tensor("wkT", [D, D], F32, kind="ExternalInput").ap(),
        "wvT": nc.dram_tensor("wvT", [D, D], F32, kind="ExternalInput").ap(),
        "winT": nc.dram_tensor("winT", [D, 3 * D], F32, kind="ExternalInput").ap(),
        "mur": nc.dram_tensor("mur", [8, 128, 1], F32, kind="ExternalInput").ap(),
        "muk": nc.dram_tensor("muk", [8, 128, 1], F32, kind="ExternalInput").ap(),
        "sinT": nc.dram_tensor("sinT", [128, SPAN], BF16, kind="ExternalInput").ap(),
        "cosT": nc.dram_tensor("cosT", [128, SPAN], BF16, kind="ExternalInput").ap(),
        "x2": x2_i,
        "qb": qb_i,
        "qrb": qrb_i,
        "kb": kvc[0:D, :],
        "krb": kvc[D:2 * D, :],
        "vb": kvc[2 * D:3 * D, :],
        "x2s": nc.dram_tensor("x2s", [D, SPAN], BF16, kind="Internal").ap(),
    }
    E3 = {
        "qb": qb_i,
        "qrb": qrb_i,
        "ya": ya_i,
        "k_parts": lambda r, h: kv_all[r * 3 * D + h * 64:r * 3 * D + (h + 1) * 64, :],
        "kr_parts": lambda r, h: kv_all[r * 3 * D + D + h * 64:r * 3 * D + D + (h + 1) * 64, :],
        "v_slice": lambda h, j: kv_all[(j // 16) * 3 * D + 2 * D + h * 64:
                                       (j // 16) * 3 * D + 2 * D + (h + 1) * 64,
                                       (j % 16) * 128:(j % 16 + 1) * 128],
    }
    E4 = _p4_drams(nc, skip_io=True)
    E4["ya"] = ya_i
    E4["x2"] = x2_i
    ysel_d = nc.dram_tensor("ysel", [128, 4], F32, kind="ExternalInput").ap()

    rg = [[0, 1, 2, 3], [4, 5, 6, 7]]
    pr = Prog(nc)

    sc1 = Scope(nc, "p1_")
    emit_phase1(nc, pr, E1, sc1)
    sc1.close()

    pr.base_deps = pr.frontier()
    with nc.sbuf_tensor("zcol", [128, 1], BF16) as zcol:
        zh = pr.add("vector", lambda e: e.memset(zcol[:], 0.0))
        zd = pr.add("gpsimd", [dma(y1pad[0:128, 0:1], zcol[:], slow=True),
                               dma(y1pad[128:256, 0:1], zcol[:], slow=True)],
                    deps=[zh], dma=True)
        c1 = pr.add("gpsimd", lambda e: e.collective_compute(
            "AllGather", ALU.bypass, rg, [y1pad], [y1_all]), deps=[zd], dma=True)

    with nc.sbuf_tensor("cand0", [128, SP], BF16) as cnd0, \
         nc.sbuf_tensor("cand1", [128, SP], BF16) as cnd1, \
         nc.sbuf_tensor("cand2", [128, SP], BF16) as cnd2, \
         nc.sbuf_tensor("cand3", [128, SP], BF16) as cnd3, \
         nc.sbuf_tensor("yselst", [128, 4], F32) as yselst, \
         nc.sbuf_tensor("yspan", [128, SP], BF16) as yspan:
        cand = [cnd0, cnd1, cnd2, cnd3]
        sl = pr.add("sync", dma(yselst[:], ysel_d), dma=True)
        prev_blend = None
        prev_store = None
        for k in range(8):
            ldh = pr.add("sync", [
                dma(cand[qq][:], y1_all[k * 128:(k + 1) * 128,
                                        qq * SPAN:qq * SPAN + SP]) for qq in range(4)],
                deps=[c1, sl, prev_blend], dma=True)
            b0 = pr.add("vector", (lambda k: lambda e: e.tensor_scalar(
                yspan[:], cnd0[:], yselst[:, 0:1], None, ALU.mult))(k),
                deps=[ldh, prev_store])
            for qq in range(1, 4):
                b0 = pr.add("vector", (lambda qq: lambda e: e.scalar_tensor_tensor(
                    yspan[:], cand[qq][:], yselst[:, qq:qq + 1], yspan[:], ALU.mult, ALU.add))(qq))
            prev_blend = b0
            prev_store = pr.add("gpsimd", dma(y1loc[k * 128:(k + 1) * 128, :], yspan[:]),
                                deps=[b0], dma=True)

    pr.base_deps = pr.frontier()
    sc2 = Scope(nc, "p2_")
    emit_phase2(nc, pr, E2, sc2)
    sc2.close()

    pr.base_deps = pr.frontier()
    c2 = pr.add("gpsimd", lambda e: e.collective_compute(
        "AllGather", ALU.bypass, rg, [kvc], [kv_all]), dma=True)

    pr.base_deps = pr.frontier()
    sc3 = Scope(nc, "p3_")
    emit_phase3(nc, pr, E3, sc3, vt_transposed=False)
    sc3.close()

    pr.base_deps = pr.frontier()
    sc4 = Scope(nc, "p4_")
    emit_phase4(nc, pr, E4, sc4)
    sc4.close()
    pr.base_deps = []

    pr.emit()
    return nc


def _fused_inputs(inputs, core):
    m = _p1_inputs(inputs, core)
    b, q = core // 4, core % 4
    sel = np.zeros((128, 4), dtype=np.float32)
    sel[:, q] = 1.0
    m["ysel"] = sel
    t0 = q * SPAN
    xT = np.ascontiguousarray(np.asarray(inputs["L0_attn_x"])[b].T)
    xh = np.zeros((D, SPAN + 1), dtype=np.float32)
    xh[:, 1:] = xT[:, t0:t0 + SPAN]
    if q > 0:
        xh[:, 0] = xT[:, t0 - 1]
    th = np.repeat(1000.0 ** (-2.0 * np.arange(1, 33, dtype=np.float64) / 64.0), 2)
    pos = np.arange(t0 + 1, t0 + SPAN + 1, dtype=np.float64)
    args = th[:, None] * pos[None, :]
    m.update({
        "xh": xh,
        "woT": np.ascontiguousarray(np.asarray(inputs["L0_attn_W_out"]).T),
        "wrT": np.ascontiguousarray(np.asarray(inputs["L0_ffn_W_r"]).T),
        "wkT": np.ascontiguousarray(np.asarray(inputs["L0_ffn_W_k"]).T),
        "wvT": np.ascontiguousarray(np.asarray(inputs["L0_ffn_W_v"]).T),
        "winT": np.ascontiguousarray(np.asarray(inputs["L1_attn_w_in"]).T),
        "mur": np.ascontiguousarray(np.asarray(inputs["L0_ffn_mu_r"]).reshape(8, 128, 1)),
        "muk": np.ascontiguousarray(np.asarray(inputs["L0_ffn_mu_k"]).reshape(8, 128, 1)),
        "sinT": np.ascontiguousarray(np.tile(np.sin(args).astype(np.float32), (2, 1)).astype(NBF)),
        "cosT": np.ascontiguousarray(np.tile(np.cos(args).astype(np.float32), (2, 1)).astype(NBF)),
    })
    b1 = np.asarray(inputs["L1_ffn_b1"])
    b1h = np.zeros((128, N_EXP * 32), dtype=np.float32)
    for e in range(N_EXP):
        for ft in range(32):
            b1h[:, e * 32 + ft] = b1[e, ft * 128:(ft + 1) * 128]
    m.update({
        "woT4": np.ascontiguousarray(np.asarray(inputs["L1_attn_w_out"]).T),
        "lng": np.ascontiguousarray(np.asarray(inputs["L1_attn_ln_gamma"]).reshape(8, 128, 1)),
        "lnb": np.ascontiguousarray(np.asarray(inputs["L1_attn_ln_beta"]).reshape(8, 128, 1)),
        "wg": np.ascontiguousarray(np.asarray(inputs["L1_ffn_Wg"])),
        "bg": np.ascontiguousarray(np.asarray(inputs["L1_ffn_bg"]).reshape(1, N_EXP)),
        "w1": np.ascontiguousarray(np.asarray(inputs["L1_ffn_W1"])),
        "b1h": b1h,
        "w2": np.ascontiguousarray(np.asarray(inputs["L1_ffn_W2"])),
        "b2mm": np.ascontiguousarray(np.asarray(inputs["L1_ffn_b2"])),
    })
    return m


def kernel_fused(**inputs):
    inputs = {k: np.asarray(v) for k, v in inputs.items()}
    nc = build_fused()
    in_maps = [_fused_inputs(inputs, c) for c in range(NCORES)]
    res = run_bass_kernel_spmd(nc, in_maps, core_ids=list(range(NCORES)))
    out = np.zeros((B, T, D), dtype=np.float32)
    for c in range(NCORES):
        b, q = c // 4, c % 4
        out[b, q * SPAN:(q + 1) * SPAN, :] = res.results[c]["yout"].T
    return out


def _p1_inputs(inputs, core):
    b, q = core // 4, core % 4
    x = np.asarray(inputs["L0_attn_x"])
    sl = slice(q * DCH, (q + 1) * DCH)
    return {
        "x": np.ascontiguousarray(x[b].T),
        "wx": np.ascontiguousarray(np.asarray(inputs["L0_attn_W_x"])[sl].T),
        "wz": np.ascontiguousarray(np.asarray(inputs["L0_attn_W_z"])[sl].T),
        "wp": np.ascontiguousarray(np.asarray(inputs["L0_attn_W_params"]).T),
        "wdt": np.ascontiguousarray(np.asarray(inputs["L0_attn_W_dt"])[sl].T),
        "bdt": np.ascontiguousarray(np.asarray(inputs["L0_attn_b_dt"])[sl].reshape(2, 128, 1)),
        "alog": np.ascontiguousarray(np.asarray(inputs["L0_attn_A_log"])[sl].reshape(2, 128, N_SSM)),
        "cw": np.ascontiguousarray(np.asarray(inputs["L0_attn_conv_w"])[sl, 0, :].reshape(2, 128, K_CONV)),
        "dp": np.ascontiguousarray(np.asarray(inputs["L0_attn_D_param"])[sl].reshape(2, 128, 1)),
    }


def run_phase1(inputs):
    nc = build_phase1()
    in_maps = [_p1_inputs(inputs, c) for c in range(NCORES)]
    res = run_bass_kernel_spmd(nc, in_maps, core_ids=list(range(NCORES)))
    y1 = np.zeros((B, D, T), dtype=NBF)
    for c in range(NCORES):
        b, q = c // 4, c % 4
        y1[b, q * DCH:(q + 1) * DCH, :] = res.results[c]["y1"]
    return y1




def _p2_inputs(inputs, y1, core):
    b, q = core // 4, core % 4
    t0 = q * SPAN
    xT = np.ascontiguousarray(np.asarray(inputs["L0_attn_x"])[b].T)
    xh = np.zeros((D, SPAN + 1), dtype=np.float32)
    xh[:, 1:] = xT[:, t0:t0 + SPAN]
    y1h = np.zeros((D, SPAN + 1), dtype=NBF)
    y1h[:, 1:] = y1[b][:, t0:t0 + SPAN]
    if q > 0:
        xh[:, 0] = xT[:, t0 - 1]
        y1h[:, 0] = y1[b][:, t0 - 1]
    th = np.repeat(1000.0 ** (-2.0 * np.arange(1, 33, dtype=np.float64) / 64.0), 2)
    pos = np.arange(t0 + 1, t0 + SPAN + 1, dtype=np.float64)
    args = th[:, None] * pos[None, :]
    sinT = np.tile(np.sin(args).astype(np.float32), (2, 1)).astype(NBF)
    cosT = np.tile(np.cos(args).astype(np.float32), (2, 1)).astype(NBF)
    return {
        "xh": xh, "y1h": y1h,
        "woT": np.ascontiguousarray(np.asarray(inputs["L0_attn_W_out"]).T),
        "wrT": np.ascontiguousarray(np.asarray(inputs["L0_ffn_W_r"]).T),
        "wkT": np.ascontiguousarray(np.asarray(inputs["L0_ffn_W_k"]).T),
        "wvT": np.ascontiguousarray(np.asarray(inputs["L0_ffn_W_v"]).T),
        "winT": np.ascontiguousarray(np.asarray(inputs["L1_attn_w_in"]).T),
        "mur": np.ascontiguousarray(np.asarray(inputs["L0_ffn_mu_r"]).reshape(8, 128, 1)),
        "muk": np.ascontiguousarray(np.asarray(inputs["L0_ffn_mu_k"]).reshape(8, 128, 1)),
        "sinT": np.ascontiguousarray(sinT), "cosT": np.ascontiguousarray(cosT),
    }


def run_phase2(inputs, y1):
    nc = build_phase2()
    in_maps = [_p2_inputs(inputs, y1, c) for c in range(NCORES)]
    res = run_bass_kernel_spmd(nc, in_maps, core_ids=list(range(NCORES)))
    return [res.results[c] for c in range(NCORES)]


def run_phase3(p2):
    nc = build_phase3()
    in_maps = []
    for c in range(NCORES):
        b = c // 4
        kA = np.concatenate([p2[b * 4 + q]["kb"] for q in range(4)], axis=1)
        krA = np.concatenate([p2[b * 4 + q]["krb"] for q in range(4)], axis=1)
        vT = np.concatenate([np.ascontiguousarray(p2[b * 4 + q]["vb"].T) for q in range(4)], axis=0)
        in_maps.append({"qb": p2[c]["qb"], "qrb": p2[c]["qrb"],
                        "kA": np.ascontiguousarray(kA), "krA": np.ascontiguousarray(krA),
                        "vT": np.ascontiguousarray(vT)})
    res = run_bass_kernel_spmd(nc, in_maps, core_ids=list(range(NCORES)))
    return [res.results[c]["ya"] for c in range(NCORES)]


def _p4_inputs(inputs, p2, p3, core):
    b1 = np.asarray(inputs["L1_ffn_b1"])
    b1h = np.zeros((128, N_EXP * 32), dtype=np.float32)
    for e in range(N_EXP):
        for ft in range(32):
            b1h[:, e * 32 + ft] = b1[e, ft * 128:(ft + 1) * 128]
    return {
        "ya": (p3[core] if p3[core] is not None else np.zeros((D, SPAN), NBF)),
        "x2": p2[core]["x2"],
        "woT4": np.ascontiguousarray(np.asarray(inputs["L1_attn_w_out"]).T),
        "lng": np.ascontiguousarray(np.asarray(inputs["L1_attn_ln_gamma"]).reshape(8, 128, 1)),
        "lnb": np.ascontiguousarray(np.asarray(inputs["L1_attn_ln_beta"]).reshape(8, 128, 1)),
        "wg": np.ascontiguousarray(np.asarray(inputs["L1_ffn_Wg"])),
        "bg": np.ascontiguousarray(np.asarray(inputs["L1_ffn_bg"]).reshape(1, N_EXP)),
        "w1": np.ascontiguousarray(np.asarray(inputs["L1_ffn_W1"])),
        "b1h": b1h,
        "w2": np.ascontiguousarray(np.asarray(inputs["L1_ffn_W2"])),
        "b2mm": np.ascontiguousarray(np.asarray(inputs["L1_ffn_b2"])),
    }


def run_phase4(inputs, p2, p3):
    nc = build_phase4()
    in_maps = [_p4_inputs(inputs, p2, p3, c) for c in range(NCORES)]
    res = run_bass_kernel_spmd(nc, in_maps, core_ids=list(range(NCORES)))
    out = np.zeros((B, T, D), dtype=np.float32)
    for c in range(NCORES):
        b, q = c // 4, c % 4
        out[b, q * SPAN:(q + 1) * SPAN, :] = res.results[c]["yout"].T
    return out




def build_phase34():
    nc = bass.Bass("TRN2", target_bir_lowering=False, detect_race_conditions=False)
    E3 = {
        "qb": nc.dram_tensor("qb", [D, SPAN], BF16, kind="ExternalInput").ap(),
        "qrb": nc.dram_tensor("qrb", [D, SPAN], BF16, kind="ExternalInput").ap(),
        "kA": nc.dram_tensor("kA", [D, T], BF16, kind="ExternalInput").ap(),
        "krA": nc.dram_tensor("krA", [D, T], BF16, kind="ExternalInput").ap(),
        "vT": nc.dram_tensor("vT", [T, D], BF16, kind="ExternalInput").ap(),
        "ya": nc.dram_tensor("ya_i", [D, SPAN], BF16, kind="Internal").ap(),
    }
    E4 = _p4_drams(nc, skip_io=True)
    E4["ya"] = E3["ya"]
    E4["x2"] = nc.dram_tensor("x2", [D, SPAN], F32, kind="ExternalInput").ap()
    pr = Prog(nc)
    sc3 = Scope(nc, "p3_")
    emit_phase3(nc, pr, E3, sc3, vt_transposed=True)
    sc3.close()
    pr.base_deps = pr.frontier()
    sc4 = Scope(nc, "p4_")
    emit_phase4(nc, pr, E4, sc4)
    sc4.close()
    pr.base_deps = []
    pr.emit()
    return nc


def run_phase34(inputs, p2):
    nc = build_phase34()
    in_maps = []
    for c in range(NCORES):
        b = c // 4
        kA = np.concatenate([p2[b * 4 + q]["kb"] for q in range(4)], axis=1)
        krA = np.concatenate([p2[b * 4 + q]["krb"] for q in range(4)], axis=1)
        vT = np.concatenate([np.ascontiguousarray(p2[b * 4 + q]["vb"].T) for q in range(4)], axis=0)
        m = {"qb": p2[c]["qb"], "qrb": p2[c]["qrb"],
             "kA": np.ascontiguousarray(kA), "krA": np.ascontiguousarray(krA),
             "vT": np.ascontiguousarray(vT), "x2": p2[c]["x2"]}
        m.update({k: v for k, v in _p4_inputs(inputs, p2, [None] * 8, c).items()
                  if k not in ("ya", "x2")})
        in_maps.append(m)
    res = run_bass_kernel_spmd(nc, in_maps, core_ids=list(range(NCORES)))
    out = np.zeros((B, T, D), dtype=np.float32)
    for c in range(NCORES):
        b, q = c // 4, c % 4
        out[b, q * SPAN:(q + 1) * SPAN, :] = res.results[c]["yout"].T
    return out


def kernel(**inputs):
    inputs = {k: np.asarray(v) for k, v in inputs.items()}
    y1 = run_phase1(inputs)
    p2 = run_phase2(inputs, y1)
    return run_phase34(inputs, p2)

